# version 06
# baseline (speedup 1.0000x reference)
import math

import numpy as np
import ml_dtypes

import concourse.bass as bass
import concourse.mybir as mybir
from concourse.bass_utils import run_bass_kernel_spmd

F32 = mybir.dt.float32
BF16 = mybir.dt.bfloat16
I32 = mybir.dt.int32
AX = mybir.ActivationFunctionType
OP = mybir.AluOpType
AXL = mybir.AxisListType

T = 2048
C = 2048
H = 16
L = 64
HL = H * L
INTER = 5504
KC = INTER // 128
CC = C // 128
CT = C // 128
NT = T // 128
NG = 4
GS = T // NG
SCALE = 1.0 / math.sqrt(128.0)
NEG = -1.0e9
EPS = 1e-5

BLOB_ROWS = 3 * 5504
SH_ROWS = BLOB_ROWS // 8
GTILES = 32
GFREE = SH_ROWS * C // (128 * GTILES)
WP_SH = C // 8


def _consts():
    identf = np.eye(128, dtype=np.float32)
    identb = np.eye(128, dtype=ml_dtypes.bfloat16)
    lidx = np.arange(128) % 64
    tidx = np.arange(64)
    maskp = np.where(tidx[None, :] <= lidx[:, None], 0.0, NEG).astype(np.float32)
    inv = (10000.0 ** (-(np.arange(64, dtype=np.float32)) / 64.0)).astype(np.float32)
    ang = np.arange(64, dtype=np.float32)[:, None] * inv[None, :]
    cosT = np.cos(ang).astype(np.float32)
    sinT = np.sin(ang).astype(np.float32)
    onesb = np.zeros((128, 128), dtype=ml_dtypes.bfloat16)
    for q in range(8):
        onesb[0:64, q * 16 + 2 * q] = 1
        onesb[64:128, q * 16 + 2 * q + 1] = 1
    bcb = np.zeros((16, 1024), dtype=np.float32)
    for q in range(8):
        bcb[2 * q, q * 128: q * 128 + 64] = 1.0
        bcb[2 * q + 1, q * 128 + 64: q * 128 + 128] = 1.0
    return identf, identb, maskp, cosT, sinT, onesb, bcb


def _host_shards(x, lq, wp, gidx, uidx, didx):
    def scr_gu(idx):
        a = np.ascontiguousarray(idx).T
        a = a.reshape(16, 128, 43, 128).transpose(2, 1, 0, 3)
        return np.ascontiguousarray(a.reshape(5504, 2048))

    def scr_dn(idx):
        a = np.ascontiguousarray(idx).T
        a = a.reshape(43, 128, 16, 128).transpose(2, 1, 0, 3)
        return np.ascontiguousarray(a.reshape(5504, 2048))

    blob = np.concatenate([scr_gu(gidx), scr_gu(uidx), scr_dn(didx)], axis=0)
    idx_shards = [np.ascontiguousarray(s) for s in np.split(blob, 8, axis=0)]
    wpa = (np.ascontiguousarray(wp).reshape(16, 128, 16, 128)
           .transpose(2, 1, 0, 3).reshape(2048, 2048))
    wp_shards = [np.ascontiguousarray(s) for s in np.split(wpa, 8, axis=0)]
    qT = np.ascontiguousarray(
        np.transpose(lq[0], (2, 1, 0)).reshape(128, HL)).astype(np.float32, copy=False)
    xs = [np.ascontiguousarray(x[j]) for j in range(8)]
    return xs, idx_shards, wp_shards, qT


class Sem:
    __slots__ = ("h", "n")

    def __init__(self, h):
        self.h = h
        self.n = 0

    def bump(self, k=1):
        self.n += k
        return self.n


def build(lut_vals, debug=False):
    nc = bass.Bass("TRN2", target_bir_lowering=False)
    dbg_kind = {"kind": "ExternalOutput"} if debug else {}

    x_h = nc.dram_tensor("x", [T, C], F32, kind="ExternalInput")
    wk_h = nc.dram_tensor("wk", [C, C], F32, kind="ExternalInput")
    wv_h = nc.dram_tensor("wv", [C, C], F32, kind="ExternalInput")
    wg_h = nc.dram_tensor("wg", [C, HL], F32, kind="ExternalInput")
    qT_h = nc.dram_tensor("qT", [128, HL], F32, kind="ExternalInput")
    idx_h = nc.dram_tensor("idxs", [SH_ROWS, C], I32, kind="ExternalInput")
    wpa_h = nc.dram_tensor("wpa", [WP_SH, C], F32, kind="ExternalInput")
    identf_h = nc.dram_tensor("identf", [128, 128], F32, kind="ExternalInput")
    identb_h = nc.dram_tensor("identb", [128, 128], BF16, kind="ExternalInput")
    maskp_h = nc.dram_tensor("maskp", [128, 64], F32, kind="ExternalInput")
    cos_h = nc.dram_tensor("cosT", [64, 64], F32, kind="ExternalInput")
    sin_h = nc.dram_tensor("sinT", [64, 64], F32, kind="ExternalInput")
    onesb_h = nc.dram_tensor("onesb", [128, 128], BF16, kind="ExternalInput")
    bcb_h = nc.dram_tensor("bcb", [16, HL], F32, kind="ExternalInput")
    out_h = nc.dram_tensor("out", [T, C], F32, kind="ExternalOutput")
    xn_scr = nc.dram_tensor("xn_scr", [T, C], BF16, **dbg_kind)
    ag1_in = nc.dram_tensor("ag1_in", [SH_ROWS, C], BF16)
    w_scr = nc.dram_tensor("w_scr", [BLOB_ROWS, C], BF16)
    ag2_in = nc.dram_tensor("ag2_in", [WP_SH, C], BF16)
    wp_scr = nc.dram_tensor("wp_scr", [C, C], BF16)
    if debug:
        w_dbg = nc.dram_tensor("w_dbg", [BLOB_ROWS, C], BF16, kind="ExternalOutput")
        wp_dbg = nc.dram_tensor("wp_dbg", [C, C], BF16, kind="ExternalOutput")
        ktb_dbg = nc.dram_tensor("ktb_dbg", [128, HL], BF16, kind="ExternalOutput")
        atp_dbg = nc.dram_tensor("atp_dbg", [128, 8 * 128], BF16, kind="ExternalOutput")
        vtm_dbg = nc.dram_tensor("vtm_dbg", [64, C], BF16, kind="ExternalOutput")
        ss_dbg = nc.dram_tensor("ss_dbg", [128, 8 * 64], F32, kind="ExternalOutput")
        wb_dbg = nc.dram_tensor("wb_dbg", [128, 8 * 64], F32, kind="ExternalOutput")
        wtp_dbg = nc.dram_tensor("wtp_dbg", [64, 8 * 128], BF16, kind="ExternalOutput")
        exg_dbg = nc.dram_tensor("exg_dbg", [128, 8 * GS], BF16, kind="ExternalOutput")
        wtg_dbg = nc.dram_tensor("wtg_dbg", [128, CC * GS], BF16, kind="ExternalOutput")
        ht_dbg = nc.dram_tensor("ht_dbg", [128, KC * GS], BF16, kind="ExternalOutput")

    script = []

    def em(eng):
        def reg(fn):
            script.append((eng, fn))
            return fn
        return reg

    from contextlib import ExitStack
    _es = ExitStack()
    with _es:
        c_idf = _es.enter_context(nc.sbuf_tensor("sb_identf", [128, 128], F32))
        c_idb = _es.enter_context(nc.sbuf_tensor("sb_identb", [128, 128], BF16))
        c_mask = _es.enter_context(nc.sbuf_tensor("sb_maskp", [128, 64], F32))
        c_cos = _es.enter_context(nc.sbuf_tensor("sb_cosT", [64, 64], F32))
        c_sin = _es.enter_context(nc.sbuf_tensor("sb_sinT", [64, 64], F32))
        c_ones = _es.enter_context(nc.sbuf_tensor("sb_onesb", [128, 128], BF16))
        c_bcb = _es.enter_context(nc.sbuf_tensor("sb_bcb", [16, HL], F32))
        qTb = _es.enter_context(nc.sbuf_tensor("qTb", [128, HL], BF16))
        gi_in = _es.enter_context(nc.sbuf_tensor("gi_in", [128, 2, GFREE], I32))
        x_t = _es.enter_context(nc.sbuf_tensor("x_t", [128, C], F32))
        xn_t = _es.enter_context(nc.sbuf_tensor("xn_t", [128, 2, C], BF16))
        ln_s = _es.enter_context(nc.sbuf_tensor("ln_s", [128, 10], F32))
        wkb = _es.enter_context(nc.sbuf_tensor("wkb", [128, C], BF16))
        k_tm = _es.enter_context(nc.sbuf_tensor("k_tm", [64, C], F32))
        krh = _es.enter_context(nc.sbuf_tensor("krh", [64, 128], F32))
        ktmp = _es.enter_context(nc.sbuf_tensor("ktmp", [64, 64], F32))
        v_tmb = _es.enter_context(nc.sbuf_tensor("v_tmb", [64, C], BF16))
        kTb = _es.enter_context(nc.sbuf_tensor("kTb", [128, HL], BF16))
        ss_f = _es.enter_context(nc.sbuf_tensor("ss_f", [128, 64], F32))
        nmax = _es.enter_context(nc.sbuf_tensor("nmax", [128, 1], F32))
        w_b = _es.enter_context(nc.sbuf_tensor("w_b", [128, 64], F32))
        rs = _es.enter_context(nc.sbuf_tensor("rs", [128, 1], F32))
        rr = _es.enter_context(nc.sbuf_tensor("rr", [128, 1], F32))
        wTp = _es.enter_context(nc.sbuf_tensor("wTp", [64, 128], BF16))
        attnP = _es.enter_context(nc.sbuf_tensor("attnP", [128, 8, 128], BF16))
        xnT = _es.enter_context(nc.sbuf_tensor("xnT", [128, CC, GS], BF16))
        wgsl = _es.enter_context(nc.sbuf_tensor("wgsl", [128, 2, HL], BF16))
        expg = _es.enter_context(nc.sbuf_tensor("expg", [128, 8, GS], BF16))
        rsum = _es.enter_context(nc.sbuf_tensor("rsum", [16, GS], F32))
        wTg = _es.enter_context(nc.sbuf_tensor("wTg", [128, CC, GS], BF16))
        hT = _es.enter_context(nc.sbuf_tensor("hT", [128, KC, GS], BF16))
        gbuf = _es.enter_context(nc.sbuf_tensor("gbuf", [128, 2, C], BF16))
        ubuf = _es.enter_context(nc.sbuf_tensor("ubuf", [128, 2, C], BF16))
        dbuf = _es.enter_context(nc.sbuf_tensor("dbuf", [128, 2, INTER], BF16))
        wpsl = _es.enter_context(nc.sbuf_tensor("wpsl", [128, 2, C], BF16))
        sgb = _es.enter_context(nc.sbuf_tensor("sgb", [128, GS], BF16))
        oT_sb = _es.enter_context(nc.sbuf_tensor("oT_sb", [128, 2, GS], F32))
        xpc = _es.enter_context(nc.sbuf_tensor("xpc", [128, 2, 128], F32))
        res_sb = _es.enter_context(nc.sbuf_tensor("res_sb", [128, 2, 128], F32))
        epsb = _es.enter_context(nc.sbuf_tensor("epsb", [128, 1], F32))
        junkv = _es.enter_context(nc.sbuf_tensor("junkv", [128, 1], F32))
        jnk2 = _es.enter_context(nc.sbuf_tensor("jnk2", [128, 16], BF16))
        junkx = _es.enter_context(nc.sbuf_tensor("junkx", [128, 16], F32))
        PGA = _es.enter_context(nc.psum_tensor("PGA", [128, 512], F32))
        PGB = _es.enter_context(nc.psum_tensor("PGB", [128, 512], F32))
        PB0 = _es.enter_context(nc.psum_tensor("PB0", [128, 512], F32))
        PB1 = _es.enter_context(nc.psum_tensor("PB1", [128, 512], F32))
        PO0 = _es.enter_context(nc.psum_tensor("PO0", [128, 512], F32))
        PO1 = _es.enter_context(nc.psum_tensor("PO1", [128, 512], F32))
        PSX = _es.enter_context(nc.psum_tensor("PSX", [128, 512], F32))
        PSY = _es.enter_context(nc.psum_tensor("PSY", [128, 512], F32))
        h_eps = _es.enter_context(nc.semaphore("s_eps"))
        h_c = _es.enter_context(nc.semaphore("s_c"))
        h_gl = _es.enter_context(nc.semaphore("s_gl"))
        h_gv = _es.enter_context(nc.semaphore("s_gv"))
        h_gw = _es.enter_context(nc.semaphore("s_gw"))
        h_ag = _es.enter_context(nc.semaphore("s_ag"))
        h_xl = _es.enter_context(nc.semaphore("s_xl"))
        h_app = _es.enter_context(nc.semaphore("s_app"))
        h_xnw = _es.enter_context(nc.semaphore("s_xnw"))
        h_x64 = _es.enter_context(nc.semaphore("s_x64"))
        h_wk = _es.enter_context(nc.semaphore("s_wk"))
        h_kmm = _es.enter_context(nc.semaphore("s_kmm"))
        h_kev = _es.enter_context(nc.semaphore("s_kev"))
        h_rope = _es.enter_context(nc.semaphore("s_rope"))
        h_ktr = _es.enter_context(nc.semaphore("s_ktr"))
        h_ktev = _es.enter_context(nc.semaphore("s_ktev"))
        h_scr = _es.enter_context(nc.semaphore("s_scr"))
        h_ss = _es.enter_context(nc.semaphore("s_ss"))
        h_we = _es.enter_context(nc.semaphore("s_we"))
        h_rr = _es.enter_context(nc.semaphore("s_rr"))
        h_wn = _es.enter_context(nc.semaphore("s_wn"))
        h_wt = _es.enter_context(nc.semaphore("s_wt"))
        h_wtev = _es.enter_context(nc.semaphore("s_wtev"))
        h_am = _es.enter_context(nc.semaphore("s_am"))
        h_aev = _es.enter_context(nc.semaphore("s_aev"))
        h_xnt = _es.enter_context(nc.semaphore("s_xnt"))
        h_wgl = _es.enter_context(nc.semaphore("s_wgl"))
        h_gmm = _es.enter_context(nc.semaphore("s_gmm"))
        h_gex = _es.enter_context(nc.semaphore("s_gex"))
        h_smm = _es.enter_context(nc.semaphore("s_smm"))
        h_rcp = _es.enter_context(nc.semaphore("s_rcp"))
        h_bc = _es.enter_context(nc.semaphore("s_bc"))
        h_gn = _es.enter_context(nc.semaphore("s_gn"))
        h_wm = _es.enter_context(nc.semaphore("s_wm"))
        h_wev = _es.enter_context(nc.semaphore("s_wev"))
        h_fs = _es.enter_context(nc.semaphore("s_fs"))
        h_gu = _es.enter_context(nc.semaphore("s_gu"))
        h_si = _es.enter_context(nc.semaphore("s_si"))
        h_ht = _es.enter_context(nc.semaphore("s_ht"))
        h_ds = _es.enter_context(nc.semaphore("s_ds"))
        h_omm = _es.enter_context(nc.semaphore("s_omm"))
        h_oev = _es.enter_context(nc.semaphore("s_oev"))
        h_tr = _es.enter_context(nc.semaphore("s_tr"))
        h_xp = _es.enter_context(nc.semaphore("s_xp"))
        h_res = _es.enter_context(nc.semaphore("s_res"))
        h_out = _es.enter_context(nc.semaphore("s_out"))
        h_d1 = _es.enter_context(nc.semaphore("s_d1"))
        h_d2 = _es.enter_context(nc.semaphore("s_d2"))
        h_d3 = _es.enter_context(nc.semaphore("s_d3"))
        h_d4 = _es.enter_context(nc.semaphore("s_d4"))
        block = _es.enter_context(nc.Block())
        s_c = Sem(h_c); s_gl = Sem(h_gl); s_gv = Sem(h_gv); s_gw = Sem(h_gw)
        s_ag = Sem(h_ag); s_xl = Sem(h_xl); s_app = Sem(h_app)
        s_xnw = Sem(h_xnw); s_x64 = Sem(h_x64); s_wk = Sem(h_wk)
        s_kmm = Sem(h_kmm); s_kev = Sem(h_kev); s_rope = Sem(h_rope)
        s_ktr = Sem(h_ktr); s_ktev = Sem(h_ktev); s_scr = Sem(h_scr)
        s_ss = Sem(h_ss); s_we = Sem(h_we); s_rr = Sem(h_rr); s_wn = Sem(h_wn)
        s_wt = Sem(h_wt); s_wtev = Sem(h_wtev); s_am = Sem(h_am)
        s_aev = Sem(h_aev); s_xnt = Sem(h_xnt)
        s_wgl = Sem(h_wgl); s_gmm = Sem(h_gmm); s_gex = Sem(h_gex)
        s_smm = Sem(h_smm); s_rcp = Sem(h_rcp); s_bc = Sem(h_bc)
        s_gn = Sem(h_gn); s_wm = Sem(h_wm); s_wev = Sem(h_wev)
        s_fs = Sem(h_fs); s_gu = Sem(h_gu); s_si = Sem(h_si)
        s_ht = Sem(h_ht); s_ds = Sem(h_ds); s_omm = Sem(h_omm)
        s_oev = Sem(h_oev); s_tr = Sem(h_tr); s_xp = Sem(h_xp)
        s_res = Sem(h_res); s_out = Sem(h_out)

        banks = [PGA, PGB, PB0, PB1, PO0, PO1, PSX, PSY]

        gw = [dbuf[:, 0, 0:GFREE], dbuf[:, 0, GFREE:2 * GFREE]]
        gi_bf = dbuf[:, 1, 0:GFREE]
        g_ct = dbuf[:, 1, GFREE:2 * GFREE]
        wp_stage = [gbuf[:, 0, :], gbuf[:, 1, :]]
        xnT64 = [xnT[:, cc, 0:64] for cc in range(CC)]

        @em("v")
        def _(e):
            e.memset(epsb[:], EPS).then_inc(h_eps, 1)

        def idx_tile_ap(k):
            return bass.AP(idx_h, k * 128 * GFREE, [[GFREE, 128], [1, GFREE]])

        def ag1_tile_ap(k):
            return bass.AP(ag1_in, k * 128 * GFREE, [[GFREE, 128], [1, GFREE]])

        @em("gp")
        def _(e):
            e.dma_start(out=gi_in[:, 0, :], in_=idx_tile_ap(0)).then_inc(s_gl.h, 16)
            e.dma_start(out=gi_in[:, 1, :], in_=idx_tile_ap(1)).then_inc(s_gl.h, 16)
            e.dma_start(out=c_idf[:], in_=identf_h.ap()).then_inc(s_c.h, 16)
            e.dma_start(out=c_idb[:], in_=identb_h.ap()).then_inc(s_c.h, 16)
            e.dma_start(out=c_mask[:], in_=maskp_h.ap()).then_inc(s_c.h, 16)
            e.dma_start(out=c_cos[:], in_=cos_h.ap()).then_inc(s_c.h, 16)
            e.dma_start(out=c_sin[:], in_=sin_h.ap()).then_inc(s_c.h, 16)
            e.dma_start(out=c_ones[:], in_=onesb_h.ap()).then_inc(s_c.h, 16)
            e.dma_start(out=c_bcb[:], in_=bcb_h.ap()).then_inc(s_c.h, 16)
            e.dma_start(out=qTb[:], in_=qT_h.ap()).then_inc(s_c.h, 16)
            e.dma_start(out=wp_stage[0], in_=wpa_h.ap()[0:128, :]).then_inc(s_c.h, 16)
            e.dma_start(out=wp_stage[1], in_=wpa_h.ap()[128:256, :]).then_inc(s_c.h, 16)
        s_gl.bump(32)
        s_c.bump(10 * 16)
        CONSTS = s_c.n

        for k in range(GTILES):
            @em("v")
            def _(e, k=k):
                e.wait_ge(s_gl.h, 16 * (k + 1))
                if k == 0:
                    e.wait_ge(s_c.h, CONSTS)
                if k >= 2:
                    e.wait_ge(s_gw.h, 16 * (k - 1))
                e.tensor_copy(out=gi_bf, in_=gi_in[:, k % 2, :])
                e.tensor_scalar(out=gw[k % 2], in0=gi_bf,
                                scalar1=0.0, scalar2=lut_vals[0],
                                op0=OP.is_equal, op1=OP.mult)
                last = None
                for v in range(1, 16):
                    e.tensor_scalar(out=g_ct, in0=gi_bf,
                                    scalar1=float(v), scalar2=lut_vals[v],
                                    op0=OP.is_equal, op1=OP.mult)
                    last = e.tensor_add(gw[k % 2], gw[k % 2], g_ct)
                last.then_inc(s_gv.h, 1)
            s_gv.bump(1)

            @em("gp")
            def _(e, k=k):
                e.wait_ge(s_gv.h, k + 2)
                e.dma_start(out=ag1_tile_ap(k), in_=gw[k % 2]).then_inc(s_gw.h, 16)
                if k + 2 < GTILES:
                    e.dma_start(out=gi_in[:, k % 2, :],
                                in_=idx_tile_ap(k + 2)).then_inc(s_gl.h, 16)
            s_gw.bump(16)
            if k + 2 < GTILES:
                s_gl.bump(16)

        @em("v")
        def _(e):
            e.memset(junkv[:], 0.0).then_inc(s_gv.h, 1)
        s_gv.bump(1)

        @em("gp")
        def _(e, want=s_c.n):
            e.wait_ge(s_c.h, want)
            e.dma_start(out=ag2_in.ap()[0:128, :], in_=wp_stage[0]).then_inc(s_gw.h, 16)
            e.dma_start(out=ag2_in.ap()[128:256, :], in_=wp_stage[1]).then_inc(s_gw.h, 16)
        s_gw.bump(32)

        AGW_WANT = s_gw.n

        ssq = ln_s[:, 0:1]; ssum = ln_s[:, 1:2]; mean = ln_s[:, 2:3]
        nm2 = ln_s[:, 3:4]; varep = ln_s[:, 4:5]; rstd = ln_s[:, 5:6]
        t9 = ln_s[:, 6:7]; nmr = ln_s[:, 7:8]; lnjk = ln_s[:, 8:9]

        def emit_ln_tile(tt):
            @em("s")
            def _(e, tt=tt):
                if tt == 0:
                    e.wait_ge(h_eps, 1)
                if tt >= 1:
                    e.wait_ge(s_app.h, tt)
                e.dma_start(out=x_t[:],
                            in_=x_h.ap()[tt * 128:(tt + 1) * 128, :]
                            ).then_inc(s_xl.h, 16)
                e.wait_ge(s_xl.h, 16 * (tt + 1))
                if tt >= 2:
                    e.wait_ge(s_xnw.h, 16 * (tt - 1))
                e.activation(out=xn_t[:, tt % 2, :], in_=x_t[:], func=AX.Square,
                             accum_out=ssq)
                e.activation(out=xn_t[:, tt % 2, :], in_=x_t[:], func=AX.Identity,
                             accum_out=ssum)
                e.mul(out=mean, in_=ssum, mul=1.0 / C)
                e.activation(out=nm2, in_=mean, func=AX.Square)
                e.activation(out=nm2, in_=nm2, func=AX.Identity,
                             scale=-1.0, bias=epsb[:])
                e.activation(out=lnjk, in_=ssq, func=AX.Identity)
                e.activation(out=varep, in_=ssq, func=AX.Identity,
                             scale=1.0 / C, bias=nm2)
                e.activation(out=varep, in_=varep, func=AX.Ln)
                e.activation(out=rstd, in_=varep, func=AX.Exp, scale=-0.5)
                e.activation(out=t9, in_=mean, func=AX.Identity, scale=rstd)
                e.mul(out=nmr, in_=t9, mul=-1.0)
                e.activation(out=lnjk, in_=ssq, func=AX.Identity)
                e.activation(out=xn_t[:, tt % 2, :], in_=x_t[:], func=AX.Identity,
                             scale=rstd, bias=nmr)
                e.activation(out=lnjk, in_=ssq,
                             func=AX.Identity).then_inc(s_app.h, 1)
            s_xl.bump(16)
            s_app.bump(1)

            @em("sy")
            def _(e, tt=tt):
                e.wait_ge(s_app.h, tt + 1)
                e.dma_start(out=xn_scr.ap()[tt * 128:(tt + 1) * 128, :],
                            in_=xn_t[:, tt % 2, :]).then_inc(s_xnw.h, 16)
            s_xnw.bump(16)

        emit_ln_tile(0)

        @em("sy")
        def _(e):
            e.wait_ge(s_xnw.h, 16)
            e.wait_ge(s_c.h, CONSTS)
            for cc in range(CC):
                e.dma_start(out=xnT64[cc],
                            in_=xn_scr.ap()[0:64, cc * 128:(cc + 1) * 128],
                            transpose=True).then_inc(s_x64.h, 16)
        s_x64.bump(16 * CC)

        for tt in range(1, NT):
            emit_ln_tile(tt)

        kvps = [PGA[0:64, :], PGB[0:64, :], PB0[0:64, :], PB1[0:64, :]]
        for m, w_hh in enumerate((wk_h, wv_h)):
            for cc in range(CC):
                i = m * CC + cc

                @em("gp")
                def _(e, cc=cc, w_hh=w_hh, i=i):
                    if i >= 1:
                        e.wait_ge(s_kmm.h, i)
                    e.dma_start(out=wkb[:],
                                in_=w_hh.ap()[cc * 128:(cc + 1) * 128, :]
                                ).then_inc(s_wk.h, 16)
                s_wk.bump(16)

                @em("t")
                def _(e, m=m, cc=cc, i=i):
                    e.wait_ge(s_wk.h, 16 * (i + 1))
                    if m == 0 and cc == 0:
                        e.wait_ge(s_x64.h, 16 * CC)
                    if m == 1 and cc == 0:
                        e.wait_ge(s_kev.h, 1)
                    mm = None
                    for n in range(4):
                        mm = e.matmul(kvps[n], xnT64[cc],
                                      wkb[:, 512 * n:512 * (n + 1)],
                                      start=(cc == 0), stop=(cc == CC - 1))
                    mm.then_inc(s_kmm.h, 1)
                s_kmm.bump(1)

            @em("v")
            def _(e, m=m):
                e.wait_ge(s_kmm.h, 16 * (m + 1))
                dst = k_tm if m == 0 else v_tmb
                mm = None
                for n in range(4):
                    mm = e.tensor_copy(out=dst[:, 512 * n:512 * (n + 1)],
                                       in_=kvps[n])
                mm.then_inc(s_kev.h, 1)
            s_kev.bump(1)

        @em("gp")
        def _(e):
            e.wait_ge(s_gw.h, AGW_WANT)
            e.wait_ge(s_aev.h, 8)
            e.collective_compute(
                "AllGather", OP.bypass, replica_groups=[list(range(8))],
                ins=[ag1_in.ap().opt()], outs=[w_scr.ap().opt()],
            ).then_inc(s_ag.h, 1)
            e.collective_compute(
                "AllGather", OP.bypass, replica_groups=[list(range(8))],
                ins=[ag2_in.ap().opt()], outs=[wp_scr.ap().opt()],
            ).then_inc(s_ag.h, 1)
        s_ag.bump(2)

        for h in range(H):
            @em("v")
            def _(e, h=h):
                if h == 0:
                    e.wait_ge(s_kev.h, 1)
                    e.wait_ge(s_c.h, CONSTS)
                else:
                    e.wait_ge(s_ktr.h, h)
                base = h * 128
                x0 = bass.AP(k_tm, base + 0, [[C, 64], [2, 64]])
                x1 = bass.AP(k_tm, base + 1, [[C, 64], [2, 64]])
                r0 = bass.AP(krh, 0, [[128, 64], [2, 64]])
                r1 = bass.AP(krh, 1, [[128, 64], [2, 64]])
                tmp = ktmp[:, :]
                e.tensor_mul(tmp, x0, c_sin[:, :])
                e.tensor_mul(r1, x1, c_cos[:, :])
                e.tensor_tensor(out=r1, in0=r1, in1=tmp, op=OP.add)
                e.tensor_mul(tmp, x1, c_sin[:, :])
                e.tensor_mul(r0, x0, c_cos[:, :])
                e.tensor_tensor(out=r0, in0=r0, in1=tmp,
                                op=OP.subtract).then_inc(s_rope.h, 1)
            s_rope.bump(1)

            @em("t")
            def _(e, h=h):
                e.wait_ge(s_rope.h, h + 1)
                if h >= 2:
                    e.wait_ge(s_ktev.h, h - 1)
                pt = (PSX, PSY)[h % 2]
                e.transpose(pt[:, 0:64], krh[:, :],
                            c_idf[0:64, 0:64]).then_inc(s_ktr.h, 1)
            s_ktr.bump(1)

            @em("v")
            def _(e, h=h):
                e.wait_ge(s_ktr.h, h + 1)
                pt = (PSX, PSY)[h % 2]
                e.tensor_copy(out=kTb[:, h * 64:(h + 1) * 64],
                              in_=pt[:, 0:64]).then_inc(s_ktev.h, 1)
            s_ktev.bump(1)

        for q in range(8):
            @em("t")
            def _(e, q=q):
                if q == 0:
                    e.wait_ge(s_ktev.h, H)
                else:
                    e.wait_ge(s_ss.h, q)
                e.matmul(PGA[0:64, 0:64], qTb[:, (2 * q) * 64:(2 * q + 1) * 64],
                         kTb[:, (2 * q) * 64:(2 * q + 1) * 64],
                         start=True, stop=True)
                e.matmul(PGA[64:128, 0:64], qTb[:, (2 * q + 1) * 64:(2 * q + 2) * 64],
                         kTb[:, (2 * q + 1) * 64:(2 * q + 2) * 64],
                         start=True, stop=True,
                         tile_position=(0, 64)).then_inc(s_scr.h, 1)
            s_scr.bump(1)

            @em("v")
            def _(e, q=q):
                e.wait_ge(s_scr.h, q + 1)
                if debug and q >= 1:
                    e.wait_ge(h_d1, 16 * q)
                e.scalar_tensor_tensor(out=ss_f[:], in0=PGA[:, 0:64], scalar=SCALE,
                                       in1=c_mask[:], op0=OP.mult, op1=OP.add)
                e.reduce_max(nmax[:], ss_f[:], axis=AXL.X, negate=True)
                e.memset(junkv[:], 0.0)
                e.tensor_scalar_add(out=ss_f[:], in0=ss_f[:],
                                    scalar1=nmax[:]).then_inc(s_ss.h, 1)
            s_ss.bump(1)

            if debug:
                @em("sy")
                def _(e, q=q):
                    e.wait_ge(s_ss.h, q + 1)
                    e.dma_start(out=ss_dbg.ap()[:, q * 64:(q + 1) * 64],
                                in_=ss_f[:]).then_inc(h_d1, 16)

            @em("s")
            def _(e, q=q):
                e.wait_ge(s_ss.h, q + 1)
                if debug and q >= 1:
                    e.wait_ge(h_d2, 16 * q)
                e.activation(out=w_b[:], in_=ss_f[:], func=AX.Exp,
                             accum_out=rs[:]).then_inc(s_we.h, 1)
            s_we.bump(1)

            @em("v")
            def _(e, q=q):
                e.wait_ge(s_we.h, q + 1)
                e.reciprocal(rr[:], rs[:])
                e.memset(junkv[:], 0.0)
                e.tensor_scalar_mul(out=w_b[:], in0=w_b[:],
                                    scalar1=rr[:]).then_inc(s_wn.h, 1)
            s_wn.bump(1)

            if debug:
                @em("sy")
                def _(e, q=q):
                    e.wait_ge(s_wn.h, q + 1)
                    e.dma_start(out=wb_dbg.ap()[:, q * 64:(q + 1) * 64],
                                in_=w_b[:]).then_inc(h_d2, 16)

            @em("t")
            def _(e, q=q):
                e.wait_ge(s_wn.h, q + 1)
                if q > 0:
                    e.wait_ge(s_wtev.h, q)
                e.transpose(PGB[0:64, 0:128], w_b[:],
                            c_idf[:, :]).then_inc(s_wt.h, 1)
            s_wt.bump(1)

            @em("v")
            def _(e, q=q):
                e.wait_ge(s_wt.h, q + 1)
                if debug and q >= 1:
                    e.wait_ge(h_d3, 16 * q)
                e.tensor_copy(out=wTp[:], in_=PGB[0:64, 0:128]).then_inc(s_wtev.h, 1)
            s_wtev.bump(1)

            if debug:
                @em("sy")
                def _(e, q=q):
                    e.wait_ge(s_wtev.h, q + 1)
                    e.dma_start(out=wtp_dbg.ap()[:, q * 128:(q + 1) * 128],
                                in_=wTp[:]).then_inc(h_d3, 16)

            @em("t")
            def _(e, q=q):
                e.wait_ge(s_wtev.h, q + 1)
                if q > 0:
                    e.wait_ge(s_aev.h, q)
                e.matmul(PB0[0:64, 0:128], wTp[:, 0:64],
                         v_tmb[:, (2 * q) * 128:(2 * q + 1) * 128],
                         start=True, stop=True)
                e.matmul(PB0[64:128, 0:128], wTp[:, 64:128],
                         v_tmb[:, (2 * q + 1) * 128:(2 * q + 2) * 128],
                         start=True, stop=True,
                         tile_position=(0, 64)).then_inc(s_am.h, 1)
            s_am.bump(1)

            @em("v")
            def _(e, q=q):
                e.wait_ge(s_am.h, q + 1)
                e.tensor_copy(out=attnP[:, q, :],
                              in_=PB0[:, 0:128]).then_inc(s_aev.h, 1)
            s_aev.bump(1)

        piece = 0
        for G in range(NG):
            @em("sy")
            def _(e, G=G):
                e.wait_ge(s_xnw.h, 16 * 4 * (G + 1))
                if G == 0:
                    e.wait_ge(s_kmm.h, 32)
                else:
                    e.wait_ge(s_gu.h, KC * G)
                for cc in range(CC):
                    e.dma_start(out=xnT[:, cc, :],
                                in_=xn_scr.ap()[G * GS:(G + 1) * GS,
                                                cc * 128:(cc + 1) * 128],
                                transpose=True).then_inc(s_xnt.h, 16)
            s_xnt.bump(16 * CC)

            for cc in range(CC):
                @em("gp")
                def _(e, G=G, cc=cc):
                    if 16 * G + cc >= 2:
                        e.wait_ge(s_gmm.h, 16 * G + cc - 1)
                    e.dma_start(out=wgsl[:, cc % 2, :],
                                in_=wg_h.ap()[cc * 128:(cc + 1) * 128, :]
                                ).then_inc(s_wgl.h, 16)
                s_wgl.bump(16)

                @em("t")
                def _(e, G=G, cc=cc):
                    e.wait_ge(s_wgl.h, 16 * (17 * G + cc + 2))
                    if cc == 0:
                        e.wait_ge(s_xnt.h, 16 * CC * (G + 1))
                        if G == 0:
                            e.wait_ge(s_aev.h, 8)
                        else:
                            e.wait_ge(s_oev.h, CT * G)
                            e.wait_ge(s_res.h, 64 * G)
                            e.wait_ge(s_si.h, KC * G)
                            e.wait_ge(s_ht.h, KC * G)
                            e.wait_ge(s_gn.h, 8 * G)
                            e.wait_ge(s_wev.h, H * G)
                            e.wait_ge(s_rcp.h, G)
                    mm = None
                    for gt in range(8):
                        mm = e.matmul(banks[gt][:, 0:GS],
                                      wgsl[:, cc % 2, gt * 128:(gt + 1) * 128],
                                      xnT[:, cc, :],
                                      start=(cc == 0), stop=(cc == CC - 1))
                    mm.then_inc(s_gmm.h, 1)
                s_gmm.bump(1)

            @em("gp")
            def _(e):
                e.dma_start(out=jnk2[:],
                            in_=wg_h.ap()[0:128, 0:16]).then_inc(s_wgl.h, 16)
            s_wgl.bump(16)

            for gt in range(8):
                @em("s")
                def _(e, G=G, gt=gt):
                    e.wait_ge(s_gmm.h, 16 * (G + 1))
                    if debug and G == 1 and gt == 0:
                        e.wait_ge(h_d4, 16)
                    e.activation(out=expg[:, gt, :], in_=banks[gt][:, 0:GS],
                                 func=AX.Exp, scale=SCALE).then_inc(s_gex.h, 1)
                s_gex.bump(1)

            @em("t")
            def _(e, G=G):
                e.wait_ge(s_gex.h, 8 * (G + 1))
                mm = None
                for q in range(8):
                    mm = e.matmul(PSX[0:16, 0:GS], c_ones[:, q * 16:(q + 1) * 16],
                                  expg[:, q, :], start=(q == 0), stop=(q == 7))
                mm.then_inc(s_smm.h, 1)
            s_smm.bump(1)

            @em("v")
            def _(e, G=G):
                e.wait_ge(s_smm.h, G + 1)
                e.reciprocal(rsum[:], PSX[0:16, 0:GS]).then_inc(s_rcp.h, 1)
            s_rcp.bump(1)

            for q in range(8):
                pb = (PB0, PB1)[q % 2]

                @em("t")
                def _(e, G=G, q=q, pb=pb):
                    if q == 0:
                        e.wait_ge(s_rcp.h, G + 1)
                    if 8 * G + q >= 2:
                        e.wait_ge(s_gn.h, 8 * G + q - 1)
                    e.matmul(pb[:, :], c_bcb[:, q * 128:(q + 1) * 128],
                             rsum[:, :], start=True, stop=True).then_inc(s_bc.h, 1)
                s_bc.bump(1)

                @em("v")
                def _(e, G=G, q=q, pb=pb):
                    e.wait_ge(s_bc.h, 8 * G + q + 1)
                    e.tensor_tensor(out=expg[:, q, :], in0=expg[:, q, :],
                                    in1=pb[:, 0:GS], op=OP.mult).then_inc(s_gn.h, 1)
                s_gn.bump(1)

            if debug and G == 0:
                @em("sy")
                def _(e):
                    e.wait_ge(s_gn.h, 8)
                    e.dma_start(out=exg_dbg.ap(),
                                in_=expg[:, :, :]).then_inc(h_d4, 16)

            for h in range(H):
                q = h // 2
                sl = slice(0, 64) if h % 2 == 0 else slice(64, 128)
                po = (PB0, PB1)[h % 2]

                @em("t")
                def _(e, G=G, h=h, q=q, sl=sl, po=po):
                    e.wait_ge(s_gn.h, 8 * (G + 1))
                    if H * G + h >= 2:
                        e.wait_ge(s_wev.h, H * G + h - 1)
                    e.matmul(po[:, 0:GS], attnP[sl, q, :], expg[sl, q, :],
                             start=True, stop=True).then_inc(s_wm.h, 1)
                s_wm.bump(1)

                @em("v")
                def _(e, G=G, h=h, po=po):
                    e.wait_ge(s_wm.h, H * G + h + 1)
                    if debug and G == 1 and h == 0:
                        e.wait_ge(h_d4, 32)
                    e.tensor_copy(out=wTg[:, h, :],
                                  in_=po[:, 0:GS]).then_inc(s_wev.h, 1)
                s_wev.bump(1)

            if debug and G == 0:
                @em("sy")
                def _(e):
                    e.wait_ge(s_wev.h, H)
                    e.dma_start(out=wtg_dbg.ap(),
                                in_=wTg[:, :, :]).then_inc(h_d4, 16)

            for it in range(KC):
                idx = KC * G + it

                @em("sy")
                def _(e, G=G, it=it, idx=idx):
                    if G == 0 and it == 0:
                        e.wait_ge(s_ag.h, 2)
                    if idx >= 2:
                        e.wait_ge(s_gu.h, idx - 1)
                    e.dma_start(out=gbuf[:, idx % 2, :],
                                in_=w_scr.ap()[it * 128:(it + 1) * 128, :]
                                ).then_inc(s_fs.h, 16)
                    e.dma_start(out=ubuf[:, idx % 2, :],
                                in_=w_scr.ap()[5504 + it * 128:5504 + (it + 1) * 128, :]
                                ).then_inc(s_fs.h, 16)
                s_fs.bump(32)

                @em("t")
                def _(e, G=G, it=it, idx=idx):
                    e.wait_ge(s_fs.h, 32 * (idx + 1))
                    if it == 0:
                        e.wait_ge(s_gex.h, 8 * (G + 1))
                    if idx >= 1:
                        e.wait_ge(s_si.h, idx)
                        e.wait_ge(s_ht.h, idx)
                    mm = None
                    for cc in range(CC):
                        mm = e.matmul(PGA[:, :],
                                      gbuf[:, idx % 2, cc * 128:(cc + 1) * 128],
                                      xnT[:, cc, :],
                                      start=(cc == 0), stop=(cc == CC - 1))
                    for cc in range(CC):
                        mm = e.matmul(PGB[:, :],
                                      ubuf[:, idx % 2, cc * 128:(cc + 1) * 128],
                                      xnT[:, cc, :],
                                      start=(cc == 0), stop=(cc == CC - 1))
                    mm.then_inc(s_gu.h, 1)
                s_gu.bump(1)

                @em("s")
                def _(e, idx=idx):
                    e.wait_ge(s_gu.h, idx + 1)
                    if idx >= 1:
                        e.wait_ge(s_ht.h, idx)
                    e.activation(out=sgb[:], in_=PGA[:, :],
                                 func=AX.Silu).then_inc(s_si.h, 1)
                s_si.bump(1)

                @em("v")
                def _(e, it=it, idx=idx):
                    e.wait_ge(s_si.h, idx + 1)
                    if debug and idx == KC:
                        e.wait_ge(h_d4, 48)
                    e.tensor_tensor(out=hT[:, it, :], in0=sgb[:],
                                    in1=PGB[:, :], op=OP.mult).then_inc(s_ht.h, 1)
                s_ht.bump(1)

            if debug and G == 0:
                @em("sy")
                def _(e):
                    e.wait_ge(s_ht.h, KC)
                    e.dma_start(out=ht_dbg.ap(),
                                in_=hT[:, :, :]).then_inc(h_d4, 16)

            for ct in range(CT):
                idx = CT * G + ct
                po = (PO0, PO1)[ct % 2]

                @em("sy")
                def _(e, G=G, ct=ct, idx=idx):
                    if idx >= 2:
                        e.wait_ge(s_omm.h, idx - 1)
                    e.dma_start(out=dbuf[:, idx % 2, :],
                                in_=bass.AP(w_scr, 11008 * 2048 + ct * 128 * INTER,
                                            [[INTER, 128], [1, INTER]])
                                ).then_inc(s_ds.h, 16)
                    e.dma_start(out=wpsl[:, idx % 2, :],
                                in_=wp_scr.ap()[ct * 128:(ct + 1) * 128, :]
                                ).then_inc(s_ds.h, 16)
                s_ds.bump(32)

                @em("t")
                def _(e, G=G, ct=ct, idx=idx, po=po):
                    e.wait_ge(s_ds.h, 32 * (idx + 1))
                    if ct == 0:
                        e.wait_ge(s_ht.h, KC * (G + 1))
                        e.wait_ge(s_wev.h, H * (G + 1))
                    if idx >= 2:
                        e.wait_ge(s_oev.h, idx - 1)
                    mm = None
                    for kc in range(KC):
                        mm = e.matmul(po[:, :],
                                      dbuf[:, idx % 2, kc * 128:(kc + 1) * 128],
                                      hT[:, kc, :], start=(kc == 0), stop=False)
                    for cc in range(CC):
                        mm = e.matmul(po[:, :],
                                      wpsl[:, idx % 2, cc * 128:(cc + 1) * 128],
                                      wTg[:, cc, :], start=False, stop=(cc == CC - 1))
                    mm.then_inc(s_omm.h, 1)
                s_omm.bump(1)

                @em("v")
                def _(e, idx=idx, po=po):
                    e.wait_ge(s_omm.h, idx + 1)
                    if idx >= 2:
                        e.wait_ge(s_tr.h, 4 * (idx - 1))
                    for st in (3, 2, 1, 0):
                        e.tensor_copy(out=oT_sb[:, idx % 2, st * 128:(st + 1) * 128],
                                      in_=po[:, st * 128:(st + 1) * 128])
                    e.memset(junkv[:], 0.0).then_inc(s_oev.h, 1)
                s_oev.bump(1)

                for tt in range(4):
                    p = piece
                    piece += 1

                    if p == 0:
                        @em("sy")
                        def _(e):
                            e.dma_start(out=xpc[:, 0, :],
                                        in_=x_h.ap()[0:128, 0:128]).then_inc(s_xp.h, 16)
                            e.dma_start(out=xpc[:, 1, :],
                                        in_=x_h.ap()[128:256, 0:128]).then_inc(s_xp.h, 16)
                        s_xp.bump(32)

                    @em("t")
                    def _(e, idx=idx, tt=tt, p=p):
                        e.wait_ge(s_oev.h, idx + 1)
                        if p >= 2:
                            e.wait_ge(s_res.h, p - 1)
                        pt = (PSX, PSY)[p % 2]
                        e.transpose(pt[:, 0:128],
                                    oT_sb[:, idx % 2, tt * 128:(tt + 1) * 128],
                                    c_idf[:, :]).then_inc(s_tr.h, 1)
                    s_tr.bump(1)

                    @em("v")
                    def _(e, p=p):
                        e.wait_ge(s_tr.h, p + 1)
                        e.wait_ge(s_xp.h, 16 * (p + 2))
                        if p >= 2:
                            e.wait_ge(s_out.h, 16 * (p - 1))
                        pt = (PSX, PSY)[p % 2]
                        e.tensor_tensor(out=res_sb[:, p % 2, :],
                                        in0=pt[:, 0:128], in1=xpc[:, p % 2, :],
                                        op=OP.add)
                        e.memset(junkv[:], 0.0).then_inc(s_res.h, 1)
                    s_res.bump(1)

                    @em("gp")
                    def _(e, G=G, ct=ct, tt=tt, p=p):
                        e.wait_ge(s_res.h, p + 1)
                        r0 = G * GS + tt * 128
                        e.dma_start(out=out_h.ap()[r0:r0 + 128, ct * 128:(ct + 1) * 128],
                                    in_=res_sb[:, p % 2, :]).then_inc(s_out.h, 16)
                    s_out.bump(16)

                    p2 = p + 2
                    @em("sy")
                    def _(e, p=p, p2=p2):
                        e.wait_ge(s_res.h, p + 1)
                        if p2 < NG * CT * 4:
                            G2 = p2 // 64
                            ct2 = (p2 % 64) // 4
                            tt2 = p2 % 4
                            r2 = G2 * GS + tt2 * 128
                            e.dma_start(out=xpc[:, p2 % 2, :],
                                        in_=x_h.ap()[r2:r2 + 128,
                                                     ct2 * 128:(ct2 + 1) * 128]
                                        ).then_inc(s_xp.h, 16)
                        else:
                            e.dma_start(out=junkx[:],
                                        in_=x_h.ap()[0:128, 0:16]
                                        ).then_inc(s_xp.h, 16)
                    s_xp.bump(16)

        @em("gp")
        def _(e, want=s_out.n):
            e.wait_ge(s_out.h, want)
            if debug:
                e.wait_ge(h_d1, 128)
                e.wait_ge(h_d2, 128)
                e.wait_ge(h_d3, 128)
                e.wait_ge(h_d4, 48)
                e.dma_start(out=w_dbg.ap(), in_=w_scr.ap()).then_inc(s_out.h, 16)
                e.dma_start(out=wp_dbg.ap(), in_=wp_scr.ap()).then_inc(s_out.h, 16)
                e.dma_start(out=ktb_dbg.ap(), in_=kTb[:]).then_inc(s_out.h, 16)
                e.dma_start(out=atp_dbg.ap(), in_=attnP[:, :, :]).then_inc(s_out.h, 16)
                e.dma_start(out=vtm_dbg.ap(), in_=v_tmb[:]).then_inc(s_out.h, 16)
                e.wait_ge(s_out.h, want + 5 * 16)

        @block.gpsimd
        def _(eng):
            for name, fn in script:
                if name == "gp":
                    fn(eng)

        @block.sync
        def _(eng):
            for name, fn in script:
                if name == "sy":
                    fn(eng)

        @block.vector
        def _(eng):
            for name, fn in script:
                if name == "v":
                    fn(eng)

        @block.scalar
        def _(eng):
            for name, fn in script:
                if name == "s":
                    fn(eng)

        @block.tensor
        def _(eng):
            for name, fn in script:
                if name == "t":
                    fn(eng)

    return nc


def kernel(L0_attn_x, L0_attn_latent_queries, L0_attn_Wk, L0_attn_Wv,
           L0_attn_Wg, L0_attn_Wp, L0_ffn_lut, L0_ffn_gate_idx,
           L0_ffn_up_idx, L0_ffn_down_idx):
    x = np.ascontiguousarray(np.asarray(L0_attn_x, dtype=np.float32))
    lut_vals = [float(v) for v in np.asarray(L0_ffn_lut, dtype=np.float32)]
    xs, idx_shards, wp_shards, qT = _host_shards(
        x, np.asarray(L0_attn_latent_queries, dtype=np.float32),
        np.asarray(L0_attn_Wp, dtype=np.float32),
        np.asarray(L0_ffn_gate_idx), np.asarray(L0_ffn_up_idx),
        np.asarray(L0_ffn_down_idx))
    identf, identb, maskp, cosT, sinT, onesb, bcb = _consts()

    wk = np.ascontiguousarray(np.asarray(L0_attn_Wk, dtype=np.float32))
    wv = np.ascontiguousarray(np.asarray(L0_attn_Wv, dtype=np.float32))
    wg = np.ascontiguousarray(np.asarray(L0_attn_Wg, dtype=np.float32))

    nc = build(lut_vals)
    in_maps = []
    for j in range(8):
        in_maps.append({
            "x": xs[j], "wk": wk, "wv": wv, "wg": wg, "qT": qT,
            "idxs": idx_shards[j], "wpa": wp_shards[j],
            "identf": identf, "identb": identb, "maskp": maskp,
            "cosT": cosT, "sinT": sinT, "onesb": onesb, "bcb": bcb,
        })
    res = run_bass_kernel_spmd(nc, in_maps, core_ids=list(range(8)))
    out = np.stack([np.asarray(r["out"], dtype=np.float32) for r in res.results],
                   axis=0)
    return out


# version 08
# speedup vs baseline: 1.0382x; 1.0382x over previous
import math

import numpy as np
import ml_dtypes

import concourse.bass as bass
import concourse.mybir as mybir
from concourse.bass_utils import run_bass_kernel_spmd

F32 = mybir.dt.float32
BF16 = mybir.dt.bfloat16
I32 = mybir.dt.int32
AX = mybir.ActivationFunctionType
OP = mybir.AluOpType
AXL = mybir.AxisListType

T = 2048
C = 2048
H = 16
L = 64
HL = H * L
INTER = 5504
KC = INTER // 128
CC = C // 128
CT = C // 128
NT = T // 128
NG = 4
GS = T // NG
SCALE = 1.0 / math.sqrt(128.0)
NEG = -1.0e9
EPS = 1e-5

BLOB_ROWS = 3 * 5504
SH_ROWS = BLOB_ROWS // 8
GTILES = 32
GFREE = SH_ROWS * C // (128 * GTILES)
WP_SH = C // 8


def _consts():
    identf = np.eye(128, dtype=np.float32)
    identb = np.eye(128, dtype=ml_dtypes.bfloat16)
    lidx = np.arange(128) % 64
    tidx = np.arange(64)
    maskp = np.where(tidx[None, :] <= lidx[:, None], 0.0, NEG).astype(np.float32)
    inv = (10000.0 ** (-(np.arange(64, dtype=np.float32)) / 64.0)).astype(np.float32)
    ang = np.arange(64, dtype=np.float32)[:, None] * inv[None, :]
    cosT = np.cos(ang).astype(np.float32)
    sinT = np.sin(ang).astype(np.float32)
    onesb = np.zeros((128, 128), dtype=ml_dtypes.bfloat16)
    for q in range(8):
        onesb[0:64, q * 16 + 2 * q] = 1
        onesb[64:128, q * 16 + 2 * q + 1] = 1
    bcb = np.zeros((16, 1024), dtype=np.float32)
    for q in range(8):
        bcb[2 * q, q * 128: q * 128 + 64] = 1.0
        bcb[2 * q + 1, q * 128 + 64: q * 128 + 128] = 1.0
    return identf, identb, maskp, cosT, sinT, onesb, bcb


def _host_shards(x, lq, wp, gidx, uidx, didx):
    def scr_gu(idx):
        a = np.ascontiguousarray(idx).T
        a = a.reshape(16, 128, 43, 128).transpose(2, 1, 0, 3)
        return np.ascontiguousarray(a.reshape(5504, 2048))

    def scr_dn(idx):
        a = np.ascontiguousarray(idx).T
        a = a.reshape(43, 128, 16, 128).transpose(2, 1, 0, 3)
        return np.ascontiguousarray(a.reshape(5504, 2048))

    blob = np.concatenate([scr_gu(gidx), scr_gu(uidx), scr_dn(didx)], axis=0)
    idx_shards = [np.ascontiguousarray(s) for s in np.split(blob, 8, axis=0)]
    wpa = (np.ascontiguousarray(wp).reshape(16, 128, 16, 128)
           .transpose(2, 1, 0, 3).reshape(2048, 2048))
    wp_shards = [np.ascontiguousarray(s) for s in np.split(wpa, 8, axis=0)]
    qT = np.ascontiguousarray(
        np.transpose(lq[0], (2, 1, 0)).reshape(128, HL)).astype(np.float32, copy=False)
    xs = [np.ascontiguousarray(x[j]) for j in range(8)]
    return xs, idx_shards, wp_shards, qT


class Sem:
    __slots__ = ("h", "n")

    def __init__(self, h):
        self.h = h
        self.n = 0

    def bump(self, k=1):
        self.n += k
        return self.n


def build(lut_vals, debug=False):
    nc = bass.Bass("TRN2", target_bir_lowering=False)
    dbg_kind = {"kind": "ExternalOutput"} if debug else {}

    x_h = nc.dram_tensor("x", [T, C], F32, kind="ExternalInput")
    wk_h = nc.dram_tensor("wk", [C, C], F32, kind="ExternalInput")
    wv_h = nc.dram_tensor("wv", [C, C], F32, kind="ExternalInput")
    wg_h = nc.dram_tensor("wg", [C, HL], F32, kind="ExternalInput")
    qT_h = nc.dram_tensor("qT", [128, HL], F32, kind="ExternalInput")
    idx_h = nc.dram_tensor("idxs", [SH_ROWS, C], I32, kind="ExternalInput")
    wpa_h = nc.dram_tensor("wpa", [WP_SH, C], F32, kind="ExternalInput")
    identf_h = nc.dram_tensor("identf", [128, 128], F32, kind="ExternalInput")
    identb_h = nc.dram_tensor("identb", [128, 128], BF16, kind="ExternalInput")
    maskp_h = nc.dram_tensor("maskp", [128, 64], F32, kind="ExternalInput")
    cos_h = nc.dram_tensor("cosT", [64, 64], F32, kind="ExternalInput")
    sin_h = nc.dram_tensor("sinT", [64, 64], F32, kind="ExternalInput")
    onesb_h = nc.dram_tensor("onesb", [128, 128], BF16, kind="ExternalInput")
    bcb_h = nc.dram_tensor("bcb", [16, HL], F32, kind="ExternalInput")
    out_h = nc.dram_tensor("out", [T, C], F32, kind="ExternalOutput")
    xn_scr = nc.dram_tensor("xn_scr", [T, C], BF16, **dbg_kind)
    ag1_in = nc.dram_tensor("ag1_in", [SH_ROWS, C], BF16)
    w_scr = nc.dram_tensor("w_scr", [BLOB_ROWS, C], BF16)
    ag2_in = nc.dram_tensor("ag2_in", [WP_SH, C], BF16)
    wp_scr = nc.dram_tensor("wp_scr", [C, C], BF16)
    if debug:
        w_dbg = nc.dram_tensor("w_dbg", [BLOB_ROWS, C], BF16, kind="ExternalOutput")
        wp_dbg = nc.dram_tensor("wp_dbg", [C, C], BF16, kind="ExternalOutput")
        ktb_dbg = nc.dram_tensor("ktb_dbg", [128, HL], BF16, kind="ExternalOutput")
        atp_dbg = nc.dram_tensor("atp_dbg", [128, 8 * 128], BF16, kind="ExternalOutput")
        vtm_dbg = nc.dram_tensor("vtm_dbg", [64, C], BF16, kind="ExternalOutput")
        ss_dbg = nc.dram_tensor("ss_dbg", [128, 8 * 64], F32, kind="ExternalOutput")
        wb_dbg = nc.dram_tensor("wb_dbg", [128, 8 * 64], F32, kind="ExternalOutput")
        wtp_dbg = nc.dram_tensor("wtp_dbg", [64, 8 * 128], BF16, kind="ExternalOutput")
        exg_dbg = nc.dram_tensor("exg_dbg", [128, 8 * GS], BF16, kind="ExternalOutput")
        wtg_dbg = nc.dram_tensor("wtg_dbg", [128, CC * GS], BF16, kind="ExternalOutput")
        ht_dbg = nc.dram_tensor("ht_dbg", [128, KC * GS], BF16, kind="ExternalOutput")

    script = []

    def em(eng):
        def reg(fn):
            script.append((eng, fn))
            return fn
        return reg

    from contextlib import ExitStack
    _es = ExitStack()
    with _es:
        c_idf = _es.enter_context(nc.sbuf_tensor("sb_identf", [128, 128], F32))
        HT_OFF = 229344 - KC * GS * 2
        hT = nc.alloc_sbuf_tensor_at("hT", [128, KC, GS], BF16, offset=HT_OFF)
        x_t = nc.alloc_sbuf_tensor_at("x_t", [128, C], F32, offset=HT_OFF)
        k_tm = nc.alloc_sbuf_tensor_at("k_tm", [64, C], F32, offset=HT_OFF + 8192)
        c_mask = _es.enter_context(nc.sbuf_tensor("sb_maskp", [128, 64], F32))
        c_cos = _es.enter_context(nc.sbuf_tensor("sb_cosT", [64, 64], F32))
        c_sin = _es.enter_context(nc.sbuf_tensor("sb_sinT", [64, 64], F32))
        c_ones = _es.enter_context(nc.sbuf_tensor("sb_onesb", [128, 128], BF16))
        c_bcb = _es.enter_context(nc.sbuf_tensor("sb_bcb", [16, HL], F32))
        qTb = _es.enter_context(nc.sbuf_tensor("qTb", [128, HL], BF16))
        gi_in = _es.enter_context(nc.sbuf_tensor("gi_in", [128, 2, GFREE], I32))
        xn_t = _es.enter_context(nc.sbuf_tensor("xn_t", [128, 2, C], BF16))
        ln_s = _es.enter_context(nc.sbuf_tensor("ln_s", [128, 10], F32))
        wkb = _es.enter_context(nc.sbuf_tensor("wkb", [128, C], BF16))
        wgb = _es.enter_context(nc.sbuf_tensor("wgb", [128, CC, HL], BF16))
        krh = _es.enter_context(nc.sbuf_tensor("krh", [64, 128], F32))
        ktmp = _es.enter_context(nc.sbuf_tensor("ktmp", [64, 64], F32))
        v_tmb = _es.enter_context(nc.sbuf_tensor("v_tmb", [64, C], BF16))
        kTb = _es.enter_context(nc.sbuf_tensor("kTb", [128, HL], BF16))
        ss_f = _es.enter_context(nc.sbuf_tensor("ss_f", [128, 64], F32))
        nmax = _es.enter_context(nc.sbuf_tensor("nmax", [128, 1], F32))
        w_b = _es.enter_context(nc.sbuf_tensor("w_b", [128, 64], F32))
        rs = _es.enter_context(nc.sbuf_tensor("rs", [128, 1], F32))
        rr = _es.enter_context(nc.sbuf_tensor("rr", [128, 1], F32))
        wTp = _es.enter_context(nc.sbuf_tensor("wTp", [64, 128], BF16))
        attnP = _es.enter_context(nc.sbuf_tensor("attnP", [128, 8, 128], BF16))
        xnT = _es.enter_context(nc.sbuf_tensor("xnT", [128, CC, GS], BF16))
        expg = _es.enter_context(nc.sbuf_tensor("expg", [128, 8, GS], BF16))
        rsum = _es.enter_context(nc.sbuf_tensor("rsum", [16, GS], F32))
        wTg = _es.enter_context(nc.sbuf_tensor("wTg", [128, CC, GS], BF16))
        gbuf = _es.enter_context(nc.sbuf_tensor("gbuf", [128, 2, C], BF16))
        ubuf = _es.enter_context(nc.sbuf_tensor("ubuf", [128, 2, C], BF16))
        dbuf = _es.enter_context(nc.sbuf_tensor("dbuf", [128, 2, INTER], BF16))
        wpsl = _es.enter_context(nc.sbuf_tensor("wpsl", [128, 2, C], BF16))
        sgb = _es.enter_context(nc.sbuf_tensor("sgb", [128, 2, GS], BF16))
        oT_sb = _es.enter_context(nc.sbuf_tensor("oT_sb", [128, 2, GS], F32))
        xpc = _es.enter_context(nc.sbuf_tensor("xpc", [128, 2, 128], F32))
        res_sb = _es.enter_context(nc.sbuf_tensor("res_sb", [128, 2, 128], F32))
        epsb = _es.enter_context(nc.sbuf_tensor("epsb", [128, 1], F32))
        junkv = _es.enter_context(nc.sbuf_tensor("junkv", [128, 1], F32))
        jnk2 = _es.enter_context(nc.sbuf_tensor("jnk2", [128, 16], BF16))
        junkx = _es.enter_context(nc.sbuf_tensor("junkx", [128, 16], F32))
        PGA = _es.enter_context(nc.psum_tensor("PGA", [128, 512], F32))
        PGB = _es.enter_context(nc.psum_tensor("PGB", [128, 512], F32))
        PB0 = _es.enter_context(nc.psum_tensor("PB0", [128, 512], F32))
        PB1 = _es.enter_context(nc.psum_tensor("PB1", [128, 512], F32))
        PO0 = _es.enter_context(nc.psum_tensor("PO0", [128, 512], F32))
        PO1 = _es.enter_context(nc.psum_tensor("PO1", [128, 512], F32))
        PSX = _es.enter_context(nc.psum_tensor("PSX", [128, 512], F32))
        PSY = _es.enter_context(nc.psum_tensor("PSY", [128, 512], F32))
        h_eps = _es.enter_context(nc.semaphore("s_eps"))
        h_c = _es.enter_context(nc.semaphore("s_c"))
        h_gl = _es.enter_context(nc.semaphore("s_gl"))
        h_gv = _es.enter_context(nc.semaphore("s_gv"))
        h_gw = _es.enter_context(nc.semaphore("s_gw"))
        h_ag = _es.enter_context(nc.semaphore("s_ag"))
        h_xl = _es.enter_context(nc.semaphore("s_xl"))
        h_app = _es.enter_context(nc.semaphore("s_app"))
        h_xnw = _es.enter_context(nc.semaphore("s_xnw"))
        h_x64 = _es.enter_context(nc.semaphore("s_x64"))
        h_wk = _es.enter_context(nc.semaphore("s_wk"))
        h_kmm = _es.enter_context(nc.semaphore("s_kmm"))
        h_kev = _es.enter_context(nc.semaphore("s_kev"))
        h_rope = _es.enter_context(nc.semaphore("s_rope"))
        h_ktr = _es.enter_context(nc.semaphore("s_ktr"))
        h_ktev = _es.enter_context(nc.semaphore("s_ktev"))
        h_scr = _es.enter_context(nc.semaphore("s_scr"))
        h_ss = _es.enter_context(nc.semaphore("s_ss"))
        h_we = _es.enter_context(nc.semaphore("s_we"))
        h_rr = _es.enter_context(nc.semaphore("s_rr"))
        h_wn = _es.enter_context(nc.semaphore("s_wn"))
        h_wt = _es.enter_context(nc.semaphore("s_wt"))
        h_wtev = _es.enter_context(nc.semaphore("s_wtev"))
        h_am = _es.enter_context(nc.semaphore("s_am"))
        h_aev = _es.enter_context(nc.semaphore("s_aev"))
        h_xnt = _es.enter_context(nc.semaphore("s_xnt"))
        h_wgl = _es.enter_context(nc.semaphore("s_wgl"))
        h_gmm = _es.enter_context(nc.semaphore("s_gmm"))
        h_gex = _es.enter_context(nc.semaphore("s_gex"))
        h_smm = _es.enter_context(nc.semaphore("s_smm"))
        h_rcp = _es.enter_context(nc.semaphore("s_rcp"))
        h_bc = _es.enter_context(nc.semaphore("s_bc"))
        h_gn = _es.enter_context(nc.semaphore("s_gn"))
        h_wm = _es.enter_context(nc.semaphore("s_wm"))
        h_wev = _es.enter_context(nc.semaphore("s_wev"))
        h_fs = _es.enter_context(nc.semaphore("s_fs"))
        h_gu = _es.enter_context(nc.semaphore("s_gu"))
        h_si = _es.enter_context(nc.semaphore("s_si"))
        h_ht = _es.enter_context(nc.semaphore("s_ht"))
        h_ds = _es.enter_context(nc.semaphore("s_ds"))
        h_omm = _es.enter_context(nc.semaphore("s_omm"))
        h_oev = _es.enter_context(nc.semaphore("s_oev"))
        h_tr = _es.enter_context(nc.semaphore("s_tr"))
        h_xp = _es.enter_context(nc.semaphore("s_xp"))
        h_res = _es.enter_context(nc.semaphore("s_res"))
        h_out = _es.enter_context(nc.semaphore("s_out"))
        h_d1 = _es.enter_context(nc.semaphore("s_d1"))
        h_d2 = _es.enter_context(nc.semaphore("s_d2"))
        h_d3 = _es.enter_context(nc.semaphore("s_d3"))
        h_d4 = _es.enter_context(nc.semaphore("s_d4"))
        block = _es.enter_context(nc.Block())
        s_c = Sem(h_c); s_gl = Sem(h_gl); s_gv = Sem(h_gv); s_gw = Sem(h_gw)
        s_ag = Sem(h_ag); s_xl = Sem(h_xl); s_app = Sem(h_app)
        s_xnw = Sem(h_xnw); s_x64 = Sem(h_x64); s_wk = Sem(h_wk)
        s_kmm = Sem(h_kmm); s_kev = Sem(h_kev); s_rope = Sem(h_rope)
        s_ktr = Sem(h_ktr); s_ktev = Sem(h_ktev); s_scr = Sem(h_scr)
        s_ss = Sem(h_ss); s_we = Sem(h_we); s_rr = Sem(h_rr); s_wn = Sem(h_wn)
        s_wt = Sem(h_wt); s_wtev = Sem(h_wtev); s_am = Sem(h_am)
        s_aev = Sem(h_aev); s_xnt = Sem(h_xnt)
        s_wgl = Sem(h_wgl); s_gmm = Sem(h_gmm); s_gex = Sem(h_gex)
        s_smm = Sem(h_smm); s_rcp = Sem(h_rcp); s_bc = Sem(h_bc)
        s_gn = Sem(h_gn); s_wm = Sem(h_wm); s_wev = Sem(h_wev)
        s_fs = Sem(h_fs); s_gu = Sem(h_gu); s_si = Sem(h_si)
        s_ht = Sem(h_ht); s_ds = Sem(h_ds); s_omm = Sem(h_omm)
        s_oev = Sem(h_oev); s_tr = Sem(h_tr); s_xp = Sem(h_xp)
        s_res = Sem(h_res); s_out = Sem(h_out)

        banks = [PGA, PGB, PB0, PB1, PO0, PO1, PSX, PSY]

        gw = [dbuf[:, 0, 0:GFREE], dbuf[:, 0, GFREE:2 * GFREE]]
        gi_bf = dbuf[:, 1, 0:GFREE]
        g_ct = dbuf[:, 1, GFREE:2 * GFREE]
        wp_stage = [gbuf[:, 0, :], gbuf[:, 1, :]]
        xnT64 = [xnT[:, cc, 0:64] for cc in range(CC)]

        @em("v")
        def _(e):
            e.memset(epsb[:], EPS).then_inc(h_eps, 1)

        def idx_tile_ap(k):
            return bass.AP(idx_h, k * 128 * GFREE, [[GFREE, 128], [1, GFREE]])

        def ag1_tile_ap(k):
            return bass.AP(ag1_in, k * 128 * GFREE, [[GFREE, 128], [1, GFREE]])

        @em("gp")
        def _(e):
            e.dma_start(out=gi_in[:, 0, :], in_=idx_tile_ap(0)).then_inc(s_gl.h, 16)
            e.dma_start(out=gi_in[:, 1, :], in_=idx_tile_ap(1)).then_inc(s_gl.h, 16)
            e.dma_start(out=c_idf[:], in_=identf_h.ap()).then_inc(s_c.h, 16)
            e.dma_start(out=c_mask[:], in_=maskp_h.ap()).then_inc(s_c.h, 16)
            e.dma_start(out=c_cos[:], in_=cos_h.ap()).then_inc(s_c.h, 16)
            e.dma_start(out=c_sin[:], in_=sin_h.ap()).then_inc(s_c.h, 16)
            e.dma_start(out=c_ones[:], in_=onesb_h.ap()).then_inc(s_c.h, 16)
            e.dma_start(out=c_bcb[:], in_=bcb_h.ap()).then_inc(s_c.h, 16)
            e.dma_start(out=qTb[:], in_=qT_h.ap()).then_inc(s_c.h, 16)
            e.dma_start(out=wp_stage[0], in_=wpa_h.ap()[0:128, :]).then_inc(s_c.h, 16)
            e.dma_start(out=wp_stage[1], in_=wpa_h.ap()[128:256, :]).then_inc(s_c.h, 16)
        s_gl.bump(32)
        s_c.bump(9 * 16)
        CONSTS = s_c.n

        @em("gp")
        def _(e):
            for cc in range(CC):
                e.dma_start(out=wgb[:, cc, :],
                            in_=wg_h.ap()[cc * 128:(cc + 1) * 128, :]
                            ).then_inc(s_wgl.h, 16)
            e.dma_start(out=jnk2[:],
                        in_=wg_h.ap()[0:128, 0:16]).then_inc(s_wgl.h, 16)
        s_wgl.bump(17 * 16)

        for k in range(GTILES):
            @em("v")
            def _(e, k=k):
                e.wait_ge(s_gl.h, 16 * (k + 1))
                if k == 0:
                    e.wait_ge(s_c.h, CONSTS)
                if k >= 2:
                    e.wait_ge(s_gw.h, 16 * (k - 1))
                e.tensor_copy(out=gi_bf, in_=gi_in[:, k % 2, :])
                e.tensor_scalar(out=gw[k % 2], in0=gi_bf,
                                scalar1=0.0, scalar2=lut_vals[0],
                                op0=OP.is_equal, op1=OP.mult)
                last = None
                for v in range(1, 16):
                    e.tensor_scalar(out=g_ct, in0=gi_bf,
                                    scalar1=float(v), scalar2=lut_vals[v],
                                    op0=OP.is_equal, op1=OP.mult)
                    last = e.tensor_add(gw[k % 2], gw[k % 2], g_ct)
                last.then_inc(s_gv.h, 1)
            s_gv.bump(1)

            @em("gp")
            def _(e, k=k):
                e.wait_ge(s_gv.h, k + 2)
                e.dma_start(out=ag1_tile_ap(k), in_=gw[k % 2]).then_inc(s_gw.h, 16)
                if k + 2 < GTILES:
                    e.dma_start(out=gi_in[:, k % 2, :],
                                in_=idx_tile_ap(k + 2)).then_inc(s_gl.h, 16)
            s_gw.bump(16)
            if k + 2 < GTILES:
                s_gl.bump(16)

        @em("v")
        def _(e):
            e.memset(junkv[:], 0.0).then_inc(s_gv.h, 1)
        s_gv.bump(1)

        @em("gp")
        def _(e, want=s_c.n):
            e.wait_ge(s_c.h, want)
            e.dma_start(out=ag2_in.ap()[0:128, :], in_=wp_stage[0]).then_inc(s_gw.h, 16)
            e.dma_start(out=ag2_in.ap()[128:256, :], in_=wp_stage[1]).then_inc(s_gw.h, 16)
        s_gw.bump(32)

        AGW_WANT = s_gw.n

        ssq = ln_s[:, 0:1]; ssum = ln_s[:, 1:2]; mean = ln_s[:, 2:3]
        nm2 = ln_s[:, 3:4]; varep = ln_s[:, 4:5]; rstd = ln_s[:, 5:6]
        t9 = ln_s[:, 6:7]; nmr = ln_s[:, 7:8]; lnjk = ln_s[:, 8:9]

        def emit_ln_tile(tt):
            @em("s")
            def _(e, tt=tt):
                if tt == 0:
                    e.wait_ge(h_eps, 1)
                if tt >= 1:
                    e.wait_ge(s_app.h, tt)
                e.dma_start(out=x_t[:],
                            in_=x_h.ap()[tt * 128:(tt + 1) * 128, :]
                            ).then_inc(s_xl.h, 16)
                e.wait_ge(s_xl.h, 16 * (tt + 1))
                if tt >= 2:
                    e.wait_ge(s_xnw.h, 16 * (tt - 1))
                e.activation(out=xn_t[:, tt % 2, :], in_=x_t[:], func=AX.Square,
                             accum_out=ssq)
                e.activation(out=xn_t[:, tt % 2, :], in_=x_t[:], func=AX.Identity,
                             accum_out=ssum)
                e.mul(out=mean, in_=ssum, mul=1.0 / C)
                e.activation(out=nm2, in_=mean, func=AX.Square)
                e.activation(out=nm2, in_=nm2, func=AX.Identity,
                             scale=-1.0, bias=epsb[:])
                e.activation(out=lnjk, in_=ssq, func=AX.Identity)
                e.activation(out=varep, in_=ssq, func=AX.Identity,
                             scale=1.0 / C, bias=nm2)
                e.activation(out=varep, in_=varep, func=AX.Ln)
                e.activation(out=rstd, in_=varep, func=AX.Exp, scale=-0.5)
                e.activation(out=t9, in_=mean, func=AX.Identity, scale=rstd)
                e.mul(out=nmr, in_=t9, mul=-1.0)
                e.activation(out=lnjk, in_=ssq, func=AX.Identity)
                e.activation(out=xn_t[:, tt % 2, :], in_=x_t[:], func=AX.Identity,
                             scale=rstd, bias=nmr)
                e.activation(out=lnjk, in_=ssq,
                             func=AX.Identity).then_inc(s_app.h, 1)
            s_xl.bump(16)
            s_app.bump(1)

            @em("sy")
            def _(e, tt=tt):
                e.wait_ge(s_app.h, tt + 1)
                e.dma_start(out=xn_scr.ap()[tt * 128:(tt + 1) * 128, :],
                            in_=xn_t[:, tt % 2, :]).then_inc(s_xnw.h, 16)
            s_xnw.bump(16)

        emit_ln_tile(0)

        @em("sy")
        def _(e):
            e.wait_ge(s_xnw.h, 16)
            e.wait_ge(s_c.h, CONSTS)
            for cc in range(CC):
                e.dma_start(out=xnT64[cc],
                            in_=xn_scr.ap()[0:64, cc * 128:(cc + 1) * 128],
                            transpose=True).then_inc(s_x64.h, 16)
        s_x64.bump(16 * CC)

        for tt in range(1, NT):
            emit_ln_tile(tt)

        kvps_m = ([PGA[0:64, :], PGB[0:64, :], PB0[0:64, :], PB1[0:64, :]],
                  [PO0[0:64, :], PO1[0:64, :], PSX[0:64, :], PSY[0:64, :]])
        for m in range(2):
            for cc in range(CC):
                i = m * CC + cc

                @em("gp")
                def _(e, m=m, cc=cc, i=i):
                    if i >= 1:
                        e.wait_ge(s_kmm.h, i)
                    e.dma_start(out=wkb[:],
                                in_=(wk_h if m == 0 else wv_h).ap()[cc * 128:(cc + 1) * 128, :]
                                ).then_inc(s_wk.h, 16)
                s_wk.bump(16)

                @em("t")
                def _(e, m=m, cc=cc, i=i):
                    e.wait_ge(s_wk.h, 16 * (i + 1))
                    if m == 0 and cc == 0:
                        e.wait_ge(s_x64.h, 16 * CC)
                    mm = None
                    for n in range(4):
                        mm = e.matmul(kvps_m[m][n], xnT64[cc],
                                      wkb[:, 512 * n:512 * (n + 1)],
                                      start=(cc == 0), stop=(cc == CC - 1))
                    mm.then_inc(s_kmm.h, 1)
                s_kmm.bump(1)

            @em("v")
            def _(e, m=m):
                e.wait_ge(s_kmm.h, 16 * (m + 1))
                dst = k_tm if m == 0 else v_tmb
                mm = None
                for n in range(4):
                    mm = e.tensor_copy(out=dst[:, 512 * n:512 * (n + 1)],
                                       in_=kvps_m[m][n])
                mm.then_inc(s_kev.h, 1)
            s_kev.bump(1)

        @em("gp")
        def _(e):
            e.wait_ge(s_gw.h, AGW_WANT)
            e.wait_ge(s_aev.h, 8)
            e.collective_compute(
                "AllGather", OP.bypass, replica_groups=[list(range(8))],
                ins=[ag1_in.ap().opt()], outs=[w_scr.ap().opt()],
            ).then_inc(s_ag.h, 1)
            e.collective_compute(
                "AllGather", OP.bypass, replica_groups=[list(range(8))],
                ins=[ag2_in.ap().opt()], outs=[wp_scr.ap().opt()],
            ).then_inc(s_ag.h, 1)
        s_ag.bump(2)

        for h in range(H):
            @em("v")
            def _(e, h=h):
                if h == 0:
                    e.wait_ge(s_kev.h, 1)
                    e.wait_ge(s_c.h, CONSTS)
                else:
                    e.wait_ge(s_ktr.h, h)
                base = h * 128
                x0 = bass.AP(k_tm, base + 0, [[C, 64], [2, 64]])
                x1 = bass.AP(k_tm, base + 1, [[C, 64], [2, 64]])
                r0 = bass.AP(krh, 0, [[128, 64], [2, 64]])
                r1 = bass.AP(krh, 1, [[128, 64], [2, 64]])
                tmp = ktmp[:, :]
                e.tensor_mul(tmp, x0, c_sin[:, :])
                e.tensor_mul(r1, x1, c_cos[:, :])
                e.tensor_tensor(out=r1, in0=r1, in1=tmp, op=OP.add)
                e.tensor_mul(tmp, x1, c_sin[:, :])
                e.tensor_mul(r0, x0, c_cos[:, :])
                e.tensor_tensor(out=r0, in0=r0, in1=tmp,
                                op=OP.subtract).then_inc(s_rope.h, 1)
            s_rope.bump(1)

            @em("t")
            def _(e, h=h):
                e.wait_ge(s_rope.h, h + 1)
                if h == 0:
                    e.wait_ge(s_kev.h, 2)
                if h >= 2:
                    e.wait_ge(s_ktev.h, h - 1)
                pt = (PSX, PSY)[h % 2]
                e.transpose(pt[:, 0:64], krh[:, :],
                            c_idf[0:64, 0:64]).then_inc(s_ktr.h, 1)
            s_ktr.bump(1)

            @em("v")
            def _(e, h=h):
                e.wait_ge(s_ktr.h, h + 1)
                pt = (PSX, PSY)[h % 2]
                e.tensor_copy(out=kTb[:, h * 64:(h + 1) * 64],
                              in_=pt[:, 0:64]).then_inc(s_ktev.h, 1)
            s_ktev.bump(1)

        for q in range(8):
            @em("t")
            def _(e, q=q):
                if q == 0:
                    e.wait_ge(s_ktev.h, H)
                else:
                    e.wait_ge(s_ss.h, q)
                e.matmul(PGA[0:64, 0:64], qTb[:, (2 * q) * 64:(2 * q + 1) * 64],
                         kTb[:, (2 * q) * 64:(2 * q + 1) * 64],
                         start=True, stop=True)
                e.matmul(PGA[64:128, 0:64], qTb[:, (2 * q + 1) * 64:(2 * q + 2) * 64],
                         kTb[:, (2 * q + 1) * 64:(2 * q + 2) * 64],
                         start=True, stop=True,
                         tile_position=(0, 64)).then_inc(s_scr.h, 1)
            s_scr.bump(1)

            @em("v")
            def _(e, q=q):
                e.wait_ge(s_scr.h, q + 1)
                if debug and q >= 1:
                    e.wait_ge(h_d1, 16 * q)
                e.scalar_tensor_tensor(out=ss_f[:], in0=PGA[:, 0:64], scalar=SCALE,
                                       in1=c_mask[:], op0=OP.mult, op1=OP.add)
                e.reduce_max(nmax[:], ss_f[:], axis=AXL.X, negate=True)
                e.memset(junkv[:], 0.0)
                e.tensor_scalar_add(out=ss_f[:], in0=ss_f[:],
                                    scalar1=nmax[:]).then_inc(s_ss.h, 1)
            s_ss.bump(1)

            if debug:
                @em("sy")
                def _(e, q=q):
                    e.wait_ge(s_ss.h, q + 1)
                    e.dma_start(out=ss_dbg.ap()[:, q * 64:(q + 1) * 64],
                                in_=ss_f[:]).then_inc(h_d1, 16)

            @em("s")
            def _(e, q=q):
                e.wait_ge(s_ss.h, q + 1)
                if debug and q >= 1:
                    e.wait_ge(h_d2, 16 * q)
                e.activation(out=w_b[:], in_=ss_f[:], func=AX.Exp,
                             accum_out=rs[:]).then_inc(s_we.h, 1)
            s_we.bump(1)

            @em("v")
            def _(e, q=q):
                e.wait_ge(s_we.h, q + 1)
                e.reciprocal(rr[:], rs[:])
                e.memset(junkv[:], 0.0)
                e.tensor_scalar_mul(out=w_b[:], in0=w_b[:],
                                    scalar1=rr[:]).then_inc(s_wn.h, 1)
            s_wn.bump(1)

            if debug:
                @em("sy")
                def _(e, q=q):
                    e.wait_ge(s_wn.h, q + 1)
                    e.dma_start(out=wb_dbg.ap()[:, q * 64:(q + 1) * 64],
                                in_=w_b[:]).then_inc(h_d2, 16)

            @em("t")
            def _(e, q=q):
                e.wait_ge(s_wn.h, q + 1)
                if q > 0:
                    e.wait_ge(s_wtev.h, q)
                e.transpose(PGB[0:64, 0:128], w_b[:],
                            c_idf[:, :]).then_inc(s_wt.h, 1)
            s_wt.bump(1)

            @em("v")
            def _(e, q=q):
                e.wait_ge(s_wt.h, q + 1)
                if debug and q >= 1:
                    e.wait_ge(h_d3, 16 * q)
                e.tensor_copy(out=wTp[:], in_=PGB[0:64, 0:128]).then_inc(s_wtev.h, 1)
            s_wtev.bump(1)

            if debug:
                @em("sy")
                def _(e, q=q):
                    e.wait_ge(s_wtev.h, q + 1)
                    e.dma_start(out=wtp_dbg.ap()[:, q * 128:(q + 1) * 128],
                                in_=wTp[:]).then_inc(h_d3, 16)

            @em("t")
            def _(e, q=q):
                e.wait_ge(s_wtev.h, q + 1)
                if q > 0:
                    e.wait_ge(s_aev.h, q)
                e.matmul(PB0[0:64, 0:128], wTp[:, 0:64],
                         v_tmb[:, (2 * q) * 128:(2 * q + 1) * 128],
                         start=True, stop=True)
                e.matmul(PB0[64:128, 0:128], wTp[:, 64:128],
                         v_tmb[:, (2 * q + 1) * 128:(2 * q + 2) * 128],
                         start=True, stop=True,
                         tile_position=(0, 64)).then_inc(s_am.h, 1)
            s_am.bump(1)

            @em("v")
            def _(e, q=q):
                e.wait_ge(s_am.h, q + 1)
                e.tensor_copy(out=attnP[:, q, :],
                              in_=PB0[:, 0:128]).then_inc(s_aev.h, 1)
            s_aev.bump(1)

        piece = 0
        for G in range(NG):
            @em("sy")
            def _(e, G=G):
                e.wait_ge(s_xnw.h, 16 * 4 * (G + 1))
                if G == 0:
                    e.wait_ge(s_kmm.h, 32)
                else:
                    e.wait_ge(s_gu.h, KC * G)
                for cc in range(CC):
                    e.dma_start(out=xnT[:, cc, :],
                                in_=xn_scr.ap()[G * GS:(G + 1) * GS,
                                                cc * 128:(cc + 1) * 128],
                                transpose=True).then_inc(s_xnt.h, 16)
            s_xnt.bump(16 * CC)

            for cc in range(CC):
                @em("t")
                def _(e, G=G, cc=cc):
                    if G == 0 and cc == 0:
                        e.wait_ge(s_wgl.h, 16 * (CC + 1))
                    if cc == 0:
                        e.wait_ge(s_xnt.h, 16 * CC * (G + 1))
                        if G == 0:
                            e.wait_ge(s_aev.h, 8)
                        else:
                            e.wait_ge(s_oev.h, CT * G)
                            e.wait_ge(s_res.h, 64 * G)
                            e.wait_ge(s_si.h, KC * G)
                            e.wait_ge(s_ht.h, KC * G)
                            e.wait_ge(s_gn.h, 8 * G)
                            e.wait_ge(s_wev.h, H * G)
                            e.wait_ge(s_rcp.h, G)
                    mm = None
                    for gt in range(8):
                        mm = e.matmul(banks[gt][:, 0:GS],
                                      wgb[:, cc, gt * 128:(gt + 1) * 128],
                                      xnT[:, cc, :],
                                      start=(cc == 0), stop=(cc == CC - 1))
                    mm.then_inc(s_gmm.h, 1)
                s_gmm.bump(1)

            for gt in range(8):
                @em("s")
                def _(e, G=G, gt=gt):
                    e.wait_ge(s_gmm.h, 16 * (G + 1))
                    if debug and G == 1 and gt == 0:
                        e.wait_ge(h_d4, 16)
                    e.activation(out=expg[:, gt, :], in_=banks[gt][:, 0:GS],
                                 func=AX.Exp, scale=SCALE).then_inc(s_gex.h, 1)
                s_gex.bump(1)

            @em("t")
            def _(e, G=G):
                e.wait_ge(s_gex.h, 8 * (G + 1))
                mm = None
                for q in range(8):
                    mm = e.matmul(PSX[0:16, 0:GS], c_ones[:, q * 16:(q + 1) * 16],
                                  expg[:, q, :], start=(q == 0), stop=(q == 7))
                mm.then_inc(s_smm.h, 1)
            s_smm.bump(1)

            @em("v")
            def _(e, G=G):
                e.wait_ge(s_smm.h, G + 1)
                e.reciprocal(rsum[:], PSX[0:16, 0:GS]).then_inc(s_rcp.h, 1)
            s_rcp.bump(1)

            for q in range(8):
                pb = (PB0, PB1)[q % 2]

                @em("t")
                def _(e, G=G, q=q, pb=pb):
                    if q == 0:
                        e.wait_ge(s_rcp.h, G + 1)
                    if 8 * G + q >= 2:
                        e.wait_ge(s_gn.h, 8 * G + q - 1)
                    e.matmul(pb[:, :], c_bcb[:, q * 128:(q + 1) * 128],
                             rsum[:, :], start=True, stop=True).then_inc(s_bc.h, 1)
                s_bc.bump(1)

                @em("v")
                def _(e, G=G, q=q, pb=pb):
                    e.wait_ge(s_bc.h, 8 * G + q + 1)
                    e.tensor_tensor(out=expg[:, q, :], in0=expg[:, q, :],
                                    in1=pb[:, 0:GS], op=OP.mult).then_inc(s_gn.h, 1)
                s_gn.bump(1)

            if debug and G == 0:
                @em("sy")
                def _(e):
                    e.wait_ge(s_gn.h, 8)
                    e.dma_start(out=exg_dbg.ap(),
                                in_=expg[:, :, :]).then_inc(h_d4, 16)

            for h in range(H):
                q = h // 2
                sl = slice(0, 64) if h % 2 == 0 else slice(64, 128)
                po = (PB0, PB1)[h % 2]

                @em("t")
                def _(e, G=G, h=h, q=q, sl=sl, po=po):
                    e.wait_ge(s_gn.h, 8 * (G + 1))
                    if H * G + h >= 2:
                        e.wait_ge(s_wev.h, H * G + h - 1)
                    e.matmul(po[:, 0:GS], attnP[sl, q, :], expg[sl, q, :],
                             start=True, stop=True).then_inc(s_wm.h, 1)
                s_wm.bump(1)

                @em("v")
                def _(e, G=G, h=h, po=po):
                    e.wait_ge(s_wm.h, H * G + h + 1)
                    if debug and G == 1 and h == 0:
                        e.wait_ge(h_d4, 32)
                    e.tensor_copy(out=wTg[:, h, :],
                                  in_=po[:, 0:GS]).then_inc(s_wev.h, 1)
                s_wev.bump(1)

            if debug and G == 0:
                @em("sy")
                def _(e):
                    e.wait_ge(s_wev.h, H)
                    e.dma_start(out=wtg_dbg.ap(),
                                in_=wTg[:, :, :]).then_inc(h_d4, 16)

            for it in range(KC):
                idx = KC * G + it

                @em("sy")
                def _(e, G=G, it=it, idx=idx):
                    if G == 0 and it == 0:
                        e.wait_ge(s_ag.h, 2)
                    if idx >= 2:
                        e.wait_ge(s_gu.h, idx - 1)
                    e.dma_start(out=gbuf[:, idx % 2, :],
                                in_=w_scr.ap()[it * 128:(it + 1) * 128, :]
                                ).then_inc(s_fs.h, 16)
                    e.dma_start(out=ubuf[:, idx % 2, :],
                                in_=w_scr.ap()[5504 + it * 128:5504 + (it + 1) * 128, :]
                                ).then_inc(s_fs.h, 16)
                s_fs.bump(32)

                gpair = (PGA, PGB)

                @em("t")
                def _(e, G=G, it=it, idx=idx, gpair=gpair):
                    e.wait_ge(s_fs.h, 32 * (idx + 1))
                    if it == 0:
                        e.wait_ge(s_gex.h, 8 * (G + 1))
                    if idx >= 1:
                        e.wait_ge(s_si.h, idx)
                        e.wait_ge(s_ht.h, idx)
                    mm = None
                    for cc in range(CC):
                        mm = e.matmul(gpair[0][:, :],
                                      gbuf[:, idx % 2, cc * 128:(cc + 1) * 128],
                                      xnT[:, cc, :],
                                      start=(cc == 0), stop=(cc == CC - 1))
                    for cc in range(CC):
                        mm = e.matmul(gpair[1][:, :],
                                      ubuf[:, idx % 2, cc * 128:(cc + 1) * 128],
                                      xnT[:, cc, :],
                                      start=(cc == 0), stop=(cc == CC - 1))
                    mm.then_inc(s_gu.h, 1)
                s_gu.bump(1)

                @em("s")
                def _(e, idx=idx, gpair=gpair):
                    e.wait_ge(s_gu.h, idx + 1)
                    if idx >= 1:
                        e.wait_ge(s_ht.h, idx)
                    e.activation(out=sgb[:, idx % 2, :], in_=gpair[0][:, :],
                                 func=AX.Silu).then_inc(s_si.h, 1)
                s_si.bump(1)

                @em("v")
                def _(e, it=it, idx=idx, gpair=gpair):
                    e.wait_ge(s_si.h, idx + 1)
                    if idx == 0:
                        e.wait_ge(s_app.h, NT)
                    if debug and idx == KC:
                        e.wait_ge(h_d4, 48)
                    e.tensor_tensor(out=hT[:, it, :], in0=sgb[:, idx % 2, :],
                                    in1=gpair[1][:, :], op=OP.mult).then_inc(s_ht.h, 1)
                s_ht.bump(1)

            if debug and G == 0:
                @em("sy")
                def _(e):
                    e.wait_ge(s_ht.h, KC)
                    e.dma_start(out=ht_dbg.ap(),
                                in_=hT[:, :, :]).then_inc(h_d4, 16)

            for ct in range(CT):
                idx = CT * G + ct
                po = (PO0, PO1)[ct % 2]

                @em("sy")
                def _(e, G=G, ct=ct, idx=idx):
                    if idx >= 2:
                        e.wait_ge(s_omm.h, idx - 1)
                    e.dma_start(out=dbuf[:, idx % 2, :],
                                in_=bass.AP(w_scr, 11008 * 2048 + ct * 128 * INTER,
                                            [[INTER, 128], [1, INTER]])
                                ).then_inc(s_ds.h, 16)
                    e.dma_start(out=wpsl[:, idx % 2, :],
                                in_=wp_scr.ap()[ct * 128:(ct + 1) * 128, :]
                                ).then_inc(s_ds.h, 16)
                s_ds.bump(32)

                @em("t")
                def _(e, G=G, ct=ct, idx=idx, po=po):
                    e.wait_ge(s_ds.h, 32 * (idx + 1))
                    if ct == 0:
                        e.wait_ge(s_ht.h, KC * (G + 1))
                        e.wait_ge(s_wev.h, H * (G + 1))
                    if idx >= 2:
                        e.wait_ge(s_oev.h, idx - 1)
                    mm = None
                    for kc in range(KC):
                        mm = e.matmul(po[:, :],
                                      dbuf[:, idx % 2, kc * 128:(kc + 1) * 128],
                                      hT[:, kc, :], start=(kc == 0), stop=False)
                    for cc in range(CC):
                        mm = e.matmul(po[:, :],
                                      wpsl[:, idx % 2, cc * 128:(cc + 1) * 128],
                                      wTg[:, cc, :], start=False, stop=(cc == CC - 1))
                    mm.then_inc(s_omm.h, 1)
                s_omm.bump(1)

                @em("v")
                def _(e, idx=idx, po=po):
                    e.wait_ge(s_omm.h, idx + 1)
                    if idx >= 2:
                        e.wait_ge(s_tr.h, 4 * (idx - 1))
                    for st in (3, 2, 1, 0):
                        e.tensor_copy(out=oT_sb[:, idx % 2, st * 128:(st + 1) * 128],
                                      in_=po[:, st * 128:(st + 1) * 128])
                    e.memset(junkv[:], 0.0).then_inc(s_oev.h, 1)
                s_oev.bump(1)

                for tt in range(4):
                    p = piece
                    piece += 1

                    if p == 0:
                        @em("sy")
                        def _(e):
                            e.dma_start(out=xpc[:, 0, :],
                                        in_=x_h.ap()[0:128, 0:128]).then_inc(s_xp.h, 16)
                            e.dma_start(out=xpc[:, 1, :],
                                        in_=x_h.ap()[128:256, 0:128]).then_inc(s_xp.h, 16)
                        s_xp.bump(32)

                    @em("t")
                    def _(e, idx=idx, tt=tt, p=p):
                        e.wait_ge(s_oev.h, idx + 1)
                        if p >= 2:
                            e.wait_ge(s_res.h, p - 1)
                        pt = (PSX, PSY)[p % 2]
                        e.transpose(pt[:, 0:128],
                                    oT_sb[:, idx % 2, tt * 128:(tt + 1) * 128],
                                    c_idf[:, :]).then_inc(s_tr.h, 1)
                    s_tr.bump(1)

                    @em("v")
                    def _(e, p=p):
                        e.wait_ge(s_tr.h, p + 1)
                        e.wait_ge(s_xp.h, 16 * (p + 2))
                        if p >= 2:
                            e.wait_ge(s_out.h, 16 * (p - 1))
                        pt = (PSX, PSY)[p % 2]
                        e.tensor_tensor(out=res_sb[:, p % 2, :],
                                        in0=pt[:, 0:128], in1=xpc[:, p % 2, :],
                                        op=OP.add)
                        e.memset(junkv[:], 0.0).then_inc(s_res.h, 1)
                    s_res.bump(1)

                    @em("gp")
                    def _(e, G=G, ct=ct, tt=tt, p=p):
                        e.wait_ge(s_res.h, p + 1)
                        r0 = G * GS + tt * 128
                        e.dma_start(out=out_h.ap()[r0:r0 + 128, ct * 128:(ct + 1) * 128],
                                    in_=res_sb[:, p % 2, :]).then_inc(s_out.h, 16)
                    s_out.bump(16)

                    p2 = p + 2
                    @em("sy")
                    def _(e, p=p, p2=p2):
                        e.wait_ge(s_res.h, p + 1)
                        if p2 < NG * CT * 4:
                            G2 = p2 // 64
                            ct2 = (p2 % 64) // 4
                            tt2 = p2 % 4
                            r2 = G2 * GS + tt2 * 128
                            e.dma_start(out=xpc[:, p2 % 2, :],
                                        in_=x_h.ap()[r2:r2 + 128,
                                                     ct2 * 128:(ct2 + 1) * 128]
                                        ).then_inc(s_xp.h, 16)
                        else:
                            e.dma_start(out=junkx[:],
                                        in_=x_h.ap()[0:128, 0:16]
                                        ).then_inc(s_xp.h, 16)
                    s_xp.bump(16)

        @em("gp")
        def _(e, want=s_out.n):
            e.wait_ge(s_out.h, want)
            if debug:
                e.wait_ge(h_d1, 128)
                e.wait_ge(h_d2, 128)
                e.wait_ge(h_d3, 128)
                e.wait_ge(h_d4, 48)
                e.dma_start(out=w_dbg.ap(), in_=w_scr.ap()).then_inc(s_out.h, 16)
                e.dma_start(out=wp_dbg.ap(), in_=wp_scr.ap()).then_inc(s_out.h, 16)
                e.dma_start(out=ktb_dbg.ap(), in_=kTb[:]).then_inc(s_out.h, 16)
                e.dma_start(out=atp_dbg.ap(), in_=attnP[:, :, :]).then_inc(s_out.h, 16)
                e.dma_start(out=vtm_dbg.ap(), in_=v_tmb[:]).then_inc(s_out.h, 16)
                e.wait_ge(s_out.h, want + 5 * 16)

        @block.gpsimd
        def _(eng):
            for name, fn in script:
                if name == "gp":
                    fn(eng)

        @block.sync
        def _(eng):
            for name, fn in script:
                if name == "sy":
                    fn(eng)

        @block.vector
        def _(eng):
            for name, fn in script:
                if name == "v":
                    fn(eng)

        @block.scalar
        def _(eng):
            for name, fn in script:
                if name == "s":
                    fn(eng)

        @block.tensor
        def _(eng):
            for name, fn in script:
                if name == "t":
                    fn(eng)

    return nc


def kernel(L0_attn_x, L0_attn_latent_queries, L0_attn_Wk, L0_attn_Wv,
           L0_attn_Wg, L0_attn_Wp, L0_ffn_lut, L0_ffn_gate_idx,
           L0_ffn_up_idx, L0_ffn_down_idx):
    x = np.ascontiguousarray(np.asarray(L0_attn_x, dtype=np.float32))
    lut_vals = [float(v) for v in np.asarray(L0_ffn_lut, dtype=np.float32)]
    xs, idx_shards, wp_shards, qT = _host_shards(
        x, np.asarray(L0_attn_latent_queries, dtype=np.float32),
        np.asarray(L0_attn_Wp, dtype=np.float32),
        np.asarray(L0_ffn_gate_idx), np.asarray(L0_ffn_up_idx),
        np.asarray(L0_ffn_down_idx))
    identf, identb, maskp, cosT, sinT, onesb, bcb = _consts()

    wk = np.ascontiguousarray(np.asarray(L0_attn_Wk, dtype=np.float32))
    wv = np.ascontiguousarray(np.asarray(L0_attn_Wv, dtype=np.float32))
    wg = np.ascontiguousarray(np.asarray(L0_attn_Wg, dtype=np.float32))

    nc = build(lut_vals)
    in_maps = []
    for j in range(8):
        in_maps.append({
            "x": xs[j], "wk": wk, "wv": wv, "wg": wg, "qT": qT,
            "idxs": idx_shards[j], "wpa": wp_shards[j],
            "identf": identf, "identb": identb, "maskp": maskp,
            "cosT": cosT, "sinT": sinT, "onesb": onesb, "bcb": bcb,
        })
    res = run_bass_kernel_spmd(nc, in_maps, core_ids=list(range(8)))
    out = np.stack([np.asarray(r["out"], dtype=np.float32) for r in res.results],
                   axis=0)
    return out


# version 09
# speedup vs baseline: 1.1239x; 1.0826x over previous
import math

import numpy as np
import ml_dtypes

import concourse.bass as bass
import concourse.mybir as mybir
from concourse.bass_utils import run_bass_kernel_spmd

F32 = mybir.dt.float32
BF16 = mybir.dt.bfloat16
I32 = mybir.dt.int32
AX = mybir.ActivationFunctionType
OP = mybir.AluOpType
AXL = mybir.AxisListType

T = 2048
C = 2048
H = 16
L = 64
HL = H * L
INTER = 5504
KC = INTER // 128
CC = C // 128
CT = C // 128
NT = T // 128
NG = 4
GS = T // NG
SCALE = 1.0 / math.sqrt(128.0)
NEG = -1.0e9
EPS = 1e-5

BLOB_ROWS = 3 * 5504
SH_ROWS = BLOB_ROWS // 8
GTILES = 32
GFREE = SH_ROWS * C // (128 * GTILES)
WP_SH = C // 8


def _consts():
    identf = np.eye(128, dtype=np.float32)
    identb = np.eye(128, dtype=ml_dtypes.bfloat16)
    lidx = np.arange(128) % 64
    tidx = np.arange(64)
    maskp = np.where(tidx[None, :] <= lidx[:, None], 0.0, NEG).astype(np.float32)
    inv = (10000.0 ** (-(np.arange(64, dtype=np.float32)) / 64.0)).astype(np.float32)
    ang = np.arange(64, dtype=np.float32)[:, None] * inv[None, :]
    cosT = np.cos(ang).astype(np.float32)
    sinT = np.sin(ang).astype(np.float32)
    onesb = np.zeros((128, 128), dtype=ml_dtypes.bfloat16)
    for q in range(8):
        onesb[0:64, q * 16 + 2 * q] = 1
        onesb[64:128, q * 16 + 2 * q + 1] = 1
    bcb = np.zeros((16, 1024), dtype=np.float32)
    for q in range(8):
        bcb[2 * q, q * 128: q * 128 + 64] = 1.0
        bcb[2 * q + 1, q * 128 + 64: q * 128 + 128] = 1.0
    return identf, identb, maskp, cosT, sinT, onesb, bcb


def _host_shards(x, lq, wp, gidx, uidx, didx):
    def scr_gu(idx):
        a = np.ascontiguousarray(idx).T
        a = a.reshape(16, 128, 43, 128).transpose(2, 1, 0, 3)
        return np.ascontiguousarray(a.reshape(5504, 2048))

    def scr_dn(idx):
        a = np.ascontiguousarray(idx).T
        a = a.reshape(43, 128, 16, 128).transpose(2, 1, 0, 3)
        return np.ascontiguousarray(a.reshape(5504, 2048))

    blob = np.concatenate([scr_gu(gidx), scr_gu(uidx), scr_dn(didx)], axis=0)
    idx_shards = [np.ascontiguousarray(s) for s in np.split(blob, 8, axis=0)]
    wpa = (np.ascontiguousarray(wp).reshape(16, 128, 16, 128)
           .transpose(2, 1, 0, 3).reshape(2048, 2048))
    wp_shards = [np.ascontiguousarray(s) for s in np.split(wpa, 8, axis=0)]
    qT = np.ascontiguousarray(
        np.transpose(lq[0], (2, 1, 0)).reshape(128, HL)).astype(np.float32, copy=False)
    xs = [np.ascontiguousarray(x[j]) for j in range(8)]
    return xs, idx_shards, wp_shards, qT


class Sem:
    __slots__ = ("h", "n")

    def __init__(self, h):
        self.h = h
        self.n = 0

    def bump(self, k=1):
        self.n += k
        return self.n


def build(lut_vals, debug=False):
    nc = bass.Bass("TRN2", target_bir_lowering=False)
    dbg_kind = {"kind": "ExternalOutput"} if debug else {}

    x_h = nc.dram_tensor("x", [T, C], F32, kind="ExternalInput")
    wk_h = nc.dram_tensor("wk", [C, C], F32, kind="ExternalInput")
    wv_h = nc.dram_tensor("wv", [C, C], F32, kind="ExternalInput")
    wg_h = nc.dram_tensor("wg", [C, HL], F32, kind="ExternalInput")
    qT_h = nc.dram_tensor("qT", [128, HL], F32, kind="ExternalInput")
    idx_h = nc.dram_tensor("idxs", [SH_ROWS, C], I32, kind="ExternalInput")
    wpa_h = nc.dram_tensor("wpa", [WP_SH, C], F32, kind="ExternalInput")
    identf_h = nc.dram_tensor("identf", [128, 128], F32, kind="ExternalInput")
    identb_h = nc.dram_tensor("identb", [128, 128], BF16, kind="ExternalInput")
    maskp_h = nc.dram_tensor("maskp", [128, 64], F32, kind="ExternalInput")
    cos_h = nc.dram_tensor("cosT", [64, 64], F32, kind="ExternalInput")
    sin_h = nc.dram_tensor("sinT", [64, 64], F32, kind="ExternalInput")
    onesb_h = nc.dram_tensor("onesb", [128, 128], BF16, kind="ExternalInput")
    bcb_h = nc.dram_tensor("bcb", [16, HL], F32, kind="ExternalInput")
    out_h = nc.dram_tensor("out", [T, C], F32, kind="ExternalOutput")
    xn_scr = nc.dram_tensor("xn_scr", [T, C], BF16, **dbg_kind)
    ag1_in = nc.dram_tensor("ag1_in", [SH_ROWS, C], BF16)
    w_scr = nc.dram_tensor("w_scr", [BLOB_ROWS, C], BF16)
    ag2_in = nc.dram_tensor("ag2_in", [WP_SH, C], BF16)
    wp_scr = nc.dram_tensor("wp_scr", [C, C], BF16)
    if debug:
        w_dbg = nc.dram_tensor("w_dbg", [BLOB_ROWS, C], BF16, kind="ExternalOutput")
        wp_dbg = nc.dram_tensor("wp_dbg", [C, C], BF16, kind="ExternalOutput")
        ktb_dbg = nc.dram_tensor("ktb_dbg", [128, HL], BF16, kind="ExternalOutput")
        atp_dbg = nc.dram_tensor("atp_dbg", [128, 8 * 128], BF16, kind="ExternalOutput")
        vtm_dbg = nc.dram_tensor("vtm_dbg", [64, C], BF16, kind="ExternalOutput")
        ss_dbg = nc.dram_tensor("ss_dbg", [128, 8 * 64], F32, kind="ExternalOutput")
        wb_dbg = nc.dram_tensor("wb_dbg", [128, 8 * 64], F32, kind="ExternalOutput")
        wtp_dbg = nc.dram_tensor("wtp_dbg", [64, 8 * 128], BF16, kind="ExternalOutput")
        exg_dbg = nc.dram_tensor("exg_dbg", [128, 8 * GS], BF16, kind="ExternalOutput")
        wtg_dbg = nc.dram_tensor("wtg_dbg", [128, CC * GS], BF16, kind="ExternalOutput")
        ht_dbg = nc.dram_tensor("ht_dbg", [128, KC * GS], BF16, kind="ExternalOutput")

    script = []

    def em(eng):
        def reg(fn):
            script.append((eng, fn))
            return fn
        return reg

    from contextlib import ExitStack
    _es = ExitStack()
    with _es:
        c_idf = _es.enter_context(nc.sbuf_tensor("sb_identf", [128, 128], F32))
        HT_OFF = 229344 - KC * GS * 2
        hT = nc.alloc_sbuf_tensor_at("hT", [128, KC, GS], BF16, offset=HT_OFF)
        x_t = nc.alloc_sbuf_tensor_at("x_t", [128, C], F32, offset=HT_OFF)
        k_tm = nc.alloc_sbuf_tensor_at("k_tm", [64, C], F32, offset=HT_OFF + 8192)
        c_mask = _es.enter_context(nc.sbuf_tensor("sb_maskp", [128, 64], F32))
        c_cos = _es.enter_context(nc.sbuf_tensor("sb_cosT", [64, 64], F32))
        c_sin = _es.enter_context(nc.sbuf_tensor("sb_sinT", [64, 64], F32))
        c_ones = _es.enter_context(nc.sbuf_tensor("sb_onesb", [128, 128], BF16))
        c_bcb = _es.enter_context(nc.sbuf_tensor("sb_bcb", [16, HL], F32))
        qTb = _es.enter_context(nc.sbuf_tensor("qTb", [128, HL], BF16))
        gi_in = _es.enter_context(nc.sbuf_tensor("gi_in", [128, 2, GFREE], I32))
        xn_t = _es.enter_context(nc.sbuf_tensor("xn_t", [128, 2, C], BF16))
        ln_s = _es.enter_context(nc.sbuf_tensor("ln_s", [128, 10], F32))
        wkb = _es.enter_context(nc.sbuf_tensor("wkb", [128, C], BF16))
        wgb = _es.enter_context(nc.sbuf_tensor("wgb", [128, CC, HL], BF16))
        krh = _es.enter_context(nc.sbuf_tensor("krh", [64, 128], F32))
        ktmp = _es.enter_context(nc.sbuf_tensor("ktmp", [64, 64], F32))
        v_tmb = _es.enter_context(nc.sbuf_tensor("v_tmb", [64, C], BF16))
        kTb = _es.enter_context(nc.sbuf_tensor("kTb", [128, HL], BF16))
        ss_f = _es.enter_context(nc.sbuf_tensor("ss_f", [128, 64], F32))
        nmax = _es.enter_context(nc.sbuf_tensor("nmax", [128, 1], F32))
        w_b = _es.enter_context(nc.sbuf_tensor("w_b", [128, 64], F32))
        rs = _es.enter_context(nc.sbuf_tensor("rs", [128, 1], F32))
        rr = _es.enter_context(nc.sbuf_tensor("rr", [128, 1], F32))
        wTp = _es.enter_context(nc.sbuf_tensor("wTp", [64, 128], BF16))
        attnP = _es.enter_context(nc.sbuf_tensor("attnP", [128, 8, 128], BF16))
        xnT = _es.enter_context(nc.sbuf_tensor("xnT", [128, CC, GS], BF16))
        expg = _es.enter_context(nc.sbuf_tensor("expg", [128, 8, GS], BF16))
        rsum = _es.enter_context(nc.sbuf_tensor("rsum", [16, GS], F32))
        wTg = _es.enter_context(nc.sbuf_tensor("wTg", [128, CC, GS], BF16))
        gbuf = _es.enter_context(nc.sbuf_tensor("gbuf", [128, 2, C], BF16))
        ubuf = _es.enter_context(nc.sbuf_tensor("ubuf", [128, 2, C], BF16))
        dbuf = _es.enter_context(nc.sbuf_tensor("dbuf", [128, 2, INTER], BF16))
        wpsl = _es.enter_context(nc.sbuf_tensor("wpsl", [128, 2, C], BF16))
        sgb = _es.enter_context(nc.sbuf_tensor("sgb", [128, 2, GS], BF16))
        oT_sb = _es.enter_context(nc.sbuf_tensor("oT_sb", [128, 2, GS], F32))
        xpc = _es.enter_context(nc.sbuf_tensor("xpc", [128, 2, 128], F32))
        res_sb = _es.enter_context(nc.sbuf_tensor("res_sb", [128, 2, 128], F32))
        epsb = _es.enter_context(nc.sbuf_tensor("epsb", [128, 1], F32))
        junkv = _es.enter_context(nc.sbuf_tensor("junkv", [128, 1], F32))
        jnk2 = _es.enter_context(nc.sbuf_tensor("jnk2", [128, 16], BF16))
        junkx = _es.enter_context(nc.sbuf_tensor("junkx", [128, 16], F32))
        PGA = _es.enter_context(nc.psum_tensor("PGA", [128, 512], F32))
        PGB = _es.enter_context(nc.psum_tensor("PGB", [128, 512], F32))
        PB0 = _es.enter_context(nc.psum_tensor("PB0", [128, 512], F32))
        PB1 = _es.enter_context(nc.psum_tensor("PB1", [128, 512], F32))
        PO0 = _es.enter_context(nc.psum_tensor("PO0", [128, 512], F32))
        PO1 = _es.enter_context(nc.psum_tensor("PO1", [128, 512], F32))
        PSX = _es.enter_context(nc.psum_tensor("PSX", [128, 512], F32))
        PSY = _es.enter_context(nc.psum_tensor("PSY", [128, 512], F32))
        h_eps = _es.enter_context(nc.semaphore("s_eps"))
        h_c = _es.enter_context(nc.semaphore("s_c"))
        h_gl = _es.enter_context(nc.semaphore("s_gl"))
        h_gv = _es.enter_context(nc.semaphore("s_gv"))
        h_gw = _es.enter_context(nc.semaphore("s_gw"))
        h_ag = _es.enter_context(nc.semaphore("s_ag"))
        h_xl = _es.enter_context(nc.semaphore("s_xl"))
        h_app = _es.enter_context(nc.semaphore("s_app"))
        h_xnw = _es.enter_context(nc.semaphore("s_xnw"))
        h_x64 = _es.enter_context(nc.semaphore("s_x64"))
        h_wk = _es.enter_context(nc.semaphore("s_wk"))
        h_kmm = _es.enter_context(nc.semaphore("s_kmm"))
        h_kev = _es.enter_context(nc.semaphore("s_kev"))
        h_rope = _es.enter_context(nc.semaphore("s_rope"))
        h_ktr = _es.enter_context(nc.semaphore("s_ktr"))
        h_ktev = _es.enter_context(nc.semaphore("s_ktev"))
        h_scr = _es.enter_context(nc.semaphore("s_scr"))
        h_ss = _es.enter_context(nc.semaphore("s_ss"))
        h_we = _es.enter_context(nc.semaphore("s_we"))
        h_rr = _es.enter_context(nc.semaphore("s_rr"))
        h_wn = _es.enter_context(nc.semaphore("s_wn"))
        h_wt = _es.enter_context(nc.semaphore("s_wt"))
        h_wtev = _es.enter_context(nc.semaphore("s_wtev"))
        h_am = _es.enter_context(nc.semaphore("s_am"))
        h_aev = _es.enter_context(nc.semaphore("s_aev"))
        h_xnt = _es.enter_context(nc.semaphore("s_xnt"))
        h_wgl = _es.enter_context(nc.semaphore("s_wgl"))
        h_gmm = _es.enter_context(nc.semaphore("s_gmm"))
        h_gex = _es.enter_context(nc.semaphore("s_gex"))
        h_smm = _es.enter_context(nc.semaphore("s_smm"))
        h_rcp = _es.enter_context(nc.semaphore("s_rcp"))
        h_bc = _es.enter_context(nc.semaphore("s_bc"))
        h_gn = _es.enter_context(nc.semaphore("s_gn"))
        h_wm = _es.enter_context(nc.semaphore("s_wm"))
        h_wev = _es.enter_context(nc.semaphore("s_wev"))
        h_fs = _es.enter_context(nc.semaphore("s_fs"))
        h_gu = _es.enter_context(nc.semaphore("s_gu"))
        h_si = _es.enter_context(nc.semaphore("s_si"))
        h_ht = _es.enter_context(nc.semaphore("s_ht"))
        h_ds = _es.enter_context(nc.semaphore("s_ds"))
        h_omm = _es.enter_context(nc.semaphore("s_omm"))
        h_oev = _es.enter_context(nc.semaphore("s_oev"))
        h_tr = _es.enter_context(nc.semaphore("s_tr"))
        h_xp = _es.enter_context(nc.semaphore("s_xp"))
        h_res = _es.enter_context(nc.semaphore("s_res"))
        h_out = _es.enter_context(nc.semaphore("s_out"))
        h_d1 = _es.enter_context(nc.semaphore("s_d1"))
        h_d2 = _es.enter_context(nc.semaphore("s_d2"))
        h_d3 = _es.enter_context(nc.semaphore("s_d3"))
        h_d4 = _es.enter_context(nc.semaphore("s_d4"))
        block = _es.enter_context(nc.Block())
        s_c = Sem(h_c); s_gl = Sem(h_gl); s_gv = Sem(h_gv); s_gw = Sem(h_gw)
        s_ag = Sem(h_ag); s_xl = Sem(h_xl); s_app = Sem(h_app)
        s_xnw = Sem(h_xnw); s_x64 = Sem(h_x64); s_wk = Sem(h_wk)
        s_kmm = Sem(h_kmm); s_kev = Sem(h_kev); s_rope = Sem(h_rope)
        s_ktr = Sem(h_ktr); s_ktev = Sem(h_ktev); s_scr = Sem(h_scr)
        s_ss = Sem(h_ss); s_we = Sem(h_we); s_rr = Sem(h_rr); s_wn = Sem(h_wn)
        s_wt = Sem(h_wt); s_wtev = Sem(h_wtev); s_am = Sem(h_am)
        s_aev = Sem(h_aev); s_xnt = Sem(h_xnt)
        s_wgl = Sem(h_wgl); s_gmm = Sem(h_gmm); s_gex = Sem(h_gex)
        s_smm = Sem(h_smm); s_rcp = Sem(h_rcp); s_bc = Sem(h_bc)
        s_gn = Sem(h_gn); s_wm = Sem(h_wm); s_wev = Sem(h_wev)
        s_fs = Sem(h_fs); s_gu = Sem(h_gu); s_si = Sem(h_si)
        s_ht = Sem(h_ht); s_ds = Sem(h_ds); s_omm = Sem(h_omm)
        s_oev = Sem(h_oev); s_tr = Sem(h_tr); s_xp = Sem(h_xp)
        s_res = Sem(h_res); s_out = Sem(h_out)

        banks = [PGA, PGB, PB0, PB1, PO0, PO1, PSX, PSY]

        gw = [dbuf[:, 0, 0:GFREE], dbuf[:, 0, GFREE:2 * GFREE]]
        gi_bf = dbuf[:, 1, 0:GFREE]
        g_ct = dbuf[:, 1, GFREE:2 * GFREE]
        wp_stage = [gbuf[:, 0, :], gbuf[:, 1, :]]
        xnT64 = [xnT[:, cc, 0:64] for cc in range(CC)]

        @em("v")
        def _(e):
            e.memset(epsb[:], EPS).then_inc(h_eps, 1)

        def idx_tile_ap(k):
            return bass.AP(idx_h, k * 128 * GFREE, [[GFREE, 128], [1, GFREE]])

        def ag1_tile_ap(k):
            return bass.AP(ag1_in, k * 128 * GFREE, [[GFREE, 128], [1, GFREE]])

        @em("gp")
        def _(e):
            e.dma_start(out=gi_in[:, 0, :], in_=idx_tile_ap(0)).then_inc(s_gl.h, 16)
            e.dma_start(out=gi_in[:, 1, :], in_=idx_tile_ap(1)).then_inc(s_gl.h, 16)
            e.dma_start(out=c_idf[:], in_=identf_h.ap()).then_inc(s_c.h, 16)
            e.dma_start(out=c_mask[:], in_=maskp_h.ap()).then_inc(s_c.h, 16)
            e.dma_start(out=c_cos[:], in_=cos_h.ap()).then_inc(s_c.h, 16)
            e.dma_start(out=c_sin[:], in_=sin_h.ap()).then_inc(s_c.h, 16)
            e.dma_start(out=c_ones[:], in_=onesb_h.ap()).then_inc(s_c.h, 16)
            e.dma_start(out=c_bcb[:], in_=bcb_h.ap()).then_inc(s_c.h, 16)
            e.dma_start(out=qTb[:], in_=qT_h.ap()).then_inc(s_c.h, 16)
            e.dma_start(out=wp_stage[0], in_=wpa_h.ap()[0:128, :]).then_inc(s_c.h, 16)
            e.dma_start(out=wp_stage[1], in_=wpa_h.ap()[128:256, :]).then_inc(s_c.h, 16)
        s_gl.bump(32)
        s_c.bump(9 * 16)
        CONSTS = s_c.n

        @em("gp")
        def _(e):
            for cc in range(CC):
                e.dma_start(out=wgb[:, cc, :],
                            in_=wg_h.ap()[cc * 128:(cc + 1) * 128, :]
                            ).then_inc(s_wgl.h, 16)
            e.dma_start(out=jnk2[:],
                        in_=wg_h.ap()[0:128, 0:16]).then_inc(s_wgl.h, 16)
        s_wgl.bump(17 * 16)

        for k in range(GTILES):
            @em("v")
            def _(e, k=k):
                e.wait_ge(s_gl.h, 16 * (k + 1))
                if k == 0:
                    e.wait_ge(s_c.h, CONSTS)
                if k >= 2:
                    e.wait_ge(s_gw.h, 16 * (k - 1))
                e.tensor_copy(out=gi_bf, in_=gi_in[:, k % 2, :])
                e.tensor_scalar(out=gw[k % 2], in0=gi_bf,
                                scalar1=0.0, scalar2=lut_vals[0],
                                op0=OP.is_equal, op1=OP.mult)
                last = None
                for v in range(1, 16):
                    e.tensor_scalar(out=g_ct, in0=gi_bf,
                                    scalar1=float(v), scalar2=lut_vals[v],
                                    op0=OP.is_equal, op1=OP.mult)
                    last = e.tensor_add(gw[k % 2], gw[k % 2], g_ct)
                last.then_inc(s_gv.h, 1)
            s_gv.bump(1)

            @em("gp")
            def _(e, k=k):
                e.wait_ge(s_gv.h, k + 2)
                e.dma_start(out=ag1_tile_ap(k), in_=gw[k % 2]).then_inc(s_gw.h, 16)
                if k + 2 < GTILES:
                    e.dma_start(out=gi_in[:, k % 2, :],
                                in_=idx_tile_ap(k + 2)).then_inc(s_gl.h, 16)
            s_gw.bump(16)
            if k + 2 < GTILES:
                s_gl.bump(16)

        @em("v")
        def _(e):
            e.memset(junkv[:], 0.0).then_inc(s_gv.h, 1)
        s_gv.bump(1)

        @em("gp")
        def _(e, want=s_c.n):
            e.wait_ge(s_c.h, want)
            e.dma_start(out=ag2_in.ap()[0:128, :], in_=wp_stage[0]).then_inc(s_gw.h, 16)
            e.dma_start(out=ag2_in.ap()[128:256, :], in_=wp_stage[1]).then_inc(s_gw.h, 16)
        s_gw.bump(32)

        AGW_WANT = s_gw.n

        ssq = ln_s[:, 0:1]; ssum = ln_s[:, 1:2]; mean = ln_s[:, 2:3]
        nm2 = ln_s[:, 3:4]; varep = ln_s[:, 4:5]; rstd = ln_s[:, 5:6]
        t9 = ln_s[:, 6:7]; nmr = ln_s[:, 7:8]; lnjk = ln_s[:, 8:9]

        def emit_ln_tile(tt):
            @em("s")
            def _(e, tt=tt):
                if tt == 0:
                    e.wait_ge(h_eps, 1)
                if tt >= 1:
                    e.wait_ge(s_app.h, tt)
                e.dma_start(out=x_t[:],
                            in_=x_h.ap()[tt * 128:(tt + 1) * 128, :]
                            ).then_inc(s_xl.h, 16)
                e.wait_ge(s_xl.h, 16 * (tt + 1))
                if tt >= 2:
                    e.wait_ge(s_xnw.h, 16 * (tt - 1))
                e.activation(out=xn_t[:, tt % 2, :], in_=x_t[:], func=AX.Square,
                             accum_out=ssq)
                e.activation(out=xn_t[:, tt % 2, :], in_=x_t[:], func=AX.Identity,
                             accum_out=ssum)
                e.mul(out=mean, in_=ssum, mul=1.0 / C)
                e.activation(out=nm2, in_=mean, func=AX.Square)
                e.activation(out=nm2, in_=nm2, func=AX.Identity,
                             scale=-1.0, bias=epsb[:])
                e.activation(out=lnjk, in_=ssq, func=AX.Identity)
                e.activation(out=varep, in_=ssq, func=AX.Identity,
                             scale=1.0 / C, bias=nm2)
                e.activation(out=varep, in_=varep, func=AX.Ln)
                e.activation(out=rstd, in_=varep, func=AX.Exp, scale=-0.5)
                e.activation(out=t9, in_=mean, func=AX.Identity, scale=rstd)
                e.mul(out=nmr, in_=t9, mul=-1.0)
                e.activation(out=lnjk, in_=ssq, func=AX.Identity)
                e.activation(out=xn_t[:, tt % 2, :], in_=x_t[:], func=AX.Identity,
                             scale=rstd, bias=nmr)
                e.activation(out=lnjk, in_=ssq,
                             func=AX.Identity).then_inc(s_app.h, 1)
            s_xl.bump(16)
            s_app.bump(1)

            @em("sy")
            def _(e, tt=tt):
                e.wait_ge(s_app.h, tt + 1)
                e.dma_start(out=xn_scr.ap()[tt * 128:(tt + 1) * 128, :],
                            in_=xn_t[:, tt % 2, :]).then_inc(s_xnw.h, 16)
            s_xnw.bump(16)

        emit_ln_tile(0)

        @em("sy")
        def _(e):
            e.wait_ge(s_xnw.h, 16)
            e.wait_ge(s_c.h, CONSTS)
            for cc in range(CC):
                e.dma_start(out=xnT64[cc],
                            in_=xn_scr.ap()[0:64, cc * 128:(cc + 1) * 128],
                            transpose=True).then_inc(s_x64.h, 16)
        s_x64.bump(16 * CC)

        for tt in range(1, NT):
            emit_ln_tile(tt)

        kvps_m = ([PGA[0:64, :], PGB[0:64, :], PB0[0:64, :], PB1[0:64, :]],
                  [PO0[0:64, :], PO1[0:64, :], PSX[0:64, :], PSY[0:64, :]])
        for m in range(2):
            for cc in range(CC):
                i = m * CC + cc

                @em("gp")
                def _(e, m=m, cc=cc, i=i):
                    if i >= 1:
                        e.wait_ge(s_kmm.h, i)
                    e.dma_start(out=wkb[:],
                                in_=(wk_h if m == 0 else wv_h).ap()[cc * 128:(cc + 1) * 128, :]
                                ).then_inc(s_wk.h, 16)
                s_wk.bump(16)

                @em("t")
                def _(e, m=m, cc=cc, i=i):
                    e.wait_ge(s_wk.h, 16 * (i + 1))
                    if m == 0 and cc == 0:
                        e.wait_ge(s_x64.h, 16 * CC)
                    mm = None
                    for n in range(4):
                        mm = e.matmul(kvps_m[m][n], xnT64[cc],
                                      wkb[:, 512 * n:512 * (n + 1)],
                                      start=(cc == 0), stop=(cc == CC - 1))
                    mm.then_inc(s_kmm.h, 1)
                s_kmm.bump(1)

            @em("v")
            def _(e, m=m):
                e.wait_ge(s_kmm.h, 16 * (m + 1))
                dst = k_tm if m == 0 else v_tmb
                mm = None
                for n in range(4):
                    mm = e.tensor_copy(out=dst[:, 512 * n:512 * (n + 1)],
                                       in_=kvps_m[m][n])
                mm.then_inc(s_kev.h, 1)
            s_kev.bump(1)

        @em("gp")
        def _(e):
            e.wait_ge(s_gw.h, AGW_WANT)
            e.wait_ge(s_aev.h, 8)
            e.collective_compute(
                "AllGather", OP.bypass, replica_groups=[list(range(8))],
                ins=[ag1_in.ap().opt()], outs=[w_scr.ap().opt()],
            ).then_inc(s_ag.h, 1)
            e.collective_compute(
                "AllGather", OP.bypass, replica_groups=[list(range(8))],
                ins=[ag2_in.ap().opt()], outs=[wp_scr.ap().opt()],
            ).then_inc(s_ag.h, 1)
        s_ag.bump(2)

        for h in range(H):
            @em("v")
            def _(e, h=h):
                if h == 0:
                    e.wait_ge(s_kev.h, 1)
                    e.wait_ge(s_c.h, CONSTS)
                else:
                    e.wait_ge(s_ktr.h, h)
                base = h * 128
                x0 = bass.AP(k_tm, base + 0, [[C, 64], [2, 64]])
                x1 = bass.AP(k_tm, base + 1, [[C, 64], [2, 64]])
                r0 = bass.AP(krh, 0, [[128, 64], [2, 64]])
                r1 = bass.AP(krh, 1, [[128, 64], [2, 64]])
                tmp = ktmp[:, :]
                e.tensor_mul(tmp, x0, c_sin[:, :])
                e.tensor_mul(r1, x1, c_cos[:, :])
                e.tensor_tensor(out=r1, in0=r1, in1=tmp, op=OP.add)
                e.tensor_mul(tmp, x1, c_sin[:, :])
                e.tensor_mul(r0, x0, c_cos[:, :])
                e.tensor_tensor(out=r0, in0=r0, in1=tmp,
                                op=OP.subtract).then_inc(s_rope.h, 1)
            s_rope.bump(1)

            @em("t")
            def _(e, h=h):
                e.wait_ge(s_rope.h, h + 1)
                if h == 0:
                    e.wait_ge(s_kev.h, 2)
                if h >= 2:
                    e.wait_ge(s_ktev.h, h - 1)
                pt = (PSX, PSY)[h % 2]
                e.transpose(pt[:, 0:64], krh[:, :],
                            c_idf[0:64, 0:64]).then_inc(s_ktr.h, 1)
            s_ktr.bump(1)

            @em("v")
            def _(e, h=h):
                e.wait_ge(s_ktr.h, h + 1)
                pt = (PSX, PSY)[h % 2]
                e.tensor_copy(out=kTb[:, h * 64:(h + 1) * 64],
                              in_=pt[:, 0:64]).then_inc(s_ktev.h, 1)
            s_ktev.bump(1)

        for q in range(8):
            @em("t")
            def _(e, q=q):
                if q == 0:
                    e.wait_ge(s_ktev.h, H)
                else:
                    e.wait_ge(s_ss.h, q)
                e.matmul(PGA[0:64, 0:64], qTb[:, (2 * q) * 64:(2 * q + 1) * 64],
                         kTb[:, (2 * q) * 64:(2 * q + 1) * 64],
                         start=True, stop=True)
                e.matmul(PGA[64:128, 0:64], qTb[:, (2 * q + 1) * 64:(2 * q + 2) * 64],
                         kTb[:, (2 * q + 1) * 64:(2 * q + 2) * 64],
                         start=True, stop=True,
                         tile_position=(0, 64)).then_inc(s_scr.h, 1)
            s_scr.bump(1)

            @em("v")
            def _(e, q=q):
                e.wait_ge(s_scr.h, q + 1)
                if debug and q >= 1:
                    e.wait_ge(h_d1, 16 * q)
                e.scalar_tensor_tensor(out=ss_f[:], in0=PGA[:, 0:64], scalar=SCALE,
                                       in1=c_mask[:], op0=OP.mult, op1=OP.add)
                e.reduce_max(nmax[:], ss_f[:], axis=AXL.X, negate=True)
                e.memset(junkv[:], 0.0)
                e.tensor_scalar_add(out=ss_f[:], in0=ss_f[:],
                                    scalar1=nmax[:]).then_inc(s_ss.h, 1)
            s_ss.bump(1)

            if debug:
                @em("sy")
                def _(e, q=q):
                    e.wait_ge(s_ss.h, q + 1)
                    e.dma_start(out=ss_dbg.ap()[:, q * 64:(q + 1) * 64],
                                in_=ss_f[:]).then_inc(h_d1, 16)

            @em("s")
            def _(e, q=q):
                e.wait_ge(s_ss.h, q + 1)
                if debug and q >= 1:
                    e.wait_ge(h_d2, 16 * q)
                e.activation(out=w_b[:], in_=ss_f[:], func=AX.Exp,
                             accum_out=rs[:]).then_inc(s_we.h, 1)
            s_we.bump(1)

            @em("v")
            def _(e, q=q):
                e.wait_ge(s_we.h, q + 1)
                e.reciprocal(rr[:], rs[:])
                e.memset(junkv[:], 0.0)
                e.tensor_scalar_mul(out=w_b[:], in0=w_b[:],
                                    scalar1=rr[:]).then_inc(s_wn.h, 1)
            s_wn.bump(1)

            if debug:
                @em("sy")
                def _(e, q=q):
                    e.wait_ge(s_wn.h, q + 1)
                    e.dma_start(out=wb_dbg.ap()[:, q * 64:(q + 1) * 64],
                                in_=w_b[:]).then_inc(h_d2, 16)

            @em("t")
            def _(e, q=q):
                e.wait_ge(s_wn.h, q + 1)
                if q > 0:
                    e.wait_ge(s_wtev.h, q)
                e.transpose(PGB[0:64, 0:128], w_b[:],
                            c_idf[:, :]).then_inc(s_wt.h, 1)
            s_wt.bump(1)

            @em("v")
            def _(e, q=q):
                e.wait_ge(s_wt.h, q + 1)
                if debug and q >= 1:
                    e.wait_ge(h_d3, 16 * q)
                e.tensor_copy(out=wTp[:], in_=PGB[0:64, 0:128]).then_inc(s_wtev.h, 1)
            s_wtev.bump(1)

            if debug:
                @em("sy")
                def _(e, q=q):
                    e.wait_ge(s_wtev.h, q + 1)
                    e.dma_start(out=wtp_dbg.ap()[:, q * 128:(q + 1) * 128],
                                in_=wTp[:]).then_inc(h_d3, 16)

            @em("t")
            def _(e, q=q):
                e.wait_ge(s_wtev.h, q + 1)
                if q > 0:
                    e.wait_ge(s_aev.h, q)
                e.matmul(PB0[0:64, 0:128], wTp[:, 0:64],
                         v_tmb[:, (2 * q) * 128:(2 * q + 1) * 128],
                         start=True, stop=True)
                e.matmul(PB0[64:128, 0:128], wTp[:, 64:128],
                         v_tmb[:, (2 * q + 1) * 128:(2 * q + 2) * 128],
                         start=True, stop=True,
                         tile_position=(0, 64)).then_inc(s_am.h, 1)
            s_am.bump(1)

            @em("v")
            def _(e, q=q):
                e.wait_ge(s_am.h, q + 1)
                e.tensor_copy(out=attnP[:, q, :],
                              in_=PB0[:, 0:128]).then_inc(s_aev.h, 1)
            s_aev.bump(1)

        piece = 0
        for G in range(NG):
            @em("sy")
            def _(e, G=G):
                e.wait_ge(s_xnw.h, 16 * 4 * (G + 1))
                if G == 0:
                    e.wait_ge(s_kmm.h, 32)
                else:
                    e.wait_ge(s_gu.h, KC * G)
                for cc in range(CC):
                    e.dma_start(out=xnT[:, cc, :],
                                in_=xn_scr.ap()[G * GS:(G + 1) * GS,
                                                cc * 128:(cc + 1) * 128],
                                transpose=True).then_inc(s_xnt.h, 16)
            s_xnt.bump(16 * CC)

            for cc in range(CC):
                @em("t")
                def _(e, G=G, cc=cc):
                    if G == 0 and cc == 0:
                        e.wait_ge(s_wgl.h, 16 * (CC + 1))
                    if cc == 0:
                        e.wait_ge(s_xnt.h, 16 * CC * (G + 1))
                        if G == 0:
                            e.wait_ge(s_aev.h, 8)
                        else:
                            e.wait_ge(s_oev.h, CT * G)
                            e.wait_ge(s_res.h, 64 * G)
                            e.wait_ge(s_si.h, KC * G)
                            e.wait_ge(s_ht.h, KC * G)
                            e.wait_ge(s_gn.h, 8 * G)
                            e.wait_ge(s_wev.h, H * G)
                            e.wait_ge(s_rcp.h, G)
                    mm = None
                    for gt in range(8):
                        mm = e.matmul(banks[gt][:, 0:GS],
                                      wgb[:, cc, gt * 128:(gt + 1) * 128],
                                      xnT[:, cc, :],
                                      start=(cc == 0), stop=(cc == CC - 1))
                    mm.then_inc(s_gmm.h, 1)
                s_gmm.bump(1)

            for gt in range(8):
                @em("s")
                def _(e, G=G, gt=gt):
                    e.wait_ge(s_gmm.h, 16 * (G + 1))
                    if debug and G == 1 and gt == 0:
                        e.wait_ge(h_d4, 16)
                    e.activation(out=expg[:, gt, :], in_=banks[gt][:, 0:GS],
                                 func=AX.Exp, scale=SCALE).then_inc(s_gex.h, 1)
                s_gex.bump(1)

            @em("t")
            def _(e, G=G):
                e.wait_ge(s_gex.h, 8 * (G + 1))
                mm = None
                for q in range(8):
                    mm = e.matmul(PSX[0:16, 0:GS], c_ones[:, q * 16:(q + 1) * 16],
                                  expg[:, q, :], start=(q == 0), stop=(q == 7))
                mm.then_inc(s_smm.h, 1)
            s_smm.bump(1)

            @em("v")
            def _(e, G=G):
                e.wait_ge(s_smm.h, G + 1)
                e.reciprocal(rsum[:], PSX[0:16, 0:GS]).then_inc(s_rcp.h, 1)
            s_rcp.bump(1)

            for q in range(8):
                pb = (PB0, PB1)[q % 2]

                @em("t")
                def _(e, G=G, q=q, pb=pb):
                    if q == 0:
                        e.wait_ge(s_rcp.h, G + 1)
                    if 8 * G + q >= 2:
                        e.wait_ge(s_gn.h, 8 * G + q - 1)
                    e.matmul(pb[:, :], c_bcb[:, q * 128:(q + 1) * 128],
                             rsum[:, :], start=True, stop=True).then_inc(s_bc.h, 1)
                s_bc.bump(1)

                @em("v")
                def _(e, G=G, q=q, pb=pb):
                    e.wait_ge(s_bc.h, 8 * G + q + 1)
                    e.tensor_tensor(out=expg[:, q, :], in0=expg[:, q, :],
                                    in1=pb[:, 0:GS], op=OP.mult).then_inc(s_gn.h, 1)
                s_gn.bump(1)

            if debug and G == 0:
                @em("sy")
                def _(e):
                    e.wait_ge(s_gn.h, 8)
                    e.dma_start(out=exg_dbg.ap(),
                                in_=expg[:, :, :]).then_inc(h_d4, 16)

            for h in range(H):
                q = h // 2
                sl = slice(0, 64) if h % 2 == 0 else slice(64, 128)
                po = (PB0, PB1)[h % 2]

                @em("t")
                def _(e, G=G, h=h, q=q, sl=sl, po=po):
                    e.wait_ge(s_gn.h, 8 * (G + 1))
                    if H * G + h >= 2:
                        e.wait_ge(s_wev.h, H * G + h - 1)
                    e.matmul(po[:, 0:GS], attnP[sl, q, :], expg[sl, q, :],
                             start=True, stop=True).then_inc(s_wm.h, 1)
                s_wm.bump(1)

                @em("v")
                def _(e, G=G, h=h, po=po):
                    e.wait_ge(s_wm.h, H * G + h + 1)
                    if debug and G == 1 and h == 0:
                        e.wait_ge(h_d4, 32)
                    e.tensor_copy(out=wTg[:, h, :],
                                  in_=po[:, 0:GS]).then_inc(s_wev.h, 1)
                s_wev.bump(1)

            if debug and G == 0:
                @em("sy")
                def _(e):
                    e.wait_ge(s_wev.h, H)
                    e.dma_start(out=wtg_dbg.ap(),
                                in_=wTg[:, :, :]).then_inc(h_d4, 16)

            for it in range(KC):
                idx = KC * G + it

                @em("sy")
                def _(e, G=G, it=it, idx=idx):
                    if G == 0 and it == 0:
                        e.wait_ge(s_ag.h, 2)
                    if idx >= 2:
                        e.wait_ge(s_gu.h, idx - 1)
                    e.dma_start(out=gbuf[:, idx % 2, :],
                                in_=w_scr.ap()[it * 128:(it + 1) * 128, :]
                                ).then_inc(s_fs.h, 16)
                    e.dma_start(out=ubuf[:, idx % 2, :],
                                in_=w_scr.ap()[5504 + it * 128:5504 + (it + 1) * 128, :]
                                ).then_inc(s_fs.h, 16)
                s_fs.bump(32)

                gpair = (PGA, PGB) if idx % 2 == 0 else (PO0, PO1)

                @em("t")
                def _(e, G=G, it=it, idx=idx, gpair=gpair):
                    e.wait_ge(s_fs.h, 32 * (idx + 1))
                    if it == 0:
                        e.wait_ge(s_gex.h, 8 * (G + 1))
                    if G == 0 and it == 1:
                        e.wait_ge(s_aev.h, 8)
                    if idx >= 2:
                        e.wait_ge(s_si.h, idx - 1)
                        e.wait_ge(s_ht.h, idx - 1)
                    mm = None
                    for cc in range(CC):
                        mm = e.matmul(gpair[0][:, :],
                                      gbuf[:, idx % 2, cc * 128:(cc + 1) * 128],
                                      xnT[:, cc, :],
                                      start=(cc == 0), stop=(cc == CC - 1))
                    for cc in range(CC):
                        mm = e.matmul(gpair[1][:, :],
                                      ubuf[:, idx % 2, cc * 128:(cc + 1) * 128],
                                      xnT[:, cc, :],
                                      start=(cc == 0), stop=(cc == CC - 1))
                    mm.then_inc(s_gu.h, 1)
                s_gu.bump(1)

                @em("s")
                def _(e, idx=idx, gpair=gpair):
                    e.wait_ge(s_gu.h, idx + 1)
                    if idx >= 2:
                        e.wait_ge(s_ht.h, idx - 1)
                    e.activation(out=sgb[:, idx % 2, :], in_=gpair[0][:, :],
                                 func=AX.Silu).then_inc(s_si.h, 1)
                s_si.bump(1)

                @em("v")
                def _(e, it=it, idx=idx, gpair=gpair):
                    e.wait_ge(s_si.h, idx + 1)
                    if idx == 0:
                        e.wait_ge(s_app.h, NT)
                    if debug and idx == KC:
                        e.wait_ge(h_d4, 48)
                    e.tensor_tensor(out=hT[:, it, :], in0=sgb[:, idx % 2, :],
                                    in1=gpair[1][:, :], op=OP.mult).then_inc(s_ht.h, 1)
                s_ht.bump(1)

            if debug and G == 0:
                @em("sy")
                def _(e):
                    e.wait_ge(s_ht.h, KC)
                    e.dma_start(out=ht_dbg.ap(),
                                in_=hT[:, :, :]).then_inc(h_d4, 16)

            for ct in range(CT):
                idx = CT * G + ct
                po = (PO0, PO1)[ct % 2]

                @em("sy")
                def _(e, G=G, ct=ct, idx=idx):
                    if idx >= 2:
                        e.wait_ge(s_omm.h, idx - 1)
                    e.dma_start(out=dbuf[:, idx % 2, :],
                                in_=bass.AP(w_scr, 11008 * 2048 + ct * 128 * INTER,
                                            [[INTER, 128], [1, INTER]])
                                ).then_inc(s_ds.h, 16)
                    e.dma_start(out=wpsl[:, idx % 2, :],
                                in_=wp_scr.ap()[ct * 128:(ct + 1) * 128, :]
                                ).then_inc(s_ds.h, 16)
                s_ds.bump(32)

                @em("t")
                def _(e, G=G, ct=ct, idx=idx, po=po):
                    e.wait_ge(s_ds.h, 32 * (idx + 1))
                    if ct == 0:
                        e.wait_ge(s_ht.h, KC * (G + 1))
                        e.wait_ge(s_wev.h, H * (G + 1))
                    if idx >= 2:
                        e.wait_ge(s_oev.h, idx - 1)
                    mm = None
                    for kc in range(KC):
                        mm = e.matmul(po[:, :],
                                      dbuf[:, idx % 2, kc * 128:(kc + 1) * 128],
                                      hT[:, kc, :], start=(kc == 0), stop=False)
                    for cc in range(CC):
                        mm = e.matmul(po[:, :],
                                      wpsl[:, idx % 2, cc * 128:(cc + 1) * 128],
                                      wTg[:, cc, :], start=False, stop=(cc == CC - 1))
                    mm.then_inc(s_omm.h, 1)
                s_omm.bump(1)

                @em("v")
                def _(e, idx=idx, po=po):
                    e.wait_ge(s_omm.h, idx + 1)
                    if idx >= 2:
                        e.wait_ge(s_tr.h, 4 * (idx - 1))
                    for st in (3, 2, 1, 0):
                        e.tensor_copy(out=oT_sb[:, idx % 2, st * 128:(st + 1) * 128],
                                      in_=po[:, st * 128:(st + 1) * 128])
                    e.memset(junkv[:], 0.0).then_inc(s_oev.h, 1)
                s_oev.bump(1)

                for tt in range(4):
                    p = piece
                    piece += 1

                    if p == 0:
                        @em("sy")
                        def _(e):
                            e.dma_start(out=xpc[:, 0, :],
                                        in_=x_h.ap()[0:128, 0:128]).then_inc(s_xp.h, 16)
                            e.dma_start(out=xpc[:, 1, :],
                                        in_=x_h.ap()[128:256, 0:128]).then_inc(s_xp.h, 16)
                        s_xp.bump(32)

                    @em("t")
                    def _(e, idx=idx, tt=tt, p=p):
                        e.wait_ge(s_oev.h, idx + 1)
                        if p >= 2:
                            e.wait_ge(s_res.h, p - 1)
                        pt = (PSX, PSY)[p % 2]
                        e.transpose(pt[:, 0:128],
                                    oT_sb[:, idx % 2, tt * 128:(tt + 1) * 128],
                                    c_idf[:, :]).then_inc(s_tr.h, 1)
                    s_tr.bump(1)

                    @em("v")
                    def _(e, p=p):
                        e.wait_ge(s_tr.h, p + 1)
                        e.wait_ge(s_xp.h, 16 * (p + 2))
                        if p >= 2:
                            e.wait_ge(s_out.h, 16 * (p - 1))
                        pt = (PSX, PSY)[p % 2]
                        e.tensor_tensor(out=res_sb[:, p % 2, :],
                                        in0=pt[:, 0:128], in1=xpc[:, p % 2, :],
                                        op=OP.add)
                        e.memset(junkv[:], 0.0).then_inc(s_res.h, 1)
                    s_res.bump(1)

                    @em("gp")
                    def _(e, G=G, ct=ct, tt=tt, p=p):
                        e.wait_ge(s_res.h, p + 1)
                        r0 = G * GS + tt * 128
                        e.dma_start(out=out_h.ap()[r0:r0 + 128, ct * 128:(ct + 1) * 128],
                                    in_=res_sb[:, p % 2, :]).then_inc(s_out.h, 16)
                    s_out.bump(16)

                    p2 = p + 2
                    @em("sy")
                    def _(e, p=p, p2=p2):
                        e.wait_ge(s_res.h, p + 1)
                        if p2 < NG * CT * 4:
                            G2 = p2 // 64
                            ct2 = (p2 % 64) // 4
                            tt2 = p2 % 4
                            r2 = G2 * GS + tt2 * 128
                            e.dma_start(out=xpc[:, p2 % 2, :],
                                        in_=x_h.ap()[r2:r2 + 128,
                                                     ct2 * 128:(ct2 + 1) * 128]
                                        ).then_inc(s_xp.h, 16)
                        else:
                            e.dma_start(out=junkx[:],
                                        in_=x_h.ap()[0:128, 0:16]
                                        ).then_inc(s_xp.h, 16)
                    s_xp.bump(16)

        @em("gp")
        def _(e, want=s_out.n):
            e.wait_ge(s_out.h, want)
            if debug:
                e.wait_ge(h_d1, 128)
                e.wait_ge(h_d2, 128)
                e.wait_ge(h_d3, 128)
                e.wait_ge(h_d4, 48)
                e.dma_start(out=w_dbg.ap(), in_=w_scr.ap()).then_inc(s_out.h, 16)
                e.dma_start(out=wp_dbg.ap(), in_=wp_scr.ap()).then_inc(s_out.h, 16)
                e.dma_start(out=ktb_dbg.ap(), in_=kTb[:]).then_inc(s_out.h, 16)
                e.dma_start(out=atp_dbg.ap(), in_=attnP[:, :, :]).then_inc(s_out.h, 16)
                e.dma_start(out=vtm_dbg.ap(), in_=v_tmb[:]).then_inc(s_out.h, 16)
                e.wait_ge(s_out.h, want + 5 * 16)

        @block.gpsimd
        def _(eng):
            for name, fn in script:
                if name == "gp":
                    fn(eng)

        @block.sync
        def _(eng):
            for name, fn in script:
                if name == "sy":
                    fn(eng)

        @block.vector
        def _(eng):
            for name, fn in script:
                if name == "v":
                    fn(eng)

        @block.scalar
        def _(eng):
            for name, fn in script:
                if name == "s":
                    fn(eng)

        @block.tensor
        def _(eng):
            for name, fn in script:
                if name == "t":
                    fn(eng)

    return nc


def kernel(L0_attn_x, L0_attn_latent_queries, L0_attn_Wk, L0_attn_Wv,
           L0_attn_Wg, L0_attn_Wp, L0_ffn_lut, L0_ffn_gate_idx,
           L0_ffn_up_idx, L0_ffn_down_idx):
    x = np.ascontiguousarray(np.asarray(L0_attn_x, dtype=np.float32))
    lut_vals = [float(v) for v in np.asarray(L0_ffn_lut, dtype=np.float32)]
    xs, idx_shards, wp_shards, qT = _host_shards(
        x, np.asarray(L0_attn_latent_queries, dtype=np.float32),
        np.asarray(L0_attn_Wp, dtype=np.float32),
        np.asarray(L0_ffn_gate_idx), np.asarray(L0_ffn_up_idx),
        np.asarray(L0_ffn_down_idx))
    identf, identb, maskp, cosT, sinT, onesb, bcb = _consts()

    wk = np.ascontiguousarray(np.asarray(L0_attn_Wk, dtype=np.float32))
    wv = np.ascontiguousarray(np.asarray(L0_attn_Wv, dtype=np.float32))
    wg = np.ascontiguousarray(np.asarray(L0_attn_Wg, dtype=np.float32))

    nc = build(lut_vals)
    in_maps = []
    for j in range(8):
        in_maps.append({
            "x": xs[j], "wk": wk, "wv": wv, "wg": wg, "qT": qT,
            "idxs": idx_shards[j], "wpa": wp_shards[j],
            "identf": identf, "identb": identb, "maskp": maskp,
            "cosT": cosT, "sinT": sinT, "onesb": onesb, "bcb": bcb,
        })
    res = run_bass_kernel_spmd(nc, in_maps, core_ids=list(range(8)))
    out = np.stack([np.asarray(r["out"], dtype=np.float32) for r in res.results],
                   axis=0)
    return out


# version 10
# speedup vs baseline: 1.1503x; 1.0235x over previous
import math

import numpy as np
import ml_dtypes

import concourse.bass as bass
import concourse.mybir as mybir
from concourse.bass_utils import run_bass_kernel_spmd

F32 = mybir.dt.float32
BF16 = mybir.dt.bfloat16
I32 = mybir.dt.int32
AX = mybir.ActivationFunctionType
OP = mybir.AluOpType
AXL = mybir.AxisListType

T = 2048
C = 2048
H = 16
L = 64
HL = H * L
INTER = 5504
KC = INTER // 128
CC = C // 128
CT = C // 128
NT = T // 128
NG = 4
GS = T // NG
SCALE = 1.0 / math.sqrt(128.0)
NEG = -1.0e9
EPS = 1e-5

BLOB_ROWS = 3 * 5504
SH_ROWS = BLOB_ROWS // 8
GTILES = 32
GFREE = SH_ROWS * C // (128 * GTILES)
WP_SH = C // 8


def _consts():
    identf = np.eye(128, dtype=np.float32)
    identb = np.eye(128, dtype=ml_dtypes.bfloat16)
    lidx = np.arange(128) % 64
    tidx = np.arange(64)
    maskp = np.where(tidx[None, :] <= lidx[:, None], 0.0, NEG).astype(np.float32)
    inv = (10000.0 ** (-(np.arange(64, dtype=np.float32)) / 64.0)).astype(np.float32)
    ang = np.arange(64, dtype=np.float32)[:, None] * inv[None, :]
    cosT = np.cos(ang).astype(np.float32)
    sinT = np.sin(ang).astype(np.float32)
    onesb = np.zeros((128, 128), dtype=ml_dtypes.bfloat16)
    for q in range(8):
        onesb[0:64, q * 16 + 2 * q] = 1
        onesb[64:128, q * 16 + 2 * q + 1] = 1
    bcb = np.zeros((16, 1024), dtype=np.float32)
    for q in range(8):
        bcb[2 * q, q * 128: q * 128 + 64] = 1.0
        bcb[2 * q + 1, q * 128 + 64: q * 128 + 128] = 1.0
    return identf, identb, maskp, cosT, sinT, onesb, bcb


def _host_shards(x, lq, wp, gidx, uidx, didx):
    def scr_gu(idx):
        a = np.ascontiguousarray(idx).T
        a = a.reshape(16, 128, 43, 128).transpose(2, 1, 0, 3)
        return np.ascontiguousarray(a.reshape(5504, 2048))

    def scr_dn(idx):
        a = np.ascontiguousarray(idx).T
        a = a.reshape(43, 128, 16, 128).transpose(2, 1, 0, 3)
        return np.ascontiguousarray(a.reshape(5504, 2048))

    blob = np.concatenate([scr_gu(gidx), scr_gu(uidx), scr_dn(didx)], axis=0)
    idx_shards = [np.ascontiguousarray(s) for s in np.split(blob, 8, axis=0)]
    wpa = (np.ascontiguousarray(wp).reshape(16, 128, 16, 128)
           .transpose(2, 1, 0, 3).reshape(2048, 2048))
    wp_shards = [np.ascontiguousarray(s) for s in np.split(wpa, 8, axis=0)]
    qT = np.ascontiguousarray(
        np.transpose(lq[0], (2, 1, 0)).reshape(128, HL)).astype(np.float32, copy=False)
    xs = [np.ascontiguousarray(x[j]) for j in range(8)]
    return xs, idx_shards, wp_shards, qT


class Sem:
    __slots__ = ("h", "n")

    def __init__(self, h):
        self.h = h
        self.n = 0

    def bump(self, k=1):
        self.n += k
        return self.n


def build(lut_vals, debug=False):
    nc = bass.Bass("TRN2", target_bir_lowering=False)
    dbg_kind = {"kind": "ExternalOutput"} if debug else {}

    x_h = nc.dram_tensor("x", [T, C], F32, kind="ExternalInput")
    wk_h = nc.dram_tensor("wk", [C, C], F32, kind="ExternalInput")
    wv_h = nc.dram_tensor("wv", [C, C], F32, kind="ExternalInput")
    wg_h = nc.dram_tensor("wg", [C, HL], F32, kind="ExternalInput")
    qT_h = nc.dram_tensor("qT", [128, HL], F32, kind="ExternalInput")
    idx_h = nc.dram_tensor("idxs", [SH_ROWS, C], I32, kind="ExternalInput")
    wpa_h = nc.dram_tensor("wpa", [WP_SH, C], F32, kind="ExternalInput")
    identf_h = nc.dram_tensor("identf", [128, 128], F32, kind="ExternalInput")
    identb_h = nc.dram_tensor("identb", [128, 128], BF16, kind="ExternalInput")
    maskp_h = nc.dram_tensor("maskp", [128, 64], F32, kind="ExternalInput")
    cos_h = nc.dram_tensor("cosT", [64, 64], F32, kind="ExternalInput")
    sin_h = nc.dram_tensor("sinT", [64, 64], F32, kind="ExternalInput")
    onesb_h = nc.dram_tensor("onesb", [128, 128], BF16, kind="ExternalInput")
    bcb_h = nc.dram_tensor("bcb", [16, HL], F32, kind="ExternalInput")
    out_h = nc.dram_tensor("out", [T, C], F32, kind="ExternalOutput")
    xn_scr = nc.dram_tensor("xn_scr", [T, C], BF16, **dbg_kind)
    ag1_in = nc.dram_tensor("ag1_in", [SH_ROWS, C], BF16)
    w_scr = nc.dram_tensor("w_scr", [BLOB_ROWS, C], BF16)
    ag2_in = nc.dram_tensor("ag2_in", [WP_SH, C], BF16)
    wp_scr = nc.dram_tensor("wp_scr", [C, C], BF16)
    if debug:
        w_dbg = nc.dram_tensor("w_dbg", [BLOB_ROWS, C], BF16, kind="ExternalOutput")
        wp_dbg = nc.dram_tensor("wp_dbg", [C, C], BF16, kind="ExternalOutput")
        ktb_dbg = nc.dram_tensor("ktb_dbg", [128, HL], BF16, kind="ExternalOutput")
        atp_dbg = nc.dram_tensor("atp_dbg", [128, 8 * 128], BF16, kind="ExternalOutput")
        vtm_dbg = nc.dram_tensor("vtm_dbg", [64, C], BF16, kind="ExternalOutput")
        ss_dbg = nc.dram_tensor("ss_dbg", [128, 8 * 64], F32, kind="ExternalOutput")
        wb_dbg = nc.dram_tensor("wb_dbg", [128, 8 * 64], F32, kind="ExternalOutput")
        wtp_dbg = nc.dram_tensor("wtp_dbg", [64, 8 * 128], BF16, kind="ExternalOutput")
        exg_dbg = nc.dram_tensor("exg_dbg", [128, 8 * GS], BF16, kind="ExternalOutput")
        wtg_dbg = nc.dram_tensor("wtg_dbg", [128, CC * GS], BF16, kind="ExternalOutput")
        ht_dbg = nc.dram_tensor("ht_dbg", [128, KC * GS], BF16, kind="ExternalOutput")

    script = []

    def em(eng):
        def reg(fn):
            script.append((eng, fn))
            return fn
        return reg

    from contextlib import ExitStack
    _es = ExitStack()
    with _es:
        c_idf = _es.enter_context(nc.sbuf_tensor("sb_identf", [128, 128], F32))
        HT_OFF = 229344 - KC * GS * 2
        hT = nc.alloc_sbuf_tensor_at("hT", [128, KC, GS], BF16, offset=HT_OFF)
        x_t = nc.alloc_sbuf_tensor_at("x_t", [128, C], F32, offset=HT_OFF)
        k_tm = nc.alloc_sbuf_tensor_at("k_tm", [64, C], F32, offset=HT_OFF + 8192)
        c_mask = _es.enter_context(nc.sbuf_tensor("sb_maskp", [128, 64], F32))
        c_cos = _es.enter_context(nc.sbuf_tensor("sb_cosT", [64, 64], F32))
        c_sin = _es.enter_context(nc.sbuf_tensor("sb_sinT", [64, 64], F32))
        c_ones = _es.enter_context(nc.sbuf_tensor("sb_onesb", [128, 128], BF16))
        c_bcb = _es.enter_context(nc.sbuf_tensor("sb_bcb", [16, HL], F32))
        qTb = _es.enter_context(nc.sbuf_tensor("qTb", [128, HL], BF16))
        gi_in = _es.enter_context(nc.sbuf_tensor("gi_in", [128, 2, GFREE], I32))
        xn_t = _es.enter_context(nc.sbuf_tensor("xn_t", [128, 2, C], BF16))
        ln_s = _es.enter_context(nc.sbuf_tensor("ln_s", [128, 10], F32))
        wkb = _es.enter_context(nc.sbuf_tensor("wkb", [128, C], BF16))
        wgb = _es.enter_context(nc.sbuf_tensor("wgb", [128, CC, HL], BF16))
        krh = _es.enter_context(nc.sbuf_tensor("krh", [64, 128], F32))
        ktmp = _es.enter_context(nc.sbuf_tensor("ktmp", [64, 64], F32))
        v_tmb = _es.enter_context(nc.sbuf_tensor("v_tmb", [64, C], BF16))
        kTb = _es.enter_context(nc.sbuf_tensor("kTb", [128, HL], BF16))
        ss_f = _es.enter_context(nc.sbuf_tensor("ss_f", [128, 64], F32))
        nmax = _es.enter_context(nc.sbuf_tensor("nmax", [128, 1], F32))
        w_b = _es.enter_context(nc.sbuf_tensor("w_b", [128, 64], F32))
        rs = _es.enter_context(nc.sbuf_tensor("rs", [128, 1], F32))
        rr = _es.enter_context(nc.sbuf_tensor("rr", [128, 1], F32))
        wTp = _es.enter_context(nc.sbuf_tensor("wTp", [64, 128], BF16))
        attnP = _es.enter_context(nc.sbuf_tensor("attnP", [128, 8, 128], BF16))
        xnT = _es.enter_context(nc.sbuf_tensor("xnT", [128, CC, GS], BF16))
        expg = _es.enter_context(nc.sbuf_tensor("expg", [128, 8, GS], BF16))
        rsum = _es.enter_context(nc.sbuf_tensor("rsum", [16, GS], F32))
        wTg = _es.enter_context(nc.sbuf_tensor("wTg", [128, CC, GS], BF16))
        gbuf = _es.enter_context(nc.sbuf_tensor("gbuf", [128, 2, C], BF16))
        ubuf = _es.enter_context(nc.sbuf_tensor("ubuf", [128, 2, C], BF16))
        dbuf = _es.enter_context(nc.sbuf_tensor("dbuf", [128, 2, INTER], BF16))
        wpsl = _es.enter_context(nc.sbuf_tensor("wpsl", [128, 2, C], BF16))
        sgb = _es.enter_context(nc.sbuf_tensor("sgb", [128, 2, GS], BF16))
        oT_sb = _es.enter_context(nc.sbuf_tensor("oT_sb", [128, 2, GS], F32))
        xpc = _es.enter_context(nc.sbuf_tensor("xpc", [128, 2, 128], F32))
        res_sb = _es.enter_context(nc.sbuf_tensor("res_sb", [128, 2, 128], F32))
        epsb = _es.enter_context(nc.sbuf_tensor("epsb", [128, 1], F32))
        junkv = _es.enter_context(nc.sbuf_tensor("junkv", [128, 1], F32))
        jnk2 = _es.enter_context(nc.sbuf_tensor("jnk2", [128, 16], BF16))
        junkx = _es.enter_context(nc.sbuf_tensor("junkx", [128, 16], F32))
        PGA = _es.enter_context(nc.psum_tensor("PGA", [128, 512], F32))
        PGB = _es.enter_context(nc.psum_tensor("PGB", [128, 512], F32))
        PB0 = _es.enter_context(nc.psum_tensor("PB0", [128, 512], F32))
        PB1 = _es.enter_context(nc.psum_tensor("PB1", [128, 512], F32))
        PO0 = _es.enter_context(nc.psum_tensor("PO0", [128, 512], F32))
        PO1 = _es.enter_context(nc.psum_tensor("PO1", [128, 512], F32))
        PSX = _es.enter_context(nc.psum_tensor("PSX", [128, 512], F32))
        PSY = _es.enter_context(nc.psum_tensor("PSY", [128, 512], F32))
        h_eps = _es.enter_context(nc.semaphore("s_eps"))
        h_c = _es.enter_context(nc.semaphore("s_c"))
        h_gl = _es.enter_context(nc.semaphore("s_gl"))
        h_gv = _es.enter_context(nc.semaphore("s_gv"))
        h_gw = _es.enter_context(nc.semaphore("s_gw"))
        h_ag = _es.enter_context(nc.semaphore("s_ag"))
        h_xl = _es.enter_context(nc.semaphore("s_xl"))
        h_app = _es.enter_context(nc.semaphore("s_app"))
        h_xnw = _es.enter_context(nc.semaphore("s_xnw"))
        h_x64 = _es.enter_context(nc.semaphore("s_x64"))
        h_wk = _es.enter_context(nc.semaphore("s_wk"))
        h_kmm = _es.enter_context(nc.semaphore("s_kmm"))
        h_kev = _es.enter_context(nc.semaphore("s_kev"))
        h_rope = _es.enter_context(nc.semaphore("s_rope"))
        h_ktr = _es.enter_context(nc.semaphore("s_ktr"))
        h_ktev = _es.enter_context(nc.semaphore("s_ktev"))
        h_scr = _es.enter_context(nc.semaphore("s_scr"))
        h_ss = _es.enter_context(nc.semaphore("s_ss"))
        h_we = _es.enter_context(nc.semaphore("s_we"))
        h_rr = _es.enter_context(nc.semaphore("s_rr"))
        h_wn = _es.enter_context(nc.semaphore("s_wn"))
        h_wt = _es.enter_context(nc.semaphore("s_wt"))
        h_wtev = _es.enter_context(nc.semaphore("s_wtev"))
        h_am = _es.enter_context(nc.semaphore("s_am"))
        h_aev = _es.enter_context(nc.semaphore("s_aev"))
        h_xnt = _es.enter_context(nc.semaphore("s_xnt"))
        h_wgl = _es.enter_context(nc.semaphore("s_wgl"))
        h_gmm = _es.enter_context(nc.semaphore("s_gmm"))
        h_gex = _es.enter_context(nc.semaphore("s_gex"))
        h_smm = _es.enter_context(nc.semaphore("s_smm"))
        h_rcp = _es.enter_context(nc.semaphore("s_rcp"))
        h_bc = _es.enter_context(nc.semaphore("s_bc"))
        h_gn = _es.enter_context(nc.semaphore("s_gn"))
        h_wm = _es.enter_context(nc.semaphore("s_wm"))
        h_wev = _es.enter_context(nc.semaphore("s_wev"))
        h_fs = _es.enter_context(nc.semaphore("s_fs"))
        h_gu = _es.enter_context(nc.semaphore("s_gu"))
        h_si = _es.enter_context(nc.semaphore("s_si"))
        h_ht = _es.enter_context(nc.semaphore("s_ht"))
        h_ds = _es.enter_context(nc.semaphore("s_ds"))
        h_omm = _es.enter_context(nc.semaphore("s_omm"))
        h_oev = _es.enter_context(nc.semaphore("s_oev"))
        h_tr = _es.enter_context(nc.semaphore("s_tr"))
        h_xp = _es.enter_context(nc.semaphore("s_xp"))
        h_res = _es.enter_context(nc.semaphore("s_res"))
        h_out = _es.enter_context(nc.semaphore("s_out"))
        h_d1 = _es.enter_context(nc.semaphore("s_d1"))
        h_d2 = _es.enter_context(nc.semaphore("s_d2"))
        h_d3 = _es.enter_context(nc.semaphore("s_d3"))
        h_d4 = _es.enter_context(nc.semaphore("s_d4"))
        block = _es.enter_context(nc.Block())
        s_c = Sem(h_c); s_gl = Sem(h_gl); s_gv = Sem(h_gv); s_gw = Sem(h_gw)
        s_ag = Sem(h_ag); s_xl = Sem(h_xl); s_app = Sem(h_app)
        s_xnw = Sem(h_xnw); s_x64 = Sem(h_x64); s_wk = Sem(h_wk)
        s_kmm = Sem(h_kmm); s_kev = Sem(h_kev); s_rope = Sem(h_rope)
        s_ktr = Sem(h_ktr); s_ktev = Sem(h_ktev); s_scr = Sem(h_scr)
        s_ss = Sem(h_ss); s_we = Sem(h_we); s_rr = Sem(h_rr); s_wn = Sem(h_wn)
        s_wt = Sem(h_wt); s_wtev = Sem(h_wtev); s_am = Sem(h_am)
        s_aev = Sem(h_aev); s_xnt = Sem(h_xnt)
        s_wgl = Sem(h_wgl); s_gmm = Sem(h_gmm); s_gex = Sem(h_gex)
        s_smm = Sem(h_smm); s_rcp = Sem(h_rcp); s_bc = Sem(h_bc)
        s_gn = Sem(h_gn); s_wm = Sem(h_wm); s_wev = Sem(h_wev)
        s_fs = Sem(h_fs); s_gu = Sem(h_gu); s_si = Sem(h_si)
        s_ht = Sem(h_ht); s_ds = Sem(h_ds); s_omm = Sem(h_omm)
        s_oev = Sem(h_oev); s_tr = Sem(h_tr); s_xp = Sem(h_xp)
        s_res = Sem(h_res); s_out = Sem(h_out)

        banks = [PGA, PGB, PB0, PB1, PO0, PO1, PSX, PSY]

        gw = [dbuf[:, 0, 0:GFREE], dbuf[:, 0, GFREE:2 * GFREE]]
        gi_bf = dbuf[:, 1, 0:GFREE]
        g_ct = dbuf[:, 1, GFREE:2 * GFREE]
        wp_stage = [gbuf[:, 0, :], gbuf[:, 1, :]]
        xnT64 = [xnT[:, cc, 0:64] for cc in range(CC)]

        @em("v")
        def _(e):
            e.memset(epsb[:], EPS).then_inc(h_eps, 1)

        def idx_tile_ap(k):
            return bass.AP(idx_h, k * 128 * GFREE, [[GFREE, 128], [1, GFREE]])

        def ag1_tile_ap(k):
            return bass.AP(ag1_in, k * 128 * GFREE, [[GFREE, 128], [1, GFREE]])

        @em("gp")
        def _(e):
            e.dma_start(out=gi_in[:, 0, :], in_=idx_tile_ap(0)).then_inc(s_gl.h, 16)
            e.dma_start(out=gi_in[:, 1, :], in_=idx_tile_ap(1)).then_inc(s_gl.h, 16)
            e.dma_start(out=c_idf[:], in_=identf_h.ap()).then_inc(s_c.h, 16)
            e.dma_start(out=c_mask[:], in_=maskp_h.ap()).then_inc(s_c.h, 16)
            e.dma_start(out=c_cos[:], in_=cos_h.ap()).then_inc(s_c.h, 16)
            e.dma_start(out=c_sin[:], in_=sin_h.ap()).then_inc(s_c.h, 16)
            e.dma_start(out=c_ones[:], in_=onesb_h.ap()).then_inc(s_c.h, 16)
            e.dma_start(out=c_bcb[:], in_=bcb_h.ap()).then_inc(s_c.h, 16)
            e.dma_start(out=qTb[:], in_=qT_h.ap()).then_inc(s_c.h, 16)
            e.dma_start(out=wp_stage[0], in_=wpa_h.ap()[0:128, :]).then_inc(s_c.h, 16)
            e.dma_start(out=wp_stage[1], in_=wpa_h.ap()[128:256, :]).then_inc(s_c.h, 16)
        s_gl.bump(32)
        s_c.bump(9 * 16)
        CONSTS = s_c.n

        @em("gp")
        def _(e):
            for cc in range(CC):
                e.dma_start(out=wgb[:, cc, :],
                            in_=wg_h.ap()[cc * 128:(cc + 1) * 128, :]
                            ).then_inc(s_wgl.h, 16)
            e.dma_start(out=jnk2[:],
                        in_=wg_h.ap()[0:128, 0:16]).then_inc(s_wgl.h, 16)
        s_wgl.bump(17 * 16)

        for k in range(GTILES):
            @em("v")
            def _(e, k=k):
                e.wait_ge(s_gl.h, 16 * (k + 1))
                if k == 0:
                    e.wait_ge(s_c.h, CONSTS)
                if k >= 2:
                    e.wait_ge(s_gw.h, 16 * (k - 1))
                e.tensor_copy(out=gi_bf, in_=gi_in[:, k % 2, :])
                e.tensor_scalar(out=gw[k % 2], in0=gi_bf,
                                scalar1=0.0, scalar2=lut_vals[0],
                                op0=OP.is_equal, op1=OP.mult)
                last = None
                for v in range(1, 16):
                    e.tensor_scalar(out=g_ct, in0=gi_bf,
                                    scalar1=float(v), scalar2=lut_vals[v],
                                    op0=OP.is_equal, op1=OP.mult)
                    last = e.tensor_add(gw[k % 2], gw[k % 2], g_ct)
                last.then_inc(s_gv.h, 1)
            s_gv.bump(1)

            @em("gp")
            def _(e, k=k):
                e.wait_ge(s_gv.h, k + 2)
                e.dma_start(out=ag1_tile_ap(k), in_=gw[k % 2]).then_inc(s_gw.h, 16)
                if k + 2 < GTILES:
                    e.dma_start(out=gi_in[:, k % 2, :],
                                in_=idx_tile_ap(k + 2)).then_inc(s_gl.h, 16)
            s_gw.bump(16)
            if k + 2 < GTILES:
                s_gl.bump(16)

        @em("v")
        def _(e):
            e.memset(junkv[:], 0.0).then_inc(s_gv.h, 1)
        s_gv.bump(1)

        @em("gp")
        def _(e, want=s_c.n):
            e.wait_ge(s_c.h, want)
            e.dma_start(out=ag2_in.ap()[0:128, :], in_=wp_stage[0]).then_inc(s_gw.h, 16)
            e.dma_start(out=ag2_in.ap()[128:256, :], in_=wp_stage[1]).then_inc(s_gw.h, 16)
        s_gw.bump(32)

        AGW_WANT = s_gw.n

        ssq = ln_s[:, 0:1]; ssum = ln_s[:, 1:2]; mean = ln_s[:, 2:3]
        nm2 = ln_s[:, 3:4]; varep = ln_s[:, 4:5]; rstd = ln_s[:, 5:6]
        t9 = ln_s[:, 6:7]; nmr = ln_s[:, 7:8]; lnjk = ln_s[:, 8:9]

        def emit_ln_tile(tt):
            @em("s")
            def _(e, tt=tt):
                if tt == 0:
                    e.wait_ge(h_eps, 1)
                if tt >= 1:
                    e.wait_ge(s_app.h, tt)
                e.dma_start(out=x_t[:],
                            in_=x_h.ap()[tt * 128:(tt + 1) * 128, :]
                            ).then_inc(s_xl.h, 16)
                e.wait_ge(s_xl.h, 16 * (tt + 1))
                if tt >= 2:
                    e.wait_ge(s_xnw.h, 16 * (tt - 1))
                e.activation(out=xn_t[:, tt % 2, :], in_=x_t[:], func=AX.Square,
                             accum_out=ssq)
                e.activation(out=xn_t[:, tt % 2, :], in_=x_t[:], func=AX.Identity,
                             accum_out=ssum)
                e.mul(out=mean, in_=ssum, mul=1.0 / C)
                e.activation(out=nm2, in_=mean, func=AX.Square)
                e.activation(out=nm2, in_=nm2, func=AX.Identity,
                             scale=-1.0, bias=epsb[:])
                e.activation(out=lnjk, in_=ssq, func=AX.Identity)
                e.activation(out=varep, in_=ssq, func=AX.Identity,
                             scale=1.0 / C, bias=nm2)
                e.activation(out=varep, in_=varep, func=AX.Ln)
                e.activation(out=rstd, in_=varep, func=AX.Exp, scale=-0.5)
                e.activation(out=t9, in_=mean, func=AX.Identity, scale=rstd)
                e.mul(out=nmr, in_=t9, mul=-1.0)
                e.activation(out=lnjk, in_=ssq, func=AX.Identity)
                e.activation(out=xn_t[:, tt % 2, :], in_=x_t[:], func=AX.Identity,
                             scale=rstd, bias=nmr)
                e.activation(out=lnjk, in_=ssq,
                             func=AX.Identity).then_inc(s_app.h, 1)
            s_xl.bump(16)
            s_app.bump(1)

            @em("sy")
            def _(e, tt=tt):
                e.wait_ge(s_app.h, tt + 1)
                e.dma_start(out=xn_scr.ap()[tt * 128:(tt + 1) * 128, :],
                            in_=xn_t[:, tt % 2, :]).then_inc(s_xnw.h, 16)
            s_xnw.bump(16)

        emit_ln_tile(0)

        @em("sy")
        def _(e):
            e.wait_ge(s_xnw.h, 16)
            e.wait_ge(s_c.h, CONSTS)
            for cc in range(CC):
                e.dma_start(out=xnT64[cc],
                            in_=xn_scr.ap()[0:64, cc * 128:(cc + 1) * 128],
                            transpose=True).then_inc(s_x64.h, 16)
        s_x64.bump(16 * CC)

        for tt in range(1, NT):
            emit_ln_tile(tt)

        kvps_m = ([PGA[0:64, :], PGB[0:64, :], PB0[0:64, :], PB1[0:64, :]],
                  [PO0[0:64, :], PO1[0:64, :], PSX[0:64, :], PSY[0:64, :]])
        for m in range(2):
            for cc in range(CC):
                i = m * CC + cc

                @em("gp")
                def _(e, m=m, cc=cc, i=i):
                    if i >= 1:
                        e.wait_ge(s_kmm.h, i)
                    e.dma_start(out=wkb[:],
                                in_=(wk_h if m == 0 else wv_h).ap()[cc * 128:(cc + 1) * 128, :]
                                ).then_inc(s_wk.h, 16)
                s_wk.bump(16)

                @em("t")
                def _(e, m=m, cc=cc, i=i):
                    e.wait_ge(s_wk.h, 16 * (i + 1))
                    if m == 0 and cc == 0:
                        e.wait_ge(s_x64.h, 16 * CC)
                    mm = None
                    for n in range(4):
                        mm = e.matmul(kvps_m[m][n], xnT64[cc],
                                      wkb[:, 512 * n:512 * (n + 1)],
                                      start=(cc == 0), stop=(cc == CC - 1))
                    mm.then_inc(s_kmm.h, 1)
                s_kmm.bump(1)

            @em("v")
            def _(e, m=m):
                e.wait_ge(s_kmm.h, 16 * (m + 1))
                dst = k_tm if m == 0 else v_tmb
                mm = None
                for n in range(4):
                    mm = e.tensor_copy(out=dst[:, 512 * n:512 * (n + 1)],
                                       in_=kvps_m[m][n])
                mm.then_inc(s_kev.h, 1)
            s_kev.bump(1)

        @em("gp")
        def _(e):
            e.wait_ge(s_gw.h, AGW_WANT)
            e.wait_ge(s_aev.h, 8)
            e.collective_compute(
                "AllGather", OP.bypass, replica_groups=[list(range(8))],
                ins=[ag1_in.ap().opt()], outs=[w_scr.ap().opt()],
            ).then_inc(s_ag.h, 1)
            e.collective_compute(
                "AllGather", OP.bypass, replica_groups=[list(range(8))],
                ins=[ag2_in.ap().opt()], outs=[wp_scr.ap().opt()],
            ).then_inc(s_ag.h, 1)
        s_ag.bump(2)

        for h in range(H):
            @em("v")
            def _(e, h=h):
                if h == 0:
                    e.wait_ge(s_kev.h, 1)
                    e.wait_ge(s_c.h, CONSTS)
                else:
                    e.wait_ge(s_ktr.h, h)
                base = h * 128
                x0 = bass.AP(k_tm, base + 0, [[C, 64], [2, 64]])
                x1 = bass.AP(k_tm, base + 1, [[C, 64], [2, 64]])
                r0 = bass.AP(krh, 0, [[128, 64], [2, 64]])
                r1 = bass.AP(krh, 1, [[128, 64], [2, 64]])
                tmp = ktmp[:, :]
                e.tensor_mul(tmp, x0, c_sin[:, :])
                e.tensor_mul(r1, x1, c_cos[:, :])
                e.tensor_tensor(out=r1, in0=r1, in1=tmp, op=OP.add)
                e.tensor_mul(tmp, x1, c_sin[:, :])
                e.tensor_mul(r0, x0, c_cos[:, :])
                e.tensor_tensor(out=r0, in0=r0, in1=tmp,
                                op=OP.subtract).then_inc(s_rope.h, 1)
            s_rope.bump(1)

            @em("t")
            def _(e, h=h):
                e.wait_ge(s_rope.h, h + 1)
                if h == 0:
                    e.wait_ge(s_kev.h, 2)
                if h >= 2:
                    e.wait_ge(s_ktev.h, h - 1)
                pt = (PSX, PSY)[h % 2]
                e.transpose(pt[:, 0:64], krh[:, :],
                            c_idf[0:64, 0:64]).then_inc(s_ktr.h, 1)
            s_ktr.bump(1)

            @em("v")
            def _(e, h=h):
                e.wait_ge(s_ktr.h, h + 1)
                pt = (PSX, PSY)[h % 2]
                e.tensor_copy(out=kTb[:, h * 64:(h + 1) * 64],
                              in_=pt[:, 0:64]).then_inc(s_ktev.h, 1)
            s_ktev.bump(1)

        for q in range(8):
            @em("t")
            def _(e, q=q):
                if q == 0:
                    e.wait_ge(s_ktev.h, H)
                else:
                    e.wait_ge(s_ss.h, q)
                e.matmul(PGA[0:64, 0:64], qTb[:, (2 * q) * 64:(2 * q + 1) * 64],
                         kTb[:, (2 * q) * 64:(2 * q + 1) * 64],
                         start=True, stop=True)
                e.matmul(PGA[64:128, 0:64], qTb[:, (2 * q + 1) * 64:(2 * q + 2) * 64],
                         kTb[:, (2 * q + 1) * 64:(2 * q + 2) * 64],
                         start=True, stop=True,
                         tile_position=(0, 64)).then_inc(s_scr.h, 1)
            s_scr.bump(1)

            @em("v")
            def _(e, q=q):
                e.wait_ge(s_scr.h, q + 1)
                if debug and q >= 1:
                    e.wait_ge(h_d1, 16 * q)
                e.scalar_tensor_tensor(out=ss_f[:], in0=PGA[:, 0:64], scalar=SCALE,
                                       in1=c_mask[:], op0=OP.mult, op1=OP.add)
                e.reduce_max(nmax[:], ss_f[:], axis=AXL.X, negate=True)
                e.memset(junkv[:], 0.0)
                e.tensor_scalar_add(out=ss_f[:], in0=ss_f[:],
                                    scalar1=nmax[:]).then_inc(s_ss.h, 1)
            s_ss.bump(1)

            if debug:
                @em("sy")
                def _(e, q=q):
                    e.wait_ge(s_ss.h, q + 1)
                    e.dma_start(out=ss_dbg.ap()[:, q * 64:(q + 1) * 64],
                                in_=ss_f[:]).then_inc(h_d1, 16)

            @em("s")
            def _(e, q=q):
                e.wait_ge(s_ss.h, q + 1)
                if debug and q >= 1:
                    e.wait_ge(h_d2, 16 * q)
                e.activation(out=w_b[:], in_=ss_f[:], func=AX.Exp,
                             accum_out=rs[:]).then_inc(s_we.h, 1)
            s_we.bump(1)

            @em("v")
            def _(e, q=q):
                e.wait_ge(s_we.h, q + 1)
                e.reciprocal(rr[:], rs[:])
                e.memset(junkv[:], 0.0)
                e.tensor_scalar_mul(out=w_b[:], in0=w_b[:],
                                    scalar1=rr[:]).then_inc(s_wn.h, 1)
            s_wn.bump(1)

            if debug:
                @em("sy")
                def _(e, q=q):
                    e.wait_ge(s_wn.h, q + 1)
                    e.dma_start(out=wb_dbg.ap()[:, q * 64:(q + 1) * 64],
                                in_=w_b[:]).then_inc(h_d2, 16)

            @em("t")
            def _(e, q=q):
                e.wait_ge(s_wn.h, q + 1)
                if q > 0:
                    e.wait_ge(s_wtev.h, q)
                e.transpose(PGB[0:64, 0:128], w_b[:],
                            c_idf[:, :]).then_inc(s_wt.h, 1)
            s_wt.bump(1)

            @em("v")
            def _(e, q=q):
                e.wait_ge(s_wt.h, q + 1)
                if debug and q >= 1:
                    e.wait_ge(h_d3, 16 * q)
                e.tensor_copy(out=wTp[:], in_=PGB[0:64, 0:128]).then_inc(s_wtev.h, 1)
            s_wtev.bump(1)

            if debug:
                @em("sy")
                def _(e, q=q):
                    e.wait_ge(s_wtev.h, q + 1)
                    e.dma_start(out=wtp_dbg.ap()[:, q * 128:(q + 1) * 128],
                                in_=wTp[:]).then_inc(h_d3, 16)

            @em("t")
            def _(e, q=q):
                e.wait_ge(s_wtev.h, q + 1)
                if q > 0:
                    e.wait_ge(s_aev.h, q)
                e.matmul(PB0[0:64, 0:128], wTp[:, 0:64],
                         v_tmb[:, (2 * q) * 128:(2 * q + 1) * 128],
                         start=True, stop=True)
                e.matmul(PB0[64:128, 0:128], wTp[:, 64:128],
                         v_tmb[:, (2 * q + 1) * 128:(2 * q + 2) * 128],
                         start=True, stop=True,
                         tile_position=(0, 64)).then_inc(s_am.h, 1)
            s_am.bump(1)

            @em("v")
            def _(e, q=q):
                e.wait_ge(s_am.h, q + 1)
                e.tensor_copy(out=attnP[:, q, :],
                              in_=PB0[:, 0:128]).then_inc(s_aev.h, 1)
            s_aev.bump(1)

        piece = 0
        for G in range(NG):
            @em("sy")
            def _(e, G=G):
                e.wait_ge(s_xnw.h, 16 * 4 * (G + 1))
                if G == 0:
                    e.wait_ge(s_kmm.h, 32)
                else:
                    e.wait_ge(s_gu.h, KC * G)
                for cc in range(CC):
                    e.dma_start(out=xnT[:, cc, :],
                                in_=xn_scr.ap()[G * GS:(G + 1) * GS,
                                                cc * 128:(cc + 1) * 128],
                                transpose=True).then_inc(s_xnt.h, 16)
            s_xnt.bump(16 * CC)

            for cc in range(CC):
                @em("t")
                def _(e, G=G, cc=cc):
                    if G == 0 and cc == 0:
                        e.wait_ge(s_wgl.h, 16 * (CC + 1))
                    if cc == 0:
                        e.wait_ge(s_xnt.h, 16 * CC * (G + 1))
                        if G == 0:
                            e.wait_ge(s_aev.h, 8)
                        else:
                            e.wait_ge(s_oev.h, CT * G)
                            e.wait_ge(s_res.h, 64 * G)
                            e.wait_ge(s_si.h, KC * G)
                            e.wait_ge(s_ht.h, KC * G)
                            e.wait_ge(s_gn.h, 8 * G)
                            e.wait_ge(s_wev.h, H * G)
                            e.wait_ge(s_rcp.h, G)
                    mm = None
                    for gt in range(8):
                        mm = e.matmul(banks[gt][:, 0:GS],
                                      wgb[:, cc, gt * 128:(gt + 1) * 128],
                                      xnT[:, cc, :],
                                      start=(cc == 0), stop=(cc == CC - 1))
                    mm.then_inc(s_gmm.h, 1)
                s_gmm.bump(1)

            for gt in range(8):
                @em("s")
                def _(e, G=G, gt=gt):
                    e.wait_ge(s_gmm.h, 16 * (G + 1))
                    if debug and G == 1 and gt == 0:
                        e.wait_ge(h_d4, 16)
                    e.activation(out=expg[:, gt, :], in_=banks[gt][:, 0:GS],
                                 func=AX.Exp, scale=SCALE).then_inc(s_gex.h, 1)
                s_gex.bump(1)

            @em("t")
            def _(e, G=G):
                e.wait_ge(s_gex.h, 8 * (G + 1))
                mm = None
                for q in range(8):
                    mm = e.matmul(PSX[0:16, 0:GS], c_ones[:, q * 16:(q + 1) * 16],
                                  expg[:, q, :], start=(q == 0), stop=(q == 7))
                mm.then_inc(s_smm.h, 1)
            s_smm.bump(1)

            @em("v")
            def _(e, G=G):
                e.wait_ge(s_smm.h, G + 1)
                e.reciprocal(rsum[:], PSX[0:16, 0:GS]).then_inc(s_rcp.h, 1)
            s_rcp.bump(1)

            for q in range(8):
                pb = (PB0, PB1)[q % 2]

                @em("t")
                def _(e, G=G, q=q, pb=pb):
                    if q == 0:
                        e.wait_ge(s_rcp.h, G + 1)
                    if 8 * G + q >= 2:
                        e.wait_ge(s_gn.h, 8 * G + q - 1)
                    e.matmul(pb[:, :], c_bcb[:, q * 128:(q + 1) * 128],
                             rsum[:, :], start=True, stop=True).then_inc(s_bc.h, 1)
                s_bc.bump(1)

                @em("v")
                def _(e, G=G, q=q, pb=pb):
                    e.wait_ge(s_bc.h, 8 * G + q + 1)
                    e.tensor_tensor(out=expg[:, q, :], in0=expg[:, q, :],
                                    in1=pb[:, 0:GS], op=OP.mult).then_inc(s_gn.h, 1)
                s_gn.bump(1)

            if debug and G == 0:
                @em("sy")
                def _(e):
                    e.wait_ge(s_gn.h, 8)
                    e.dma_start(out=exg_dbg.ap(),
                                in_=expg[:, :, :]).then_inc(h_d4, 16)

            for h in range(H):
                q = h // 2
                sl = slice(0, 64) if h % 2 == 0 else slice(64, 128)
                po = (PB0, PB1)[h % 2]

                @em("t")
                def _(e, G=G, h=h, q=q, sl=sl, po=po):
                    e.wait_ge(s_gn.h, 8 * (G + 1))
                    if H * G + h >= 2:
                        e.wait_ge(s_wev.h, H * G + h - 1)
                    e.matmul(po[:, 0:GS], attnP[sl, q, :], expg[sl, q, :],
                             start=True, stop=True).then_inc(s_wm.h, 1)
                s_wm.bump(1)

                @em("v")
                def _(e, G=G, h=h, po=po):
                    e.wait_ge(s_wm.h, H * G + h + 1)
                    if debug and G == 1 and h == 0:
                        e.wait_ge(h_d4, 32)
                    e.tensor_copy(out=wTg[:, h, :],
                                  in_=po[:, 0:GS]).then_inc(s_wev.h, 1)
                s_wev.bump(1)

            if debug and G == 0:
                @em("sy")
                def _(e):
                    e.wait_ge(s_wev.h, H)
                    e.dma_start(out=wtg_dbg.ap(),
                                in_=wTg[:, :, :]).then_inc(h_d4, 16)

            for it in range(KC):
                idx = KC * G + it

                @em("sy")
                def _(e, G=G, it=it, idx=idx):
                    if G == 0 and it == 0:
                        e.wait_ge(s_ag.h, 2)
                    if idx >= 2:
                        e.wait_ge(s_gu.h, idx - 1)
                    e.dma_start(out=gbuf[:, idx % 2, :],
                                in_=w_scr.ap()[it * 128:(it + 1) * 128, :]
                                ).then_inc(s_fs.h, 16)
                    e.dma_start(out=ubuf[:, idx % 2, :],
                                in_=w_scr.ap()[5504 + it * 128:5504 + (it + 1) * 128, :]
                                ).then_inc(s_fs.h, 16)
                s_fs.bump(32)

                gpair = (PGA, PGB) if idx % 2 == 0 else (PO0, PO1)

                @em("t")
                def _(e, G=G, it=it, idx=idx, gpair=gpair):
                    e.wait_ge(s_fs.h, 32 * (idx + 1))
                    if it == 0:
                        e.wait_ge(s_gex.h, 8 * (G + 1))
                    if G == 0 and it == 1:
                        e.wait_ge(s_aev.h, 8)
                    if idx >= 2:
                        e.wait_ge(s_si.h, idx - 1)
                        e.wait_ge(s_ht.h, idx - 1)
                    mm = None
                    for cc in range(CC):
                        mm = e.matmul(gpair[0][:, :],
                                      gbuf[:, idx % 2, cc * 128:(cc + 1) * 128],
                                      xnT[:, cc, :],
                                      start=(cc == 0), stop=(cc == CC - 1))
                    for cc in range(CC):
                        mm = e.matmul(gpair[1][:, :],
                                      ubuf[:, idx % 2, cc * 128:(cc + 1) * 128],
                                      xnT[:, cc, :],
                                      start=(cc == 0), stop=(cc == CC - 1))
                    mm.then_inc(s_gu.h, 1)
                s_gu.bump(1)

                @em("s")
                def _(e, idx=idx, gpair=gpair):
                    e.wait_ge(s_gu.h, idx + 1)
                    if idx >= 2:
                        e.wait_ge(s_ht.h, idx - 1)
                    e.activation(out=sgb[:, idx % 2, :], in_=gpair[0][:, :],
                                 func=AX.Silu).then_inc(s_si.h, 1)
                s_si.bump(1)

                @em("v")
                def _(e, it=it, idx=idx, gpair=gpair):
                    e.wait_ge(s_si.h, idx + 1)
                    if idx == 0:
                        e.wait_ge(s_app.h, NT)
                    if debug and idx == KC:
                        e.wait_ge(h_d4, 48)
                    e.tensor_tensor(out=hT[:, it, :], in0=sgb[:, idx % 2, :],
                                    in1=gpair[1][:, :], op=OP.mult).then_inc(s_ht.h, 1)
                s_ht.bump(1)

            if debug and G == 0:
                @em("sy")
                def _(e):
                    e.wait_ge(s_ht.h, KC)
                    e.dma_start(out=ht_dbg.ap(),
                                in_=hT[:, :, :]).then_inc(h_d4, 16)

            def emit_pieces(G, ct, idx):
                for tt in range(4):
                    p = 4 * idx + tt

                    if p == 0:
                        @em("sy")
                        def _(e):
                            e.dma_start(out=xpc[:, 0, :],
                                        in_=x_h.ap()[0:128, 0:128]).then_inc(s_xp.h, 16)
                            e.dma_start(out=xpc[:, 1, :],
                                        in_=x_h.ap()[128:256, 0:128]).then_inc(s_xp.h, 16)
                        s_xp.bump(32)

                    @em("t")
                    def _(e, idx=idx, tt=tt, p=p):
                        e.wait_ge(s_oev.h, idx + 1)
                        if p >= 2:
                            e.wait_ge(s_res.h, p - 1)
                        pt = (PSX, PSY)[p % 2]
                        e.transpose(pt[:, 0:128],
                                    oT_sb[:, idx % 2, tt * 128:(tt + 1) * 128],
                                    c_idf[:, :]).then_inc(s_tr.h, 1)
                    s_tr.bump(1)

                    @em("v")
                    def _(e, p=p):
                        e.wait_ge(s_tr.h, p + 1)
                        e.wait_ge(s_xp.h, 16 * (p + 2))
                        if p >= 2:
                            e.wait_ge(s_out.h, 16 * (p - 1))
                        pt = (PSX, PSY)[p % 2]
                        e.tensor_tensor(out=res_sb[:, p % 2, :],
                                        in0=pt[:, 0:128], in1=xpc[:, p % 2, :],
                                        op=OP.add)
                        e.memset(junkv[:], 0.0).then_inc(s_res.h, 1)
                    s_res.bump(1)

                    @em("gp")
                    def _(e, G=G, ct=ct, tt=tt, p=p):
                        e.wait_ge(s_res.h, p + 1)
                        r0 = G * GS + tt * 128
                        e.dma_start(out=out_h.ap()[r0:r0 + 128, ct * 128:(ct + 1) * 128],
                                    in_=res_sb[:, p % 2, :]).then_inc(s_out.h, 16)
                    s_out.bump(16)

                    p2 = p + 2
                    @em("sy")
                    def _(e, p=p, p2=p2):
                        e.wait_ge(s_res.h, p + 1)
                        if p2 < NG * CT * 4:
                            G2 = p2 // 64
                            ct2 = (p2 % 64) // 4
                            tt2 = p2 % 4
                            r2 = G2 * GS + tt2 * 128
                            e.dma_start(out=xpc[:, p2 % 2, :],
                                        in_=x_h.ap()[r2:r2 + 128,
                                                     ct2 * 128:(ct2 + 1) * 128]
                                        ).then_inc(s_xp.h, 16)
                        else:
                            e.dma_start(out=junkx[:],
                                        in_=x_h.ap()[0:128, 0:16]
                                        ).then_inc(s_xp.h, 16)
                    s_xp.bump(16)

            for ct in range(CT):
                idx = CT * G + ct
                po = (PO0, PO1)[ct % 2]

                @em("sy")
                def _(e, G=G, ct=ct, idx=idx):
                    if idx >= 2:
                        e.wait_ge(s_omm.h, idx - 1)
                    e.dma_start(out=dbuf[:, idx % 2, :],
                                in_=bass.AP(w_scr, 11008 * 2048 + ct * 128 * INTER,
                                            [[INTER, 128], [1, INTER]])
                                ).then_inc(s_ds.h, 16)
                    e.dma_start(out=wpsl[:, idx % 2, :],
                                in_=wp_scr.ap()[ct * 128:(ct + 1) * 128, :]
                                ).then_inc(s_ds.h, 16)
                s_ds.bump(32)

                @em("t")
                def _(e, G=G, ct=ct, idx=idx, po=po):
                    e.wait_ge(s_ds.h, 32 * (idx + 1))
                    if ct == 0:
                        e.wait_ge(s_ht.h, KC * (G + 1))
                        e.wait_ge(s_wev.h, H * (G + 1))
                    if idx >= 2:
                        e.wait_ge(s_oev.h, idx - 1)
                    mm = None
                    for kc in range(KC):
                        mm = e.matmul(po[:, :],
                                      dbuf[:, idx % 2, kc * 128:(kc + 1) * 128],
                                      hT[:, kc, :], start=(kc == 0), stop=False)
                    for cc in range(CC):
                        mm = e.matmul(po[:, :],
                                      wpsl[:, idx % 2, cc * 128:(cc + 1) * 128],
                                      wTg[:, cc, :], start=False, stop=(cc == CC - 1))
                    mm.then_inc(s_omm.h, 1)
                s_omm.bump(1)

                @em("v")
                def _(e, idx=idx, po=po):
                    e.wait_ge(s_omm.h, idx + 1)
                    if idx >= 2:
                        e.wait_ge(s_tr.h, 4 * (idx - 1))
                    for st in (3, 2, 1, 0):
                        e.tensor_copy(out=oT_sb[:, idx % 2, st * 128:(st + 1) * 128],
                                      in_=po[:, st * 128:(st + 1) * 128])
                    e.memset(junkv[:], 0.0).then_inc(s_oev.h, 1)
                s_oev.bump(1)

                if ct >= 1:
                    emit_pieces(G, ct - 1, idx - 1)
            emit_pieces(G, CT - 1, CT * G + CT - 1)

        @em("gp")
        def _(e, want=s_out.n):
            e.wait_ge(s_out.h, want)
            if debug:
                e.wait_ge(h_d1, 128)
                e.wait_ge(h_d2, 128)
                e.wait_ge(h_d3, 128)
                e.wait_ge(h_d4, 48)
                e.dma_start(out=w_dbg.ap(), in_=w_scr.ap()).then_inc(s_out.h, 16)
                e.dma_start(out=wp_dbg.ap(), in_=wp_scr.ap()).then_inc(s_out.h, 16)
                e.dma_start(out=ktb_dbg.ap(), in_=kTb[:]).then_inc(s_out.h, 16)
                e.dma_start(out=atp_dbg.ap(), in_=attnP[:, :, :]).then_inc(s_out.h, 16)
                e.dma_start(out=vtm_dbg.ap(), in_=v_tmb[:]).then_inc(s_out.h, 16)
                e.wait_ge(s_out.h, want + 5 * 16)

        @block.gpsimd
        def _(eng):
            for name, fn in script:
                if name == "gp":
                    fn(eng)

        @block.sync
        def _(eng):
            for name, fn in script:
                if name == "sy":
                    fn(eng)

        @block.vector
        def _(eng):
            for name, fn in script:
                if name == "v":
                    fn(eng)

        @block.scalar
        def _(eng):
            for name, fn in script:
                if name == "s":
                    fn(eng)

        @block.tensor
        def _(eng):
            for name, fn in script:
                if name == "t":
                    fn(eng)

    return nc


def kernel(L0_attn_x, L0_attn_latent_queries, L0_attn_Wk, L0_attn_Wv,
           L0_attn_Wg, L0_attn_Wp, L0_ffn_lut, L0_ffn_gate_idx,
           L0_ffn_up_idx, L0_ffn_down_idx):
    x = np.ascontiguousarray(np.asarray(L0_attn_x, dtype=np.float32))
    lut_vals = [float(v) for v in np.asarray(L0_ffn_lut, dtype=np.float32)]
    xs, idx_shards, wp_shards, qT = _host_shards(
        x, np.asarray(L0_attn_latent_queries, dtype=np.float32),
        np.asarray(L0_attn_Wp, dtype=np.float32),
        np.asarray(L0_ffn_gate_idx), np.asarray(L0_ffn_up_idx),
        np.asarray(L0_ffn_down_idx))
    identf, identb, maskp, cosT, sinT, onesb, bcb = _consts()

    wk = np.ascontiguousarray(np.asarray(L0_attn_Wk, dtype=np.float32))
    wv = np.ascontiguousarray(np.asarray(L0_attn_Wv, dtype=np.float32))
    wg = np.ascontiguousarray(np.asarray(L0_attn_Wg, dtype=np.float32))

    nc = build(lut_vals)
    in_maps = []
    for j in range(8):
        in_maps.append({
            "x": xs[j], "wk": wk, "wv": wv, "wg": wg, "qT": qT,
            "idxs": idx_shards[j], "wpa": wp_shards[j],
            "identf": identf, "identb": identb, "maskp": maskp,
            "cosT": cosT, "sinT": sinT, "onesb": onesb, "bcb": bcb,
        })
    res = run_bass_kernel_spmd(nc, in_maps, core_ids=list(range(8)))
    out = np.stack([np.asarray(r["out"], dtype=np.float32) for r in res.results],
                   axis=0)
    return out


# version 16
# speedup vs baseline: 1.5652x; 1.3608x over previous
import math

import numpy as np
import ml_dtypes

import concourse.bass as bass
import concourse.mybir as mybir
from concourse.bass_utils import run_bass_kernel_spmd

F32 = mybir.dt.float32
BF16 = mybir.dt.bfloat16
I32 = mybir.dt.int32
AX = mybir.ActivationFunctionType
OP = mybir.AluOpType
AXL = mybir.AxisListType

T = 2048
C = 2048
H = 16
L = 64
HL = H * L
INTER = 5504
KC = INTER // 128
CC = C // 128
CT = C // 128
NT = T // 128
NG = 4
GS = T // NG
SCALE = 1.0 / math.sqrt(128.0)
NEG = -1.0e9
EPS = 1e-5

BLOB_ROWS = 3 * 5504
SH_ROWS = BLOB_ROWS // 8
GTILES = 32
GFREE = SH_ROWS * C // (128 * GTILES)
WP_SH = C // 8


def _consts():
    identf = np.eye(128, dtype=np.float32)
    identb = np.eye(128, dtype=ml_dtypes.bfloat16)
    lidx = np.arange(128) % 64
    tidx = np.arange(64)
    maskp = np.where(tidx[None, :] <= lidx[:, None], 0.0, NEG).astype(np.float32)
    inv = (10000.0 ** (-(np.arange(64, dtype=np.float32)) / 64.0)).astype(np.float32)
    ang = np.arange(64, dtype=np.float32)[:, None] * inv[None, :]
    cosT = np.cos(ang).astype(np.float32)
    sinT = np.sin(ang).astype(np.float32)
    onesb = np.zeros((128, 128), dtype=ml_dtypes.bfloat16)
    for q in range(8):
        onesb[0:64, q * 16 + 2 * q] = 1
        onesb[64:128, q * 16 + 2 * q + 1] = 1
    bcb = np.zeros((16, 1024), dtype=np.float32)
    for q in range(8):
        bcb[2 * q, q * 128: q * 128 + 64] = 1.0
        bcb[2 * q + 1, q * 128 + 64: q * 128 + 128] = 1.0
    return identf, identb, maskp, cosT, sinT, onesb, bcb


def _host_shards(x, lq, wp, gidx, uidx, didx):
    def scr_gu(idx):
        a = np.ascontiguousarray(idx).T
        a = a.reshape(16, 128, 43, 128).transpose(2, 1, 0, 3)
        return np.ascontiguousarray(a.reshape(5504, 2048))

    def scr_dn(idx):
        a = np.ascontiguousarray(idx).T
        a = a.reshape(43, 128, 16, 128).transpose(2, 1, 0, 3)
        return np.ascontiguousarray(a.reshape(5504, 2048))

    blob = np.concatenate([scr_gu(gidx), scr_gu(uidx), scr_dn(didx)], axis=0)
    idx_shards = [np.ascontiguousarray(s) for s in np.split(blob, 8, axis=0)]
    wpa = (np.ascontiguousarray(wp).reshape(16, 128, 16, 128)
           .transpose(2, 1, 0, 3).reshape(2048, 2048))
    wp_shards = [np.ascontiguousarray(s) for s in np.split(wpa, 8, axis=0)]
    qT = np.ascontiguousarray(
        np.transpose(lq[0], (2, 1, 0)).reshape(128, HL)).astype(np.float32, copy=False)
    xs = [np.ascontiguousarray(x[j]) for j in range(8)]
    return xs, idx_shards, wp_shards, qT


class Sem:
    __slots__ = ("h", "n")

    def __init__(self, h):
        self.h = h
        self.n = 0

    def bump(self, k=1):
        self.n += k
        return self.n


def build(lut_vals, debug=False):
    nc = bass.Bass("TRN2", target_bir_lowering=False)
    dbg_kind = {"kind": "ExternalOutput"} if debug else {}

    x_h = nc.dram_tensor("x", [T, C], F32, kind="ExternalInput")
    wk_h = nc.dram_tensor("wk", [C, C], F32, kind="ExternalInput")
    wv_h = nc.dram_tensor("wv", [C, C], F32, kind="ExternalInput")
    wg_h = nc.dram_tensor("wg", [C, HL], F32, kind="ExternalInput")
    qT_h = nc.dram_tensor("qT", [128, HL], F32, kind="ExternalInput")
    idx_h = nc.dram_tensor("idxs", [SH_ROWS, C], I32, kind="ExternalInput")
    wpa_h = nc.dram_tensor("wpa", [WP_SH, C], F32, kind="ExternalInput")
    identf_h = nc.dram_tensor("identf", [128, 128], F32, kind="ExternalInput")
    identb_h = nc.dram_tensor("identb", [128, 128], BF16, kind="ExternalInput")
    maskp_h = nc.dram_tensor("maskp", [128, 64], F32, kind="ExternalInput")
    cos_h = nc.dram_tensor("cosT", [64, 64], F32, kind="ExternalInput")
    sin_h = nc.dram_tensor("sinT", [64, 64], F32, kind="ExternalInput")
    onesb_h = nc.dram_tensor("onesb", [128, 128], BF16, kind="ExternalInput")
    bcb_h = nc.dram_tensor("bcb", [16, HL], F32, kind="ExternalInput")
    out_h = nc.dram_tensor("out", [T, C], F32, kind="ExternalOutput")
    xn_scr = nc.dram_tensor("xn_scr", [T, C], BF16, **dbg_kind)
    ag1_in = nc.dram_tensor("ag1_in", [SH_ROWS, C], BF16)
    w_scr = nc.dram_tensor("w_scr", [BLOB_ROWS, C], BF16)
    ag2_in = nc.dram_tensor("ag2_in", [WP_SH, C], BF16)
    wp_scr = nc.dram_tensor("wp_scr", [C, C], BF16)
    if debug:
        w_dbg = nc.dram_tensor("w_dbg", [BLOB_ROWS, C], BF16, kind="ExternalOutput")
        wp_dbg = nc.dram_tensor("wp_dbg", [C, C], BF16, kind="ExternalOutput")
        ktb_dbg = nc.dram_tensor("ktb_dbg", [128, HL], BF16, kind="ExternalOutput")
        atp_dbg = nc.dram_tensor("atp_dbg", [128, 8 * 128], BF16, kind="ExternalOutput")
        vtm_dbg = nc.dram_tensor("vtm_dbg", [64, C], BF16, kind="ExternalOutput")
        ss_dbg = nc.dram_tensor("ss_dbg", [128, 8 * 64], F32, kind="ExternalOutput")
        wb_dbg = nc.dram_tensor("wb_dbg", [128, 8 * 64], F32, kind="ExternalOutput")
        wtp_dbg = nc.dram_tensor("wtp_dbg", [64, 8 * 128], BF16, kind="ExternalOutput")
        exg_dbg = nc.dram_tensor("exg_dbg", [128, 8 * GS], BF16, kind="ExternalOutput")
        wtg_dbg = nc.dram_tensor("wtg_dbg", [128, CC * GS], BF16, kind="ExternalOutput")
        ht_dbg = nc.dram_tensor("ht_dbg", [128, KC * GS], BF16, kind="ExternalOutput")

    script = []

    def em(eng):
        def reg(fn):
            script.append((eng, fn))
            return fn
        return reg

    from contextlib import ExitStack
    _es = ExitStack()
    with _es:
        c_idf = _es.enter_context(nc.sbuf_tensor("sb_identf", [128, 128], F32))
        HT_OFF = 229344 - KC * GS * 2
        hT = nc.alloc_sbuf_tensor_at("hT", [128, KC, GS], BF16, offset=HT_OFF)
        x_t = nc.alloc_sbuf_tensor_at("x_t", [128, C], F32, offset=HT_OFF)
        k_tm = nc.alloc_sbuf_tensor_at("k_tm", [64, C], F32, offset=HT_OFF + 8192)
        kvstg = nc.alloc_sbuf_tensor_at("kvstg", [128, 2, C], F32,
                                        offset=HT_OFF + 16384)
        c_mask = _es.enter_context(nc.sbuf_tensor("sb_maskp", [128, 64], F32))
        c_cos = _es.enter_context(nc.sbuf_tensor("sb_cosT", [64, 64], F32))
        c_sin = _es.enter_context(nc.sbuf_tensor("sb_sinT", [64, 64], F32))
        c_ones = _es.enter_context(nc.sbuf_tensor("sb_onesb", [128, 128], BF16))
        c_bcb = _es.enter_context(nc.sbuf_tensor("sb_bcb", [16, HL], F32))
        qTb = _es.enter_context(nc.sbuf_tensor("qTb", [128, HL], BF16))
        gi_in = _es.enter_context(nc.sbuf_tensor("gi_in", [128, 2, GFREE], I32))
        xn_t = _es.enter_context(nc.sbuf_tensor("xn_t", [128, 2, C], BF16))
        ln_s = _es.enter_context(nc.sbuf_tensor("ln_s", [128, 10], F32))
        wkb = _es.enter_context(nc.sbuf_tensor("wkb", [128, C], BF16))
        wgb = _es.enter_context(nc.sbuf_tensor("wgb", [128, CC, HL], BF16))
        krh = _es.enter_context(nc.sbuf_tensor("krh", [64, 128], F32))
        ktmp = _es.enter_context(nc.sbuf_tensor("ktmp", [64, 64], F32))
        v_tmb = _es.enter_context(nc.sbuf_tensor("v_tmb", [64, C], BF16))
        kTb = _es.enter_context(nc.sbuf_tensor("kTb", [128, HL], BF16))
        ss_f = _es.enter_context(nc.sbuf_tensor("ss_f", [128, 64], F32))
        nmax = _es.enter_context(nc.sbuf_tensor("nmax", [128, 1], F32))
        w_b = _es.enter_context(nc.sbuf_tensor("w_b", [128, 64], F32))
        rs = _es.enter_context(nc.sbuf_tensor("rs", [128, 1], F32))
        rr = _es.enter_context(nc.sbuf_tensor("rr", [128, 1], F32))
        wTp = _es.enter_context(nc.sbuf_tensor("wTp", [64, 128], BF16))
        attnP = _es.enter_context(nc.sbuf_tensor("attnP", [128, 8, 128], BF16))
        xnT = _es.enter_context(nc.sbuf_tensor("xnT", [128, CC, GS], BF16))
        expg = _es.enter_context(nc.sbuf_tensor("expg", [128, 8, GS], BF16))
        rsum = _es.enter_context(nc.sbuf_tensor("rsum", [16, GS], F32))
        wTg = _es.enter_context(nc.sbuf_tensor("wTg", [128, CC, GS], BF16))
        gbuf = _es.enter_context(nc.sbuf_tensor("gbuf", [128, 2, C], BF16))
        ubuf = _es.enter_context(nc.sbuf_tensor("ubuf", [128, 2, C], BF16))
        dbuf = _es.enter_context(nc.sbuf_tensor("dbuf", [128, 2, INTER], BF16))
        wpsl = _es.enter_context(nc.sbuf_tensor("wpsl", [128, 2, C], BF16))
        sgb = _es.enter_context(nc.sbuf_tensor("sgb", [128, 2, GS], BF16))
        oT_sb = _es.enter_context(nc.sbuf_tensor("oT_sb", [128, 2, GS], F32))
        xpc = _es.enter_context(nc.sbuf_tensor("xpc", [128, 2, 128], F32))
        res_sb = _es.enter_context(nc.sbuf_tensor("res_sb", [128, 2, 128], F32))
        epsb = _es.enter_context(nc.sbuf_tensor("epsb", [128, 1], F32))
        junkv = _es.enter_context(nc.sbuf_tensor("junkv", [128, 1], F32))
        jnk2 = _es.enter_context(nc.sbuf_tensor("jnk2", [128, 16], BF16))
        junkx = _es.enter_context(nc.sbuf_tensor("junkx", [128, 16], F32))
        PGA = _es.enter_context(nc.psum_tensor("PGA", [128, 512], F32))
        PGB = _es.enter_context(nc.psum_tensor("PGB", [128, 512], F32))
        PB0 = _es.enter_context(nc.psum_tensor("PB0", [128, 512], F32))
        PB1 = _es.enter_context(nc.psum_tensor("PB1", [128, 512], F32))
        PO0 = _es.enter_context(nc.psum_tensor("PO0", [128, 512], F32))
        PO1 = _es.enter_context(nc.psum_tensor("PO1", [128, 512], F32))
        PSX = _es.enter_context(nc.psum_tensor("PSX", [128, 512], F32))
        PSY = _es.enter_context(nc.psum_tensor("PSY", [128, 512], F32))
        h_eps = _es.enter_context(nc.semaphore("s_eps"))
        h_c = _es.enter_context(nc.semaphore("s_c"))
        h_gl = _es.enter_context(nc.semaphore("s_gl"))
        h_gv = _es.enter_context(nc.semaphore("s_gv"))
        h_gw = _es.enter_context(nc.semaphore("s_gw"))
        h_ag = _es.enter_context(nc.semaphore("s_ag"))
        h_xl = _es.enter_context(nc.semaphore("s_xl"))
        h_app = _es.enter_context(nc.semaphore("s_app"))
        h_xnw = _es.enter_context(nc.semaphore("s_xnw"))
        h_x64 = _es.enter_context(nc.semaphore("s_x64"))
        h_wk = _es.enter_context(nc.semaphore("s_wk"))
        h_kmm = _es.enter_context(nc.semaphore("s_kmm"))
        h_kev = _es.enter_context(nc.semaphore("s_kev"))
        h_rope = _es.enter_context(nc.semaphore("s_rope"))
        h_ktr = _es.enter_context(nc.semaphore("s_ktr"))
        h_ktev = _es.enter_context(nc.semaphore("s_ktev"))
        h_scr = _es.enter_context(nc.semaphore("s_scr"))
        h_ss = _es.enter_context(nc.semaphore("s_ss"))
        h_we = _es.enter_context(nc.semaphore("s_we"))
        h_rr = _es.enter_context(nc.semaphore("s_rr"))
        h_wn = _es.enter_context(nc.semaphore("s_wn"))
        h_wt = _es.enter_context(nc.semaphore("s_wt"))
        h_wtev = _es.enter_context(nc.semaphore("s_wtev"))
        h_am = _es.enter_context(nc.semaphore("s_am"))
        h_aev = _es.enter_context(nc.semaphore("s_aev"))
        h_xnt = _es.enter_context(nc.semaphore("s_xnt"))
        h_wgl = _es.enter_context(nc.semaphore("s_wgl"))
        h_gmm = _es.enter_context(nc.semaphore("s_gmm"))
        h_gex = _es.enter_context(nc.semaphore("s_gex"))
        h_smm = _es.enter_context(nc.semaphore("s_smm"))
        h_rcp = _es.enter_context(nc.semaphore("s_rcp"))
        h_bc = _es.enter_context(nc.semaphore("s_bc"))
        h_gn = _es.enter_context(nc.semaphore("s_gn"))
        h_wm = _es.enter_context(nc.semaphore("s_wm"))
        h_wev = _es.enter_context(nc.semaphore("s_wev"))
        h_fs = _es.enter_context(nc.semaphore("s_fs"))
        h_gu = _es.enter_context(nc.semaphore("s_gu"))
        h_si = _es.enter_context(nc.semaphore("s_si"))
        h_ht = _es.enter_context(nc.semaphore("s_ht"))
        h_ds = _es.enter_context(nc.semaphore("s_ds"))
        h_omm = _es.enter_context(nc.semaphore("s_omm"))
        h_oev = _es.enter_context(nc.semaphore("s_oev"))
        h_tr = _es.enter_context(nc.semaphore("s_tr"))
        h_xp = _es.enter_context(nc.semaphore("s_xp"))
        h_res = _es.enter_context(nc.semaphore("s_res"))
        h_out = _es.enter_context(nc.semaphore("s_out"))
        h_kl = _es.enter_context(nc.semaphore("s_kl"))
        h_kc = _es.enter_context(nc.semaphore("s_kc"))
        h_d1 = _es.enter_context(nc.semaphore("s_d1"))
        h_d2 = _es.enter_context(nc.semaphore("s_d2"))
        h_d3 = _es.enter_context(nc.semaphore("s_d3"))
        h_d4 = _es.enter_context(nc.semaphore("s_d4"))
        block = _es.enter_context(nc.Block())
        s_c = Sem(h_c); s_gl = Sem(h_gl); s_gv = Sem(h_gv); s_gw = Sem(h_gw)
        s_ag = Sem(h_ag); s_xl = Sem(h_xl); s_app = Sem(h_app)
        s_xnw = Sem(h_xnw); s_x64 = Sem(h_x64); s_wk = Sem(h_wk)
        s_kmm = Sem(h_kmm); s_kev = Sem(h_kev); s_rope = Sem(h_rope)
        s_ktr = Sem(h_ktr); s_ktev = Sem(h_ktev); s_scr = Sem(h_scr)
        s_ss = Sem(h_ss); s_we = Sem(h_we); s_rr = Sem(h_rr); s_wn = Sem(h_wn)
        s_wt = Sem(h_wt); s_wtev = Sem(h_wtev); s_am = Sem(h_am)
        s_aev = Sem(h_aev); s_xnt = Sem(h_xnt)
        s_wgl = Sem(h_wgl); s_gmm = Sem(h_gmm); s_gex = Sem(h_gex)
        s_smm = Sem(h_smm); s_rcp = Sem(h_rcp); s_bc = Sem(h_bc)
        s_gn = Sem(h_gn); s_wm = Sem(h_wm); s_wev = Sem(h_wev)
        s_fs = Sem(h_fs); s_gu = Sem(h_gu); s_si = Sem(h_si)
        s_ht = Sem(h_ht); s_ds = Sem(h_ds); s_omm = Sem(h_omm)
        s_oev = Sem(h_oev); s_tr = Sem(h_tr); s_xp = Sem(h_xp)
        s_res = Sem(h_res); s_out = Sem(h_out)
        s_kl = Sem(h_kl); s_kc = Sem(h_kc)

        banks = [PGA, PGB, PB0, PB1, PO0, PO1, PSX, PSY]

        gw = [dbuf[:, 0, 0:GFREE], dbuf[:, 0, GFREE:2 * GFREE],
              dbuf[:, 0, 2 * GFREE:3 * GFREE]]
        gi_bf = dbuf[:, 1, 0:GFREE]
        g_ct = dbuf[:, 1, GFREE:2 * GFREE]
        wp_stage = [gbuf[:, 0, :], gbuf[:, 1, :]]
        xnT64 = [xnT[:, cc, 0:64] for cc in range(CC)]

        @em("v")
        def _(e):
            e.memset(epsb[:], EPS).then_inc(h_eps, 1)

        def idx_tile_ap(k):
            return bass.AP(idx_h, k * 128 * GFREE, [[GFREE, 128], [1, GFREE]])

        def ag1_tile_ap(k):
            return bass.AP(ag1_in, k * 128 * GFREE, [[GFREE, 128], [1, GFREE]])

        @em("gp")
        def _(e):
            e.dma_start(out=gi_in[:, 0, :], in_=idx_tile_ap(0)).then_inc(s_gl.h, 16)
            e.dma_start(out=gi_in[:, 1, :], in_=idx_tile_ap(1)).then_inc(s_gl.h, 16)
            e.dma_start(out=c_idf[:], in_=identf_h.ap()).then_inc(s_c.h, 16)
            e.dma_start(out=c_mask[:], in_=maskp_h.ap()).then_inc(s_c.h, 16)
            e.dma_start(out=c_cos[:], in_=cos_h.ap()).then_inc(s_c.h, 16)
            e.dma_start(out=c_sin[:], in_=sin_h.ap()).then_inc(s_c.h, 16)
            e.dma_start(out=c_ones[:], in_=onesb_h.ap()).then_inc(s_c.h, 16)
            e.dma_start(out=c_bcb[:], in_=bcb_h.ap()).then_inc(s_c.h, 16)
            e.dma_start(out=qTb[:], in_=qT_h.ap()).then_inc(s_c.h, 16)
            e.dma_start(out=wp_stage[0], in_=wpa_h.ap()[0:128, :]).then_inc(s_c.h, 16)
            e.dma_start(out=wp_stage[1], in_=wpa_h.ap()[128:256, :]).then_inc(s_c.h, 16)
        s_gl.bump(32)
        s_c.bump(9 * 16)
        CONSTS = s_c.n

        @em("gp")
        def _(e):
            for cc in range(CC):
                e.dma_start(out=wgb[:, cc, :],
                            in_=wg_h.ap()[cc * 128:(cc + 1) * 128, :]
                            ).then_inc(s_wgl.h, 16)
            e.dma_start(out=jnk2[:],
                        in_=wg_h.ap()[0:128, 0:16]).then_inc(s_wgl.h, 16)
        s_wgl.bump(17 * 16)

        for k in range(GTILES):
            @em("v")
            def _(e, k=k):
                e.wait_ge(s_gl.h, 16 * (k + 1))
                if k == 0:
                    e.wait_ge(s_c.h, CONSTS)
                if k >= 3:
                    e.wait_ge(s_gw.h, 16 * (k - 2))
                e.tensor_copy(out=gi_bf, in_=gi_in[:, k % 2, :])
                e.tensor_scalar(out=gw[k % 3], in0=gi_bf,
                                scalar1=0.0, scalar2=lut_vals[0],
                                op0=OP.is_equal, op1=OP.mult)
                last = None
                for v in range(1, 16):
                    e.tensor_scalar(out=g_ct, in0=gi_bf,
                                    scalar1=float(v), scalar2=lut_vals[v],
                                    op0=OP.is_equal, op1=OP.mult)
                    last = e.tensor_add(gw[k % 3], gw[k % 3], g_ct)
                last.then_inc(s_gv.h, 1)
            s_gv.bump(1)

            if k + 2 < GTILES:
                @em("gp")
                def _(e, k=k):
                    e.wait_ge(s_gv.h, k + 1)
                    e.dma_start(out=gi_in[:, k % 2, :],
                                in_=idx_tile_ap(k + 2)).then_inc(s_gl.h, 16)
                s_gl.bump(16)

            @em("gp")
            def _(e, k=k):
                e.wait_ge(s_gv.h, k + 2)
                e.dma_start(out=ag1_tile_ap(k), in_=gw[k % 3]).then_inc(s_gw.h, 16)
            s_gw.bump(16)

        @em("v")
        def _(e):
            e.memset(junkv[:], 0.0).then_inc(s_gv.h, 1)
        s_gv.bump(1)

        @em("gp")
        def _(e, want=s_c.n):
            e.wait_ge(s_c.h, want)
            e.dma_start(out=ag2_in.ap()[0:128, :], in_=wp_stage[0]).then_inc(s_gw.h, 16)
            e.dma_start(out=ag2_in.ap()[128:256, :], in_=wp_stage[1]).then_inc(s_gw.h, 16)
        s_gw.bump(32)

        AGW_WANT = s_gw.n

        ssq = ln_s[:, 0:1]; ssum = ln_s[:, 1:2]; mean = ln_s[:, 2:3]
        nm2 = ln_s[:, 3:4]; varep = ln_s[:, 4:5]; rstd = ln_s[:, 5:6]
        t9 = ln_s[:, 6:7]; nmr = ln_s[:, 7:8]; lnjk = ln_s[:, 8:9]

        def emit_ln_tile(tt):
            @em("s")
            def _(e, tt=tt):
                if tt == 0:
                    e.wait_ge(h_eps, 1)
                if tt >= 1:
                    e.wait_ge(s_app.h, tt)
                e.dma_start(out=x_t[:],
                            in_=x_h.ap()[tt * 128:(tt + 1) * 128, :]
                            ).then_inc(s_xl.h, 16)
                e.wait_ge(s_xl.h, 16 * (tt + 1))
                if tt >= 2:
                    e.wait_ge(s_xnw.h, 16 * (tt - 1))
                e.activation(out=xn_t[:, tt % 2, :], in_=x_t[:], func=AX.Square,
                             accum_out=ssq)
                e.activation(out=xn_t[:, tt % 2, :], in_=x_t[:], func=AX.Identity,
                             accum_out=ssum)
                e.mul(out=mean, in_=ssum, mul=1.0 / C)
                e.activation(out=nm2, in_=mean, func=AX.Square)
                e.activation(out=nm2, in_=nm2, func=AX.Identity,
                             scale=-1.0, bias=epsb[:])
                e.activation(out=lnjk, in_=ssq, func=AX.Identity)
                e.activation(out=varep, in_=ssq, func=AX.Identity,
                             scale=1.0 / C, bias=nm2)
                e.activation(out=varep, in_=varep, func=AX.Ln)
                e.activation(out=rstd, in_=varep, func=AX.Exp, scale=-0.5)
                e.activation(out=t9, in_=mean, func=AX.Identity, scale=rstd)
                e.mul(out=nmr, in_=t9, mul=-1.0)
                e.activation(out=lnjk, in_=ssq, func=AX.Identity)
                e.activation(out=xn_t[:, tt % 2, :], in_=x_t[:], func=AX.Identity,
                             scale=rstd, bias=nmr)
                e.activation(out=lnjk, in_=ssq,
                             func=AX.Identity).then_inc(s_app.h, 1)
            s_xl.bump(16)
            s_app.bump(1)

            @em("sy")
            def _(e, tt=tt):
                e.wait_ge(s_app.h, tt + 1)
                e.dma_start(out=xn_scr.ap()[tt * 128:(tt + 1) * 128, :],
                            in_=xn_t[:, tt % 2, :]).then_inc(s_xnw.h, 16)
            s_xnw.bump(16)

        def emit_kv_stage(tt):
            for i in (2 * tt, 2 * tt + 1):
                @em("sy")
                def _(e, i=i):
                    if i >= 2:
                        e.wait_ge(s_kc.h, i - 1)
                    src_h = wk_h if i < CC else wv_h
                    cc2 = i % CC
                    e.dma_start(out=kvstg[:, i % 2, :],
                                in_=src_h.ap()[cc2 * 128:(cc2 + 1) * 128, :]
                                ).then_inc(s_kl.h, 16)
                s_kl.bump(16)
            if tt == NT - 1:
                @em("sy")
                def _(e):
                    e.dma_start(out=junkx[:],
                                in_=x_h.ap()[0:128, 0:16]).then_inc(s_kl.h, 16)
                s_kl.bump(16)
            for i in (2 * tt, 2 * tt + 1):
                @em("s")
                def _(e, i=i):
                    e.wait_ge(s_kl.h, 16 * (2 * (i // 2) + 2))
                    if i >= 1:
                        e.wait_ge(s_kmm.h, i)
                    e.activation(out=wkb[:], in_=kvstg[:, i % 2, :],
                                 func=AX.Copy).then_inc(s_kc.h, 1)
                s_kc.bump(1)

        emit_ln_tile(0)

        @em("sy")
        def _(e):
            e.wait_ge(s_xnw.h, 16)
            e.wait_ge(s_c.h, CONSTS)
            for cc in range(CC):
                e.dma_start(out=xnT64[cc],
                            in_=xn_scr.ap()[0:64, cc * 128:(cc + 1) * 128],
                            transpose=True).then_inc(s_x64.h, 16)
        s_x64.bump(16 * CC)

        emit_kv_stage(0)
        for tt in range(1, NT):
            emit_ln_tile(tt)
            emit_kv_stage(tt)

        kvps_m = ([PGA[0:64, :], PGB[0:64, :], PB0[0:64, :], PB1[0:64, :]],
                  [PO0[0:64, :], PO1[0:64, :], PSX[0:64, :], PSY[0:64, :]])
        for m in range(2):
            for cc in range(CC):
                i = m * CC + cc

                @em("t")
                def _(e, m=m, cc=cc, i=i):
                    e.wait_ge(s_kc.h, i + 1)
                    if m == 0 and cc == 0:
                        e.wait_ge(s_x64.h, 16 * CC)
                    mm = None
                    for n in range(4):
                        mm = e.matmul(kvps_m[m][n], xnT64[cc],
                                      wkb[:, 512 * n:512 * (n + 1)],
                                      start=(cc == 0), stop=(cc == CC - 1))
                    mm.then_inc(s_kmm.h, 1)
                s_kmm.bump(1)

            @em("v")
            def _(e, m=m):
                e.wait_ge(s_kmm.h, 16 * (m + 1))
                dst = k_tm if m == 0 else v_tmb
                mm = None
                for n in range(4):
                    mm = e.tensor_copy(out=dst[:, 512 * n:512 * (n + 1)],
                                       in_=kvps_m[m][n])
                mm.then_inc(s_kev.h, 1)
            s_kev.bump(1)

        @em("gp")
        def _(e):
            e.wait_ge(s_gw.h, AGW_WANT)
            e.wait_ge(s_ktev.h, 16)
            e.collective_compute(
                "AllGather", OP.bypass, replica_groups=[list(range(8))],
                ins=[ag1_in.ap().opt()], outs=[w_scr.ap().opt()],
            ).then_inc(s_ag.h, 1)
            e.collective_compute(
                "AllGather", OP.bypass, replica_groups=[list(range(8))],
                ins=[ag2_in.ap().opt()], outs=[wp_scr.ap().opt()],
            ).then_inc(s_ag.h, 1)
        s_ag.bump(2)

        for h in range(H):
            @em("v")
            def _(e, h=h):
                if h == 0:
                    e.wait_ge(s_kev.h, 1)
                    e.wait_ge(s_c.h, CONSTS)
                else:
                    e.wait_ge(s_ktr.h, h)
                base = h * 128
                x0 = bass.AP(k_tm, base + 0, [[C, 64], [2, 64]])
                x1 = bass.AP(k_tm, base + 1, [[C, 64], [2, 64]])
                r0 = bass.AP(krh, 0, [[128, 64], [2, 64]])
                r1 = bass.AP(krh, 1, [[128, 64], [2, 64]])
                tmp = ktmp[:, :]
                e.tensor_mul(tmp, x0, c_sin[:, :])
                e.tensor_mul(r1, x1, c_cos[:, :])
                e.tensor_tensor(out=r1, in0=r1, in1=tmp, op=OP.add)
                e.tensor_mul(tmp, x1, c_sin[:, :])
                e.tensor_mul(r0, x0, c_cos[:, :])
                e.tensor_tensor(out=r0, in0=r0, in1=tmp,
                                op=OP.subtract).then_inc(s_rope.h, 1)
            s_rope.bump(1)

            @em("t")
            def _(e, h=h):
                e.wait_ge(s_rope.h, h + 1)
                if h == 0:
                    e.wait_ge(s_kev.h, 2)
                if h >= 2:
                    e.wait_ge(s_ktev.h, h - 1)
                pt = (PSX, PSY)[h % 2]
                e.transpose(pt[:, 0:64], krh[:, :],
                            c_idf[0:64, 0:64]).then_inc(s_ktr.h, 1)
            s_ktr.bump(1)

            @em("v")
            def _(e, h=h):
                e.wait_ge(s_ktr.h, h + 1)
                pt = (PSX, PSY)[h % 2]
                e.tensor_copy(out=kTb[:, h * 64:(h + 1) * 64],
                              in_=pt[:, 0:64]).then_inc(s_ktev.h, 1)
            s_ktev.bump(1)

        for q in range(8):
            @em("t")
            def _(e, q=q):
                if q == 0:
                    e.wait_ge(s_ktev.h, H)
                else:
                    e.wait_ge(s_ss.h, q)
                e.matmul(PGA[0:64, 0:64], qTb[:, (2 * q) * 64:(2 * q + 1) * 64],
                         kTb[:, (2 * q) * 64:(2 * q + 1) * 64],
                         start=True, stop=True)
                e.matmul(PGA[64:128, 0:64], qTb[:, (2 * q + 1) * 64:(2 * q + 2) * 64],
                         kTb[:, (2 * q + 1) * 64:(2 * q + 2) * 64],
                         start=True, stop=True,
                         tile_position=(0, 64)).then_inc(s_scr.h, 1)
            s_scr.bump(1)

            @em("v")
            def _(e, q=q):
                e.wait_ge(s_scr.h, q + 1)
                if debug and q >= 1:
                    e.wait_ge(h_d1, 16 * q)
                e.scalar_tensor_tensor(out=ss_f[:], in0=PGA[:, 0:64], scalar=SCALE,
                                       in1=c_mask[:], op0=OP.mult, op1=OP.add)
                e.reduce_max(nmax[:], ss_f[:], axis=AXL.X, negate=True)
                e.memset(junkv[:], 0.0)
                e.tensor_scalar_add(out=ss_f[:], in0=ss_f[:],
                                    scalar1=nmax[:]).then_inc(s_ss.h, 1)
            s_ss.bump(1)

            if debug:
                @em("sy")
                def _(e, q=q):
                    e.wait_ge(s_ss.h, q + 1)
                    e.dma_start(out=ss_dbg.ap()[:, q * 64:(q + 1) * 64],
                                in_=ss_f[:]).then_inc(h_d1, 16)

            @em("s")
            def _(e, q=q):
                e.wait_ge(s_ss.h, q + 1)
                if debug and q >= 1:
                    e.wait_ge(h_d2, 16 * q)
                e.activation(out=w_b[:], in_=ss_f[:], func=AX.Exp,
                             accum_out=rs[:]).then_inc(s_we.h, 1)
            s_we.bump(1)

            @em("v")
            def _(e, q=q):
                e.wait_ge(s_we.h, q + 1)
                e.reciprocal(rr[:], rs[:])
                e.memset(junkv[:], 0.0)
                e.tensor_scalar_mul(out=w_b[:], in0=w_b[:],
                                    scalar1=rr[:]).then_inc(s_wn.h, 1)
            s_wn.bump(1)

            if debug:
                @em("sy")
                def _(e, q=q):
                    e.wait_ge(s_wn.h, q + 1)
                    e.dma_start(out=wb_dbg.ap()[:, q * 64:(q + 1) * 64],
                                in_=w_b[:]).then_inc(h_d2, 16)

            @em("t")
            def _(e, q=q):
                e.wait_ge(s_wn.h, q + 1)
                if q > 0:
                    e.wait_ge(s_wtev.h, q)
                e.transpose(PGB[0:64, 0:128], w_b[:],
                            c_idf[:, :]).then_inc(s_wt.h, 1)
            s_wt.bump(1)

            @em("v")
            def _(e, q=q):
                e.wait_ge(s_wt.h, q + 1)
                if debug and q >= 1:
                    e.wait_ge(h_d3, 16 * q)
                e.tensor_copy(out=wTp[:], in_=PGB[0:64, 0:128]).then_inc(s_wtev.h, 1)
            s_wtev.bump(1)

            if debug:
                @em("sy")
                def _(e, q=q):
                    e.wait_ge(s_wtev.h, q + 1)
                    e.dma_start(out=wtp_dbg.ap()[:, q * 128:(q + 1) * 128],
                                in_=wTp[:]).then_inc(h_d3, 16)

            @em("t")
            def _(e, q=q):
                e.wait_ge(s_wtev.h, q + 1)
                if q > 0:
                    e.wait_ge(s_aev.h, q)
                e.matmul(PB0[0:64, 0:128], wTp[:, 0:64],
                         v_tmb[:, (2 * q) * 128:(2 * q + 1) * 128],
                         start=True, stop=True)
                e.matmul(PB0[64:128, 0:128], wTp[:, 64:128],
                         v_tmb[:, (2 * q + 1) * 128:(2 * q + 2) * 128],
                         start=True, stop=True,
                         tile_position=(0, 64)).then_inc(s_am.h, 1)
            s_am.bump(1)

            @em("v")
            def _(e, q=q):
                e.wait_ge(s_am.h, q + 1)
                e.tensor_copy(out=attnP[:, q, :],
                              in_=PB0[:, 0:128]).then_inc(s_aev.h, 1)
            s_aev.bump(1)

        piece = 0
        for G in range(NG):
            @em("sy")
            def _(e, G=G):
                e.wait_ge(s_xnw.h, 16 * 4 * (G + 1))
                if G == 0:
                    e.wait_ge(s_kmm.h, 32)
                else:
                    e.wait_ge(s_gu.h, KC * G)
                for cc in range(CC):
                    e.dma_start(out=xnT[:, cc, :],
                                in_=xn_scr.ap()[G * GS:(G + 1) * GS,
                                                cc * 128:(cc + 1) * 128],
                                transpose=True).then_inc(s_xnt.h, 16)
            s_xnt.bump(16 * CC)

            for cc in range(CC):
                @em("t")
                def _(e, G=G, cc=cc):
                    if G == 0 and cc == 0:
                        e.wait_ge(s_wgl.h, 16 * (CC + 1))
                    if cc == 0:
                        e.wait_ge(s_xnt.h, 16 * CC * (G + 1))
                        if G == 0:
                            e.wait_ge(s_aev.h, 8)
                        else:
                            e.wait_ge(s_oev.h, CT * G)
                            e.wait_ge(s_res.h, 64 * G)
                            e.wait_ge(s_si.h, KC * G)
                            e.wait_ge(s_ht.h, KC * G)
                            e.wait_ge(s_gn.h, 8 * G)
                            e.wait_ge(s_wev.h, H * G)
                            e.wait_ge(s_rcp.h, G)
                    mm = None
                    for gt in range(8):
                        mm = e.matmul(banks[gt][:, 0:GS],
                                      wgb[:, cc, gt * 128:(gt + 1) * 128],
                                      xnT[:, cc, :],
                                      start=(cc == 0), stop=(cc == CC - 1))
                    mm.then_inc(s_gmm.h, 1)
                s_gmm.bump(1)

            for gt in range(8):
                @em("s")
                def _(e, G=G, gt=gt):
                    e.wait_ge(s_gmm.h, 16 * (G + 1))
                    if debug and G == 1 and gt == 0:
                        e.wait_ge(h_d4, 16)
                    e.activation(out=expg[:, gt, :], in_=banks[gt][:, 0:GS],
                                 func=AX.Exp, scale=SCALE).then_inc(s_gex.h, 1)
                s_gex.bump(1)

            @em("t")
            def _(e, G=G):
                e.wait_ge(s_gex.h, 8 * (G + 1))
                mm = None
                for q in range(8):
                    mm = e.matmul(PSX[0:16, 0:GS], c_ones[:, q * 16:(q + 1) * 16],
                                  expg[:, q, :], start=(q == 0), stop=(q == 7))
                mm.then_inc(s_smm.h, 1)
            s_smm.bump(1)

            @em("v")
            def _(e, G=G):
                e.wait_ge(s_smm.h, G + 1)
                e.reciprocal(rsum[:], PSX[0:16, 0:GS]).then_inc(s_rcp.h, 1)
            s_rcp.bump(1)

            for q in range(8):
                pb = (PB0, PB1)[q % 2]

                @em("t")
                def _(e, G=G, q=q, pb=pb):
                    if q == 0:
                        e.wait_ge(s_rcp.h, G + 1)
                    if 8 * G + q >= 2:
                        e.wait_ge(s_gn.h, 8 * G + q - 1)
                    e.matmul(pb[:, :], c_bcb[:, q * 128:(q + 1) * 128],
                             rsum[:, :], start=True, stop=True).then_inc(s_bc.h, 1)
                s_bc.bump(1)

                @em("v")
                def _(e, G=G, q=q, pb=pb):
                    e.wait_ge(s_bc.h, 8 * G + q + 1)
                    e.tensor_tensor(out=expg[:, q, :], in0=expg[:, q, :],
                                    in1=pb[:, 0:GS], op=OP.mult).then_inc(s_gn.h, 1)
                s_gn.bump(1)

            if debug and G == 0:
                @em("sy")
                def _(e):
                    e.wait_ge(s_gn.h, 8)
                    e.dma_start(out=exg_dbg.ap(),
                                in_=expg[:, :, :]).then_inc(h_d4, 16)

            for h in range(H):
                q = h // 2
                sl = slice(0, 64) if h % 2 == 0 else slice(64, 128)
                po = (PB0, PB1)[h % 2]

                @em("t")
                def _(e, G=G, h=h, q=q, sl=sl, po=po):
                    e.wait_ge(s_gn.h, 8 * (G + 1))
                    if H * G + h >= 2:
                        e.wait_ge(s_wev.h, H * G + h - 1)
                    e.matmul(po[:, 0:GS], attnP[sl, q, :], expg[sl, q, :],
                             start=True, stop=True).then_inc(s_wm.h, 1)
                s_wm.bump(1)

                @em("v")
                def _(e, G=G, h=h, po=po):
                    e.wait_ge(s_wm.h, H * G + h + 1)
                    if debug and G == 1 and h == 0:
                        e.wait_ge(h_d4, 32)
                    e.tensor_copy(out=wTg[:, h, :],
                                  in_=po[:, 0:GS]).then_inc(s_wev.h, 1)
                s_wev.bump(1)

            if debug and G == 0:
                @em("sy")
                def _(e):
                    e.wait_ge(s_wev.h, H)
                    e.dma_start(out=wtg_dbg.ap(),
                                in_=wTg[:, :, :]).then_inc(h_d4, 16)

            for it in range(KC):
                idx = KC * G + it

                @em("sy")
                def _(e, G=G, it=it, idx=idx):
                    if G == 0 and it == 0:
                        e.wait_ge(s_ag.h, 2)
                    if idx >= 2:
                        e.wait_ge(s_gu.h, idx - 1)
                    e.dma_start(out=gbuf[:, idx % 2, :],
                                in_=w_scr.ap()[it * 128:(it + 1) * 128, :]
                                ).then_inc(s_fs.h, 16)
                    e.dma_start(out=ubuf[:, idx % 2, :],
                                in_=w_scr.ap()[5504 + it * 128:5504 + (it + 1) * 128, :]
                                ).then_inc(s_fs.h, 16)
                s_fs.bump(32)

                gpair = (PGA, PGB) if idx % 2 == 0 else (PO0, PO1)

                @em("t")
                def _(e, G=G, it=it, idx=idx, gpair=gpair):
                    e.wait_ge(s_fs.h, 32 * (idx + 1))
                    if it == 0:
                        e.wait_ge(s_gex.h, 8 * (G + 1))
                    if G == 0 and it == 1:
                        e.wait_ge(s_aev.h, 8)
                    if idx >= 2:
                        e.wait_ge(s_si.h, idx - 1)
                        e.wait_ge(s_ht.h, idx - 1)
                    mm = None
                    for cc in range(CC):
                        mm = e.matmul(gpair[0][:, :],
                                      gbuf[:, idx % 2, cc * 128:(cc + 1) * 128],
                                      xnT[:, cc, :],
                                      start=(cc == 0), stop=(cc == CC - 1))
                    for cc in range(CC):
                        mm = e.matmul(gpair[1][:, :],
                                      ubuf[:, idx % 2, cc * 128:(cc + 1) * 128],
                                      xnT[:, cc, :],
                                      start=(cc == 0), stop=(cc == CC - 1))
                    mm.then_inc(s_gu.h, 1)
                s_gu.bump(1)

                @em("s")
                def _(e, idx=idx, gpair=gpair):
                    e.wait_ge(s_gu.h, idx + 1)
                    if idx >= 2:
                        e.wait_ge(s_ht.h, idx - 1)
                    e.activation(out=sgb[:, idx % 2, :], in_=gpair[0][:, :],
                                 func=AX.Silu).then_inc(s_si.h, 1)
                s_si.bump(1)

                @em("v")
                def _(e, it=it, idx=idx, gpair=gpair):
                    e.wait_ge(s_si.h, idx + 1)
                    if idx == 0:
                        e.wait_ge(s_app.h, NT)
                        e.wait_ge(s_kc.h, 32)
                    if debug and idx == KC:
                        e.wait_ge(h_d4, 48)
                    e.tensor_tensor(out=hT[:, it, :], in0=sgb[:, idx % 2, :],
                                    in1=gpair[1][:, :], op=OP.mult).then_inc(s_ht.h, 1)
                s_ht.bump(1)

            if debug and G == 0:
                @em("sy")
                def _(e):
                    e.wait_ge(s_ht.h, KC)
                    e.dma_start(out=ht_dbg.ap(),
                                in_=hT[:, :, :]).then_inc(h_d4, 16)

            def emit_pieces(G, ct, idx):
                for tt in range(4):
                    p = 4 * idx + tt

                    if p == 0:
                        @em("s")
                        def _(e):
                            e.dma_start(out=xpc[:, 0, :],
                                        in_=x_h.ap()[0:128, 0:128]).then_inc(s_xp.h, 16)
                            e.dma_start(out=xpc[:, 1, :],
                                        in_=x_h.ap()[128:256, 0:128]).then_inc(s_xp.h, 16)
                        s_xp.bump(32)

                    @em("t")
                    def _(e, idx=idx, tt=tt, p=p):
                        e.wait_ge(s_oev.h, idx + 1)
                        if p >= 4:
                            e.wait_ge(s_res.h, p - 3)
                        pt = (PSX, PSY, PGA, PGB)[p % 4]
                        e.transpose(pt[:, 0:128],
                                    oT_sb[:, idx % 2, tt * 128:(tt + 1) * 128],
                                    c_idf[:, :]).then_inc(s_tr.h, 1)
                    s_tr.bump(1)

                    @em("v")
                    def _(e, p=p):
                        e.wait_ge(s_tr.h, p + 1)
                        e.wait_ge(s_xp.h, 16 * (p + 2))
                        if p >= 2:
                            e.wait_ge(s_out.h, 16 * (p - 1))
                        pt = (PSX, PSY, PGA, PGB)[p % 4]
                        e.tensor_tensor(out=res_sb[:, p % 2, :],
                                        in0=pt[:, 0:128], in1=xpc[:, p % 2, :],
                                        op=OP.add)
                        e.memset(junkv[:], 0.0).then_inc(s_res.h, 1)
                    s_res.bump(1)

                    @em("gp")
                    def _(e, G=G, ct=ct, tt=tt, p=p):
                        e.wait_ge(s_res.h, p + 1)
                        r0 = G * GS + tt * 128
                        e.dma_start(out=out_h.ap()[r0:r0 + 128, ct * 128:(ct + 1) * 128],
                                    in_=res_sb[:, p % 2, :]).then_inc(s_out.h, 16)
                    s_out.bump(16)

                    p2 = p + 2
                    @em("s")
                    def _(e, p=p, p2=p2):
                        e.wait_ge(s_res.h, p + 1)
                        if p2 < NG * CT * 4:
                            G2 = p2 // 64
                            ct2 = (p2 % 64) // 4
                            tt2 = p2 % 4
                            r2 = G2 * GS + tt2 * 128
                            e.dma_start(out=xpc[:, p2 % 2, :],
                                        in_=x_h.ap()[r2:r2 + 128,
                                                     ct2 * 128:(ct2 + 1) * 128]
                                        ).then_inc(s_xp.h, 16)
                        else:
                            e.dma_start(out=junkx[:],
                                        in_=x_h.ap()[0:128, 0:16]
                                        ).then_inc(s_xp.h, 16)
                    s_xp.bump(16)

            for ct in range(CT):
                idx = CT * G + ct
                po = (PO0, PO1)[ct % 2]

                @em("sy")
                def _(e, G=G, ct=ct, idx=idx):
                    if idx >= 2:
                        e.wait_ge(s_omm.h, idx - 1)
                    e.dma_start(out=dbuf[:, idx % 2, :],
                                in_=bass.AP(w_scr, 11008 * 2048 + ct * 128 * INTER,
                                            [[INTER, 128], [1, INTER]])
                                ).then_inc(s_ds.h, 16)
                    e.dma_start(out=wpsl[:, idx % 2, :],
                                in_=wp_scr.ap()[ct * 128:(ct + 1) * 128, :]
                                ).then_inc(s_ds.h, 16)
                s_ds.bump(32)

                @em("t")
                def _(e, G=G, ct=ct, idx=idx, po=po):
                    e.wait_ge(s_ds.h, 32 * (idx + 1))
                    if ct == 0:
                        e.wait_ge(s_ht.h, KC * (G + 1))
                        e.wait_ge(s_wev.h, H * (G + 1))
                    if idx >= 2:
                        e.wait_ge(s_oev.h, idx - 1)
                    mm = None
                    for kc in range(KC):
                        mm = e.matmul(po[:, :],
                                      dbuf[:, idx % 2, kc * 128:(kc + 1) * 128],
                                      hT[:, kc, :], start=(kc == 0), stop=False)
                    for cc in range(CC):
                        mm = e.matmul(po[:, :],
                                      wpsl[:, idx % 2, cc * 128:(cc + 1) * 128],
                                      wTg[:, cc, :], start=False, stop=(cc == CC - 1))
                    mm.then_inc(s_omm.h, 1)
                s_omm.bump(1)

                @em("v")
                def _(e, idx=idx, po=po):
                    e.wait_ge(s_omm.h, idx + 1)
                    if idx >= 2:
                        e.wait_ge(s_tr.h, 4 * (idx - 1))
                    for st in (3, 2, 1, 0):
                        e.tensor_copy(out=oT_sb[:, idx % 2, st * 128:(st + 1) * 128],
                                      in_=po[:, st * 128:(st + 1) * 128])
                    e.memset(junkv[:], 0.0).then_inc(s_oev.h, 1)
                s_oev.bump(1)

                if ct >= 1:
                    emit_pieces(G, ct - 1, idx - 1)
            emit_pieces(G, CT - 1, CT * G + CT - 1)

        @em("gp")
        def _(e, want=s_out.n):
            e.wait_ge(s_out.h, want)
            if debug:
                e.wait_ge(h_d1, 128)
                e.wait_ge(h_d2, 128)
                e.wait_ge(h_d3, 128)
                e.wait_ge(h_d4, 48)
                e.dma_start(out=w_dbg.ap(), in_=w_scr.ap()).then_inc(s_out.h, 16)
                e.dma_start(out=wp_dbg.ap(), in_=wp_scr.ap()).then_inc(s_out.h, 16)
                e.dma_start(out=ktb_dbg.ap(), in_=kTb[:]).then_inc(s_out.h, 16)
                e.dma_start(out=atp_dbg.ap(), in_=attnP[:, :, :]).then_inc(s_out.h, 16)
                e.dma_start(out=vtm_dbg.ap(), in_=v_tmb[:]).then_inc(s_out.h, 16)
                e.wait_ge(s_out.h, want + 5 * 16)

        @block.gpsimd
        def _(eng):
            for name, fn in script:
                if name == "gp":
                    fn(eng)

        @block.sync
        def _(eng):
            for name, fn in script:
                if name == "sy":
                    fn(eng)

        @block.vector
        def _(eng):
            for name, fn in script:
                if name == "v":
                    fn(eng)

        @block.scalar
        def _(eng):
            for name, fn in script:
                if name == "s":
                    fn(eng)

        @block.tensor
        def _(eng):
            for name, fn in script:
                if name == "t":
                    fn(eng)

    return nc


def kernel(L0_attn_x, L0_attn_latent_queries, L0_attn_Wk, L0_attn_Wv,
           L0_attn_Wg, L0_attn_Wp, L0_ffn_lut, L0_ffn_gate_idx,
           L0_ffn_up_idx, L0_ffn_down_idx):
    x = np.ascontiguousarray(np.asarray(L0_attn_x, dtype=np.float32))
    lut_vals = [float(v) for v in np.asarray(L0_ffn_lut, dtype=np.float32)]
    xs, idx_shards, wp_shards, qT = _host_shards(
        x, np.asarray(L0_attn_latent_queries, dtype=np.float32),
        np.asarray(L0_attn_Wp, dtype=np.float32),
        np.asarray(L0_ffn_gate_idx), np.asarray(L0_ffn_up_idx),
        np.asarray(L0_ffn_down_idx))
    identf, identb, maskp, cosT, sinT, onesb, bcb = _consts()

    wk = np.ascontiguousarray(np.asarray(L0_attn_Wk, dtype=np.float32))
    wv = np.ascontiguousarray(np.asarray(L0_attn_Wv, dtype=np.float32))
    wg = np.ascontiguousarray(np.asarray(L0_attn_Wg, dtype=np.float32))

    nc = build(lut_vals)
    in_maps = []
    for j in range(8):
        in_maps.append({
            "x": xs[j], "wk": wk, "wv": wv, "wg": wg, "qT": qT,
            "idxs": idx_shards[j], "wpa": wp_shards[j],
            "identf": identf, "identb": identb, "maskp": maskp,
            "cosT": cosT, "sinT": sinT, "onesb": onesb, "bcb": bcb,
        })
    res = run_bass_kernel_spmd(nc, in_maps, core_ids=list(range(8)))
    out = np.stack([np.asarray(r["out"], dtype=np.float32) for r in res.results],
                   axis=0)
    return out
